# version 23
# baseline (speedup 1.0000x reference)
import math
import numpy as np
import ml_dtypes
import concourse.bass as bass
import concourse.mybir as mybir
from concourse.bass_utils import run_bass_kernel_spmd
from contextlib import ExitStack

F32 = mybir.dt.float32
BF16 = mybir.dt.bfloat16
I32 = mybir.dt.int32
ALU = mybir.AluOpType
ACT = mybir.ActivationFunctionType
AX = mybir.AxisListType

D = 2048
C = 16
SEQ = 4096
OWN = 2048
FI = 4096
FC = 32
EPS = 1e-5
C1 = 1.0 / math.sqrt(2048.0)
C2 = 1.0 / 64.0
CS = 1.0 / math.sqrt(2048.0)
QB = 127.0
NEG = -1.0e12

TILES_ROLE = [
    [0, 1, 4, 5, 8, 9, 12, 13, 16, 17, 20, 21, 24, 25, 28, 29],
    [2, 3, 6, 7, 10, 11, 14, 15, 18, 19, 22, 23, 26, 27, 30, 31],
]
CNT = [(g + 4) // 4 for g in range(32)]

ENGS = ("sync", "gpsimd", "vector", "scalar", "tensor")
OWN_SEM = {"sync": "LD", "gpsimd": "GD", "vector": "VV", "scalar": "SV",
           "tensor": "TV"}
SKIP_SELF = set()


class Prog:
    NCHAIN = 6

    def __init__(self, nc, stack):
        self.nc = nc
        self.items = {e: [] for e in ENGS}
        self.sems = {}
        self.cnt = {}
        names = ["TV", "VV", "SV", "GC", "BS", "BG", "BV", "BC", "BT"]
        for q in ("LD", "GD"):
            for r in range(self.NCHAIN):
                names.append(f"{q}{r}")
        for name in names:
            self.sems[name] = stack.enter_context(nc.semaphore(name))
            self.cnt[name] = 0
        self.qidx = {"sync": 0, "gpsimd": 0}
        self.qidx_b = {"sync": 0, "gpsimd": 0}

    def emit(self, eng, fn, waits=(), inc=None, amt=1):
        tok = None
        incs = []
        if inc is not None:
            self.cnt[inc] += amt
            tok = (inc, self.cnt[inc])
            incs.append((inc, amt))
        w = tuple(t for t in waits if t is not None)
        self.items[eng].append([fn, w, incs])
        return tok

    def dma(self, eng, out, in_, waits=(), lane="a"):
        q = "LD" if eng == "sync" else "GD"
        nch_a = 4
        if lane == "a":
            idx = self.qidx[eng]
            self.qidx[eng] = idx + 1
            r = idx % nch_a
            k = idx // nch_a
        else:
            idx = self.qidx_b[eng]
            self.qidx_b[eng] = idx + 1
            r = nch_a + idx % (self.NCHAIN - nch_a)
            k = idx // (self.NCHAIN - nch_a)
        sem = f"{q}{r}"
        w = list(waits)
        if k > 0:
            w.append((sem, 16 * k))
        return self.emit(eng, (lambda e, o=out, i=in_: e.dma_start(out=o, in_=i)),
                         w, sem, 16)

    def dma_out(self, eng, out, in_, waits=()):
        return self.dma(eng, out, in_, waits)

    def barrier(self):
        bar = {"sync": "BS", "gpsimd": "BG", "vector": "BV", "scalar": "BC",
               "tensor": "BT"}
        toks = []
        for eng in ENGS:
            b = bar[eng]
            if eng in ("sync", "gpsimd"):
                self.cnt[b] += 1
                toks.append((b, self.cnt[b]))
                q = "LD" if eng == "sync" else "GD"
                w = [(f"{q}{r}", self.cnt[f"{q}{r}"])
                     for r in range(self.NCHAIN)]
                self.items[eng].append([(lambda e: e.nop()), tuple(w),
                                        [(b, 1)]])
            else:
                toks.append(self.dummy[eng](b))
        for eng in ENGS:
            self.items[eng].append([None, tuple(toks), []])

    def finalize(self, block):
        for eng in ENGS:
            items = self.items[eng]

            def body(e, items=items, eng=eng):
                waited = {}
                for fn, waits, incs in items:
                    for (s, v) in waits:
                        if (eng, s) in SKIP_SELF:
                            continue
                        if waited.get(s, -1) >= v:
                            continue
                        e.wait_ge(self.sems[s], v)
                        waited[s] = v
                    if fn is None:
                        continue
                    inst = fn(e)
                    for (s, amt) in incs:
                        inst.then_inc(self.sems[s], amt)
            getattr(block, eng)(body)


class Bank:
    def __init__(self, ap):
        self.ap = ap
        self.writer = None
        self.readers = []

    def wwaits(self):
        w = [t for t in self.readers if t is not None]
        if self.writer is not None:
            w.append(self.writer)
        self.readers = []
        return w

    def set_writer(self, tok):
        self.writer = tok

    def rwaits(self):
        return [self.writer] if self.writer else []

    def add_reader(self, tok):
        if tok is not None:
            self.readers.append(tok)


def fm_view(dram_ap, r0, r1, c0, c1):
    v = dram_ap[r0:r1, c0:c1].rearrange("(c p) t -> c p t", p=128)
    return v.transpose([1, 0, 2])


def build_program(dbg=()):
    nc = bass.Bass("TRN2", target_bir_lowering=False,
                   detect_race_conditions=False)
    xs = nc.dram_tensor("x_seq", [D, SEQ], F32, kind="ExternalInput").ap()
    xo = nc.dram_tensor("x_own", [D, OWN], F32, kind="ExternalInput").ap()
    qkv = nc.dram_tensor("qkv", [D, 3 * D], F32, kind="ExternalInput").ap()
    wo_d = nc.dram_tensor("o_proj", [D, D], F32, kind="ExternalInput").ap()
    wup = nc.dram_tensor("w_up", [FI, D], F32, kind="ExternalInput").ap()
    wdn = nc.dram_tensor("w_down", [D, FI], F32, kind="ExternalInput").ap()
    gam = nc.dram_tensor("gammaC", [128, C], F32, kind="ExternalInput").ap()
    bet = nc.dram_tensor("betaC", [128, C], F32, kind="ExternalInput").ap()
    msk = nc.dram_tensor("masks", [16 * 128, 512], F32, kind="ExternalInput").ap()
    h128f = nc.dram_tensor("h128f", [128, 128], F32, kind="ExternalInput").ap()
    idf_d = nc.dram_tensor("identf", [128, 128], F32, kind="ExternalInput").ap()
    idb_d = nc.dram_tensor("identb", [128, 128], BF16, kind="ExternalInput").ap()
    cst_d = nc.dram_tensor("consts", [128, 8], F32, kind="ExternalInput").ap()
    onr_d = nc.dram_tensor("ones_row", [1, 128], F32, kind="ExternalInput").ap()
    out_d = nc.dram_tensor("outT", [D, OWN], F32, kind="ExternalOutput").ap()

    xn_s = nc.dram_tensor("s_xn", [D, OWN], F32).ap()
    xn2_s = nc.dram_tensor("s_xn2", [D, SEQ], BF16).ap()
    qt_s = nc.dram_tensor("s_qt", [D, OWN], BF16).ap()
    kt_s = nc.dram_tensor("s_kt", [D, SEQ], BF16).ap()
    v_s = nc.dram_tensor("s_v", [SEQ, D], BF16).ap()
    ot_s = nc.dram_tensor("s_ot", [D, OWN], BF16).ap()
    h_s = nc.dram_tensor("s_h", [FI, OWN], F32).ap()
    hq_s = nc.dram_tensor("s_hq", [FI, OWN], BF16).ap()
    g_s = nc.dram_tensor("s_g", [D, OWN], F32).ap()
    acc_s = nc.dram_tensor("s_acc", [128, 256], F32).ap()
    scr = {"s_xn": xn_s, "s_xn2": xn2_s, "s_qt": qt_s, "s_kt": kt_s,
           "s_v": v_s, "s_ot": ot_s, "s_h": h_s, "s_hq": hq_s, "s_g": g_s}
    dbg_outs = {}
    for name in dbg:
        t = scr[name]
        dbg_outs[name] = nc.dram_tensor(
            "dbg_" + name, list(t.shape), F32, kind="ExternalOutput").ap()

    with ExitStack() as st:
        en = st.enter_context
        P = Prog(nc, st)
        gamC = en(nc.sbuf_tensor("gamC", [128, C], F32))
        betC = en(nc.sbuf_tensor("betC", [128, C], F32))
        H128 = en(nc.sbuf_tensor("H128", [128, 128], F32))
        IDF = en(nc.sbuf_tensor("IDF", [128, 128], F32))
        IDB = en(nc.sbuf_tensor("IDB", [128, 128], BF16))
        CSTS = en(nc.sbuf_tensor("CSTS", [128, 8], F32))
        ONR = en(nc.sbuf_tensor("ONR", [1, 128], F32))
        ONC = en(nc.sbuf_tensor("ONC", [128, 1], F32))
        ONCB = en(nc.sbuf_tensor("ONCB", [128, 1], BF16))
        f2tm = en(nc.sbuf_tensor("f2tm", [128, 16], F32))
        m2tm = en(nc.sbuf_tensor("m2tm", [128, 16], F32))
        wrow = en(nc.sbuf_tensor("wrow", [1, 4], F32))
        WCOL = en(nc.sbuf_tensor("WCOL", [128, 2], F32))
        nc.const_aps.register(F32, -0.5, CSTS[:, 0:1])
        nc.const_aps.register(F32, 0.5, CSTS[:, 1:2])
        nc.const_aps.register(F32, EPS, CSTS[:, 2:3])
        banks = [Bank(en(nc.psum_tensor(f"pb{i}", [128, 512], F32))[:])
                 for i in range(8)]
        DUMS = en(nc.sbuf_tensor("DUMS", [1, 1], F32))

        def _dummy_vector(b):
            return P.emit("vector", lambda e: e.engine_nop(), (), b)

        def _dummy_scalar(b):
            return P.emit("scalar", (lambda e: e.activation(
                DUMS[:], CSTS[0:1, 3:4], ACT.Identity)), (), b)

        def _dummy_tensor(b):
            bk = banks[7]
            w = bk.wwaits()
            tok = P.emit("tensor", (lambda e: e.matmul(
                bk.ap[0:1, 0:1], ONC[0:1, 0:1], ONC[0:1, 0:1],
                start=True, stop=True)), w, b)
            bk.set_writer(tok)
            return tok
        P.dummy = {"vector": _dummy_vector, "scalar": _dummy_scalar,
                   "tensor": _dummy_tensor}
        blk = en(nc.Block())

        tg = P.dma("sync", gamC[:], gam[:, :])
        tbq = P.dma("sync", betC[:], bet[:, :])
        th = P.dma("sync", H128[:], h128f[:, :])
        tif = P.dma("sync", IDF[:], idf_d[:, :])
        tib = P.dma("sync", IDB[:], idb_d[:, :])
        tc = P.dma("sync", CSTS[:], cst_d[:, :])
        tonr = P.dma("sync", ONR[:], onr_d[:, :])
        tonc = P.emit("vector", lambda e: e.tensor_copy(ONC[:], CSTS[:, 3:4]),
                      [tc], "VV")
        toncb = P.emit("vector", lambda e: e.tensor_copy(ONCB[:], CSTS[:, 3:4]),
                       [tc], "VV")

        def ln_group(n, xbuf, x_tok, xb16, xb16_tok, sqbuf, xn2buf, xnbuf,
                     rows, want_xn, gd):
            b_sx, b_sq, b_mu, b_r12, b_r1 = (banks[0], banks[1], banks[2],
                                             banks[3], banks[4])
            tsq = P.emit("scalar", (lambda e, xbuf=xbuf, sqbuf=sqbuf, n=n:
                         e.activation(sqbuf[:, 0:C * n], xbuf[:, 0:C * n],
                                      ACT.Square)),
                         [x_tok] + gd.get("sq", []), "SV")

            def mk_sums(bank, src, tok):
                w = bank.wwaits() + [tok, toncb]
                t = None
                for c in range(C):
                    last = (c == C - 1)
                    t = P.emit("tensor", (lambda e, c=c, src=src, bank=bank,
                               n=n, last=last:
                               e.matmul(bank.ap[0:1, 0:n], ONCB[:],
                                        src[:, c * n:(c + 1) * n],
                                        start=(c == 0), stop=last)),
                               w if c == 0 else (), "TV" if last else None)
                bank.set_writer(t)
                return t
            t1 = mk_sums(b_sx, xb16, xb16_tok)
            t2 = mk_sums(b_sq, sqbuf, tsq)
            r = rows
            tmu = P.emit("vector", (lambda e, n=n: e.tensor_scalar_mul(
                r["mu"][:, 0:n], b_sx.ap[0:1, 0:n], 1.0 / D)),
                [t1] + gd.get("rows", []), "VV")
            b_sx.add_reader(tmu)
            tm2 = P.emit("vector", (lambda e, n=n: e.tensor_mul(
                r["tmp"][:, 0:n], r["mu"][:, 0:n], r["mu"][:, 0:n])), [tmu], "VV")
            tvar = P.emit("vector", (lambda e, n=n: e.scalar_tensor_tensor(
                r["var"][:, 0:n], b_sq.ap[0:1, 0:n], 1.0 / D, r["tmp"][:, 0:n],
                ALU.mult, ALU.subtract)), [t2, tm2], "VV")
            b_sq.add_reader(tvar)
            tl1 = P.emit("scalar", (lambda e, n=n: e.activation(
                r["tmp"][:, 0:n], r["var"][:, 0:n], ACT.Ln, bias=EPS)),
                [tvar, tc], "SV")
            tr1 = P.emit("scalar", (lambda e, n=n: e.activation(
                r["r1"][:, 0:n], r["tmp"][:, 0:n], ACT.Exp, scale=-0.5)),
                [tl1], "SV")
            ta = P.emit("vector", (lambda e, n=n: e.tensor_mul(
                r["tmp"][:, 0:n], r["var"][:, 0:n], r["r1"][:, 0:n])), [tr1], "VV")
            tb2 = P.emit("vector", (lambda e, n=n: e.tensor_mul(
                r["tmp"][:, 0:n], r["tmp"][:, 0:n], r["r1"][:, 0:n])), [ta], "VV")
            tl2 = P.emit("scalar", (lambda e, n=n: e.activation(
                r["s2r"][:, 0:n], r["tmp"][:, 0:n], ACT.Ln, bias=EPS)), [tb2], "SV")
            tr2 = P.emit("scalar", (lambda e, n=n: e.activation(
                r["s2r"][:, 0:n], r["s2r"][:, 0:n], ACT.Exp, scale=-0.5)),
                [tl2], "SV")
            tr12 = P.emit("vector", (lambda e, n=n: e.tensor_mul(
                r["r12"][:, 0:n], r["r1"][:, 0:n], r["s2r"][:, 0:n])), [tr2], "VV")

            def bcast(bank, row, tok):
                w = bank.wwaits() + [tok, tonr]
                t = P.emit("tensor", (lambda e, bank=bank, row=row, n=n:
                           e.matmul(bank.ap[:, 0:n], ONR[:], row[:, 0:n],
                                    start=True, stop=True)), w, "TV")
                bank.set_writer(t)
                return t
            tmub = bcast(b_mu, r["mu"], tmu)
            tr12b = bcast(b_r12, r["r12"], tr12)
            tr1b = bcast(b_r1, r["r1"], tr1) if want_xn else None
            tmus = P.emit("vector", (lambda e, n=n: e.tensor_copy(
                r["mub"][:, 0:n], b_mu.ap[:, 0:n])),
                [tmub] + gd.get("rowsb", []), "VV")
            b_mu.add_reader(tmus)
            tr12s = P.emit("vector", (lambda e, n=n: e.tensor_copy(
                r["r12b"][:, 0:n], b_r12.ap[:, 0:n])), [tr12b], "VV")
            b_r12.add_reader(tr12s)
            if want_xn:
                tr1s = P.emit("vector", (lambda e, n=n: e.tensor_copy(
                    r["r1b"][:, 0:n], b_r1.ap[:, 0:n])), [tr1b], "VV")
                b_r1.add_reader(tr1s)
            toks_xn2, toks_xn = [], []
            for c in range(C):
                sl = slice(c * n, (c + 1) * n)
                tt1 = P.emit("vector", (lambda e, sl=sl, xbuf=xbuf, n=n:
                             e.tensor_sub(xbuf[:, sl], xbuf[:, sl],
                                          r["mub"][:, 0:n])),
                             [tmus, tsq, t1], "VV")
                if want_xn:
                    txn = P.emit("vector", (lambda e, sl=sl, xbuf=xbuf,
                                 xnbuf=xnbuf, n=n: e.tensor_mul(
                                     xnbuf[:, sl], xbuf[:, sl],
                                     r["r1b"][:, 0:n])),
                                 [tt1, tr1s, t2] + gd.get("xn", []), "VV")
                    toks_xn.append(txn)
                tt2 = P.emit("vector", (lambda e, sl=sl, xbuf=xbuf, n=n:
                             e.tensor_mul(xbuf[:, sl], xbuf[:, sl],
                                          r["r12b"][:, 0:n])),
                             [tt1, tr12s], "VV")
                t2f = P.emit("scalar", (lambda e, sl=sl, c=c, xbuf=xbuf,
                             xn2buf=xn2buf: e.activation(
                                 xn2buf[:, sl], xbuf[:, sl], ACT.Identity,
                                 bias=betC[:, c:c + 1], scale=gamC[:, c:c + 1])),
                             [tt2, tg, tbq, t1] + gd.get("x2", []), "SV")
                toks_xn2.append(t2f)
            new_guard = {"rows": [tmub, tr12b] + ([tr1b] if tr1b else []),
                         "rowsb": toks_xn2, "sq": [t2]}
            return toks_xn2, toks_xn, new_guard

        WPHW = 512
        WPT = []
        for (_wd, _hw) in ((wup, WPHW), (wdn, WPHW)):
            _per = _wd.shape[1] // _hw
            for _ti in range((_wd.shape[0] // 128) * _per):
                WPT.append((_wd, _hw, _per, _ti))

        def wp_make(ph, lo, hi, nbuf=8):
            hw = WPHW
            wt2 = [ph.enter_context(nc.sbuf_tensor(f"wm{lo}_{i}",
                   [128, hw], F32)) for i in range(nbuf)]
            abf = ph.enter_context(nc.sbuf_tensor(f"wm{lo}a", [128, hw], F32))
            accv = ph.enter_context(nc.sbuf_tensor(f"wm{lo}c",
                                                   [128, hi - lo], F32))
            st = {"pend": [], "pos": lo, "tprev": None, "tread": {}, "cnt": 0}

            def step(k):
                for (tw, wt, gc, cnt) in st["pend"]:
                    ta = P.emit("scalar", (lambda e, wt=wt, gc=gc:
                                e.activation(abf[:], wt[:], ACT.Abs,
                                             accum_out=accv[:, gc:gc + 1])),
                                [tw, st["tprev"]], "SV")
                    st["tprev"] = ta
                    st["tread"][cnt] = ta
                st["pend"] = []
                for _ in range(min(k, hi - st["pos"])):
                    gidx = st["pos"]
                    wdram, hw_, per, ti = WPT[gidx]
                    cnt = st["cnt"]
                    wt = wt2[cnt % nbuf]
                    rt, hf = ti // per, ti % per
                    eng = "sync" if cnt % 2 == 0 else "gpsimd"
                    tw = P.dma(eng, wt[:],
                               wdram[rt * 128:(rt + 1) * 128,
                                     hf * hw_:(hf + 1) * hw_],
                               [st["tread"].get(cnt - nbuf)], lane="b")
                    st["pend"].append((tw, wt, gidx - lo, cnt))
                    st["pos"] += 1
                    st["cnt"] = cnt + 1

            def flush():
                while st["pend"] or st["pos"] < hi:
                    step(nbuf)
                P.dma("gpsimd", acc_s[:, lo:hi], accv[:],
                      [st["tprev"]], lane="b")
            return step, flush

        def _phase0():
            with ExitStack() as ph:
                n = 512
                ngr = OWN // n
                xb = [ph.enter_context(nc.sbuf_tensor("p1x0", [128, C * n],
                                                      F32))] * 2
                xnb = ph.enter_context(nc.sbuf_tensor("p1xn", [128, C * n], F32))
                x2b2 = [ph.enter_context(nc.sbuf_tensor(f"p1x2{i}",
                        [128, C * n], BF16)) for i in range(2)]
                wq = ph.enter_context(nc.sbuf_tensor("p1wq", [128, C * D], BF16))
                qst2 = [ph.enter_context(nc.sbuf_tensor("p1qs",
                        [128, C * n], BF16))] * 2
                rows = {k: ph.enter_context(nc.sbuf_tensor(f"p1r{k}", [1, n], F32))
                        for k in ("mu", "var", "r1", "r12", "tmp", "s2r")}
                for k in ("mub", "r12b", "r1b"):
                    rows[k] = ph.enter_context(
                        nc.sbuf_tensor(f"p1r{k}", [128, n], F32))
                twq = P.dma("gpsimd", wq[:].rearrange("p (c j) -> p c j", c=C),
                            fm_view(qkv, 0, D, 0, D))
                wp1s, wp1f = wp_make(ph, 0, 64, nbuf=4)
                prev = [None, None]
                guard = {}
                guards = [{}, {}]
                xguard = []
                for gi in range(ngr):
                    buf = xb[gi % 2]
                    x2b = x2b2[gi % 2]
                    qst = qst2[gi % 2]
                    guard = guards[gi % 2]
                    oth = guards[(gi + 1) % 2]
                    for key in ("rows", "rowsb", "xn", "sq"):
                        guard[key] = oth.get(key, [])
                    tx = P.dma("sync", buf[:].rearrange("p (c t) -> p c t", c=C),
                               fm_view(xo, 0, D, gi * n, (gi + 1) * n),
                               [prev[gi % 2]] + guard.get("x2", []) + xguard)
                    txc = P.dma("gpsimd",
                                x2b[:].rearrange("p (c t) -> p c t", c=C),
                                fm_view(xo, 0, D, gi * n, (gi + 1) * n),
                                guard.get("x2", []))
                    t2, txn, ng = ln_group(n, buf, tx, x2b, txc, qst, x2b,
                                           xnb, rows, True, guard)
                    guard.update(ng)
                    xguard = t2
                    tspill = P.dma("gpsimd",
                                   fm_view(xn_s, 0, D, gi * n, (gi + 1) * n),
                                   xnb[:].rearrange("p (c t) -> p c t", c=C), txn)
                    guard["xn"] = [tspill]
                    evtoks = []
                    for j in range(C):
                        wp1s(1)
                        bk = banks[5 + (j % 2)]
                        w0 = bk.wwaits() + t2 + [twq]
                        tmm = None
                        for c in range(C):
                            tmm = P.emit("tensor", (lambda e, j=j, c=c, bk=bk,
                                         n=n, x2b=x2b:
                                         e.matmul(bk.ap[:, 0:n],
                                                  wq[:, c * D + j * 128:
                                                     c * D + (j + 1) * 128],
                                                  x2b[:, c * n:(c + 1) * n],
                                                  start=(c == 0), stop=(c == C - 1))),
                                         w0 if c == 0 else (),
                                         "TV" if c == C - 1 else None)
                        bk.set_writer(tmm)
                        tev = P.emit("scalar", (lambda e, j=j, bk=bk, n=n,
                            qst=qst: e.copy(
                            qst[:, j * n:(j + 1) * n], bk.ap[:, 0:n])),
                            bk.rwaits() + [t2[0]], "SV")
                        bk.add_reader(tev)
                        evtoks.append(tev)
                    tq = P.dma("gpsimd", fm_view(qt_s, 0, D, gi * n, (gi + 1) * n),
                               qst[:].rearrange("p (c t) -> p c t", c=C), evtoks)
                    prev[gi % 2] = tq
                    guard["x2"] = [tq]
                    guard["sq"] = guard.get("sq", []) + [tq]
                    guards[gi % 2] = guard
                wp1f()
                P.barrier()

        _phase0()
        def _phase1():
            with ExitStack() as ph:
                n = 512
                ngr = SEQ // n
                xb = [ph.enter_context(nc.sbuf_tensor(f"p2x{i}", [128, C * n], F32))
                      for i in range(2)]
                x2b2 = [ph.enter_context(nc.sbuf_tensor(f"p2x2{i}",
                        [128, C * n], BF16)) for i in range(2)]
                wk = ph.enter_context(nc.sbuf_tensor("p2wk", [128, C * D], BF16))
                kst2 = [ph.enter_context(nc.sbuf_tensor("p2ks",
                        [128, C * n], BF16))] * 2
                rows = {k: ph.enter_context(nc.sbuf_tensor(f"p2r{k}", [1, n], F32))
                        for k in ("mu", "var", "r1", "r12", "tmp", "s2r")}
                for k in ("mub", "r12b", "r1b"):
                    rows[k] = ph.enter_context(
                        nc.sbuf_tensor(f"p2r{k}", [128, n], F32))
                twk = P.dma("gpsimd", wk[:].rearrange("p (c j) -> p c j", c=C),
                            fm_view(qkv, 0, D, D, 2 * D))
                wp2s, wp2f = wp_make(ph, 64, 256, nbuf=4)
                prev = [None, None]
                guard = {}
                guards = [{}, {}]
                for gi in range(ngr):
                    buf = xb[gi % 2]
                    x2b = x2b2[gi % 2]
                    kst = kst2[gi % 2]
                    guard = guards[gi % 2]
                    oth = guards[(gi + 1) % 2]
                    for key in ("rows", "rowsb", "xn", "sq"):
                        guard[key] = oth.get(key, [])
                    tx = P.dma("sync", buf[:].rearrange("p (c t) -> p c t", c=C),
                               fm_view(xs, 0, D, gi * n, (gi + 1) * n),
                               [prev[gi % 2]])
                    txc = P.dma("gpsimd",
                                x2b[:].rearrange("p (c t) -> p c t", c=C),
                                fm_view(xs, 0, D, gi * n, (gi + 1) * n),
                                guard.get("x2", []))
                    t2, _, ng = ln_group(n, buf, tx, x2b, txc, kst, x2b,
                                         None, rows, False, guard)
                    guard.update(ng)
                    tx2 = P.dma("gpsimd",
                                fm_view(xn2_s, 0, D, gi * n, (gi + 1) * n),
                                x2b[:].rearrange("p (c t) -> p c t", c=C), t2)
                    evtoks = []
                    for j in range(C):
                        wp2s(2)
                        bk = banks[5 + (j % 2)]
                        w0 = bk.wwaits() + t2 + [twk]
                        tmm = None
                        for c in range(C):
                            tmm = P.emit("tensor", (lambda e, j=j, c=c, bk=bk,
                                         n=n, x2b=x2b:
                                         e.matmul(bk.ap[:, 0:n],
                                                  wk[:, c * D + j * 128:
                                                     c * D + (j + 1) * 128],
                                                  x2b[:, c * n:(c + 1) * n],
                                                  start=(c == 0), stop=(c == C - 1))),
                                         w0 if c == 0 else (),
                                         "TV" if c == C - 1 else None)
                        bk.set_writer(tmm)
                        tev = P.emit("scalar", (lambda e, j=j, bk=bk, n=n,
                            kst=kst: e.copy(
                            kst[:, j * n:(j + 1) * n], bk.ap[:, 0:n])),
                            bk.rwaits() + [t2[0]], "SV")
                        bk.add_reader(tev)
                        evtoks.append(tev)
                    tk = P.dma("gpsimd", fm_view(kt_s, 0, D, gi * n, (gi + 1) * n),
                               kst[:].rearrange("p (c t) -> p c t", c=C), evtoks)
                    prev[gi % 2] = tk
                    guard["x2"] = [tk, tx2]
                    guard["sq"] = guard.get("sq", []) + [tk]
                    guards[gi % 2] = guard
                wp2f()
                P.barrier()

        _phase1()
        def _phase2():
            with ExitStack() as ph:
                n = 512
                ngr = SEQ // n
                x2b = [ph.enter_context(nc.sbuf_tensor(f"p2bx{i}", [128, C * n],
                                                       BF16)) for i in range(2)]
                wv = ph.enter_context(nc.sbuf_tensor("p2bwv", [128, C * D], BF16))
                vst = ph.enter_context(nc.sbuf_tensor("p2bvs", [128, 4 * D], BF16))
                twv = P.dma("gpsimd", wv[:].rearrange("p (c j) -> p c j", c=C),
                            fm_view(qkv, 0, D, 2 * D, 3 * D))
                prev = [None, None]
                vspill = None
                for gi in range(ngr):
                    buf = x2b[gi % 2]
                    tx = P.dma("sync", buf[:].rearrange("p (c t) -> p c t", c=C),
                               fm_view(xn2_s, 0, D, gi * n, (gi + 1) * n),
                               [prev[gi % 2]])
                    evtoks = []
                    rdtoks = []
                    for ts in range(4):
                        for vn in range(4):
                            bk = banks[(ts * 4 + vn) % 4]
                            w0 = bk.wwaits() + [tx, twv]
                            tmm = None
                            for c in range(C):
                                tmm = P.emit("tensor",
                                             (lambda e, ts=ts, vn=vn, c=c, bk=bk,
                                              buf=buf, n=n:
                                              e.matmul(bk.ap[:, 0:512],
                                                       buf[:, c * n + ts * 128:
                                                          c * n + (ts + 1) * 128],
                                                       wv[:, c * D + vn * 512:
                                                          c * D + (vn + 1) * 512],
                                                       start=(c == 0),
                                                       stop=(c == C - 1))),
                                             w0 if c == 0 else (),
                                             "TV" if c == C - 1 else None)
                            bk.set_writer(tmm)
                            rdtoks.append(tmm)
                            tev = P.emit("scalar", (lambda e, ts=ts, vn=vn, bk=bk:
                                         e.copy(vst[:, ts * D + vn * 512:
                                                    ts * D + (vn + 1) * 512],
                                                bk.ap[:, 0:512])),
                                         bk.rwaits() + ([vspill] if vspill else []),
                                         "SV")
                            bk.add_reader(tev)
                            evtoks.append(tev)
                    vspill = P.dma("gpsimd",
                                   v_s[gi * n:(gi + 1) * n, :]
                                   .rearrange("(s p) d -> s p d", p=128)
                                   .transpose([1, 0, 2]),
                                   vst[:].rearrange("p (s d) -> p s d", s=4),
                                   evtoks)
                    prev[gi % 2] = P.emit("vector", lambda e: e.engine_nop(),
                                          rdtoks, "VV")
                P.barrier()

        _phase2()
        def _phase3():
            with ExitStack() as ph:
                qsb = ph.enter_context(nc.sbuf_tensor("p3q", [128, C * 512], BF16))
                ksb = [ph.enter_context(nc.sbuf_tensor(f"p3k{i}", [128, C * 512],
                                                       BF16)) for i in range(2)]
                vsb = [ph.enter_context(nc.sbuf_tensor(f"p3v{i}", [128, 4 * D],
                                                       BF16)) for i in range(2)]
                oacc = ph.enter_context(nc.sbuf_tensor("p3o", [128, 4 * D], F32))
                mrow = [ph.enter_context(nc.sbuf_tensor(f"p3m{i}", [128, 4], F32))
                        for i in range(2)]
                lrow = ph.enter_context(nc.sbuf_tensor("p3l", [128, 4], F32))
                srow = ph.enter_context(nc.sbuf_tensor("p3s", [128, 4], F32))
                drow = ph.enter_context(nc.sbuf_tensor("p3d", [128, 4], F32))
                nbr = ph.enter_context(nc.sbuf_tensor("p3nb", [128, 4], F32))
                rmax = ph.enter_context(nc.sbuf_tensor("p3rm", [128, 4], F32))
                masks = ph.enter_context(nc.sbuf_tensor("p3msk", [128, 16 * 512],
                                                        F32))
                smsk = ph.enter_context(nc.sbuf_tensor("p3sm", [128, 4 * 512],
                                                       F32))
                pb16 = ph.enter_context(nc.sbuf_tensor("p3p", [128, 4 * 512],
                                                       BF16))
                ptb = ph.enter_context(nc.sbuf_tensor("p3pt", [128, 512], BF16))
                ob16 = ph.enter_context(nc.sbuf_tensor("p3ob", [128, D], BF16))
                otst = ph.enter_context(nc.sbuf_tensor("p3ot", [128, C * 512],
                                                       BF16))
                linv = ph.enter_context(nc.sbuf_tensor("p3li", [128, 4], F32))
                tmsk = P.dma("sync",
                             masks[:].rearrange("p (q t) -> p q t", q=16),
                             msk[:, :].rearrange("(q p) t -> q p t", p=128)
                             .transpose([1, 0, 2]))
                b_sc4 = [banks[0], banks[1], banks[2], banks[3]]
                b_pt = banks[4]
                b_av2 = [banks[5], banks[6]]
                b_fin = banks[7]
                for sti in range(4):
                    kmax = CNT[TILES_ROLE[0][sti * 4 + 3]]
                    tq = P.dma("sync", qsb[:].rearrange("p (c t) -> p c t", c=C),
                               fm_view(qt_s, 0, D, sti * 512, (sti + 1) * 512))
                    oacc_done = {qi: [] for qi in range(4)}
                    prev_tex = {}
                    prev_tl = {}
                    prev_tdx = {}
                    prev_smsk = {}
                    prev_ptm = {}
                    kprev = [None, None]
                    vprev = [None, None]
                    for kt in range(kmax):
                        kb = ksb[kt % 2]
                        vb = vsb[kt % 2]
                        tkl = P.dma("sync", kb[:].rearrange("p (c t) -> p c t", c=C),
                                    fm_view(kt_s, 0, D, kt * 512, (kt + 1) * 512),
                                    [kprev[kt % 2]])
                        tvl = P.dma("sync",
                                    vb[:].rearrange("p (s d) -> p s d", s=4),
                                    v_s[kt * 512:(kt + 1) * 512, :]
                                    .rearrange("(s p) d -> s p d", p=128)
                                    .transpose([1, 0, 2]), [vprev[kt % 2]])
                        kread, vread = [], []
                        active = [qi for qi in range(4)
                                  if kt < CNT[TILES_ROLE[0][sti * 4 + qi]]]
                        texs, tdxs = {}, {}
                        for qi in active:
                            g_cnt = CNT[TILES_ROLE[0][sti * 4 + qi]]
                            first = (kt == 0)
                            diag = (kt == g_cnt - 1)
                            bsc = b_sc4[qi]
                            w0 = bsc.wwaits() + [tq, tkl]
                            tmm = None
                            for c in range(C):
                                tmm = P.emit("tensor",
                                             (lambda e, qi=qi, c=c, bsc=bsc, kb=kb:
                                              e.matmul(bsc.ap[:, 0:512],
                                                       qsb[:, c * 512 + qi * 128:
                                                          c * 512 + (qi + 1) * 128],
                                                       kb[:, c * 512:(c + 1) * 512],
                                                       start=(c == 0),
                                                       stop=(c == C - 1))),
                                             w0 if c == 0 else (),
                                             "TV" if c == C - 1 else None)
                            bsc.set_writer(tmm)
                            kread.append(tmm)
                            if diag:
                                qidx = sti * 4 + qi
                                msl = slice(qi * 512, (qi + 1) * 512)
                                tmadd = P.emit("vector",
                                               (lambda e, qidx=qidx, bsc=bsc,
                                                msl=msl:
                                                e.tensor_add(
                                                    smsk[:, msl], bsc.ap[:, 0:512],
                                                    masks[:, qidx * 512:
                                                          (qidx + 1) * 512])),
                                               bsc.rwaits() + [tmsk,
                                               prev_smsk.get(qi)], "VV")
                                bsc.add_reader(tmadd)
                                sc_src = smsk[:, msl]
                                tprep = [tmadd]
                            else:
                                sc_src = bsc.ap[:, 0:512]
                                tprep = bsc.rwaits()
                            trm = P.emit("vector", (lambda e, qi=qi, s=sc_src:
                                         e.reduce_max(rmax[:, qi:qi + 1], s,
                                                      axis=AX.X)), tprep, "VV")
                            if not diag:
                                bsc.add_reader(trm)
                            mo, mn = mrow[kt % 2], mrow[(kt + 1) % 2]
                            tdx = None
                            if first:
                                tmn = P.emit("vector", (lambda e, qi=qi, mn=mn:
                                             e.tensor_copy(mn[:, qi:qi + 1],
                                                           rmax[:, qi:qi + 1])),
                                             [trm], "VV")
                            else:
                                tmn = P.emit("vector", (lambda e, qi=qi, mo=mo, mn=mn:
                                             e.tensor_max(mn[:, qi:qi + 1],
                                                          mo[:, qi:qi + 1],
                                                          rmax[:, qi:qi + 1])),
                                             [trm], "VV")
                                tda = P.emit("vector", (lambda e, qi=qi, mo=mo, mn=mn:
                                             e.tensor_sub(drow[:, qi:qi + 1],
                                                          mo[:, qi:qi + 1],
                                                          mn[:, qi:qi + 1])),
                                             [tmn, prev_tdx.get(qi)], "VV")
                                tdx = P.emit("scalar", (lambda e, qi=qi:
                                             e.activation(drow[:, qi:qi + 1],
                                                          drow[:, qi:qi + 1],
                                                          ACT.Exp, scale=CS)),
                                             [tda], "SV")
                            tnb = P.emit("vector", (lambda e, qi=qi, mn=mn:
                                         e.tensor_scalar_mul(nbr[:, qi:qi + 1],
                                                             mn[:, qi:qi + 1],
                                                             -CS)),
                                         [tmn, prev_tex.get(qi)], "VV")
                            psl = slice(qi * 512, (qi + 1) * 512)
                            tex = P.emit("scalar", (lambda e, qi=qi, s=sc_src,
                                         psl=psl:
                                         e.activation(pb16[:, psl], s, ACT.Exp,
                                                      bias=nbr[:, qi:qi + 1],
                                                      scale=CS,
                                                      accum_out=srow[:, qi:qi + 1])),
                                         [tnb, prev_tl.get(qi),
                                          prev_ptm.get(qi)] + tprep, "SV")
                            prev_tex[qi] = tex
                            texs[qi] = tex
                            tdxs[qi] = tdx
                            if not diag:
                                bsc.add_reader(tex)
                            if diag:
                                prev_smsk[qi] = tex
                            if first:
                                tl = P.emit("vector", (lambda e, qi=qi:
                                            e.tensor_copy(lrow[:, qi:qi + 1],
                                                          srow[:, qi:qi + 1])),
                                            [tex], "VV")
                            else:
                                tl = P.emit("vector", (lambda e, qi=qi:
                                            e.scalar_tensor_tensor(
                                                lrow[:, qi:qi + 1],
                                                lrow[:, qi:qi + 1],
                                                drow[:, qi:qi + 1],
                                                srow[:, qi:qi + 1],
                                                ALU.mult, ALU.add)),
                                            [tex, tdx], "VV")
                            prev_tl[qi] = tl
                        for qi in active:
                            g_cnt = CNT[TILES_ROLE[0][sti * 4 + qi]]
                            first = (kt == 0)
                            diag = (kt == g_cnt - 1)
                            tdx = tdxs[qi]
                            psl = slice(qi * 512, (qi + 1) * 512)
                            wpt = b_pt.wwaits() + [texs[qi], tib]
                            tptm = None
                            for k4 in range(4):
                                tptm = P.emit("tensor", (lambda e, k4=k4, psl=psl,
                                              qi=qi:
                                              e.matmul(b_pt.ap[:, k4 * 128:
                                                               (k4 + 1) * 128],
                                                       pb16[:, qi * 512 + k4 * 128:
                                                            qi * 512 +
                                                            (k4 + 1) * 128],
                                                       IDB[:], start=True,
                                                       stop=True)),
                                              wpt if k4 == 0 else (),
                                              "TV" if k4 == 3 else None)
                            b_pt.set_writer(tptm)
                            prev_ptm[qi] = tptm
                            tpte = P.emit("scalar", lambda e: e.copy(ptb[:],
                                          b_pt.ap[:]), b_pt.rwaits(), "SV")
                            b_pt.add_reader(tpte)
                            for vn in range(4):
                                bav = b_av2[vn % 2]
                                wav = bav.wwaits() + [tpte, tvl]
                                tav = None
                                for k4 in range(4):
                                    tav = P.emit("tensor",
                                                 (lambda e, vn=vn, k4=k4, bav=bav,
                                                  vb=vb:
                                                  e.matmul(bav.ap[:, 0:512],
                                                           ptb[:, k4 * 128:
                                                               (k4 + 1) * 128],
                                                           vb[:, k4 * D + vn * 512:
                                                              k4 * D +
                                                              (vn + 1) * 512],
                                                           start=(k4 == 0),
                                                           stop=(k4 == 3))),
                                                 wav if k4 == 0 else (),
                                                 "TV" if k4 == 3 else None)
                                bav.set_writer(tav)
                                vread.append(tav)
                                osl = slice(qi * D + vn * 512,
                                            qi * D + (vn + 1) * 512)
                                if first:
                                    tou = P.emit("vector",
                                                 (lambda e, osl=osl, bav=bav:
                                                  e.tensor_copy(oacc[:, osl],
                                                                bav.ap[:, 0:512])),
                                                 bav.rwaits(), "VV")
                                else:
                                    tou = P.emit("vector",
                                                 (lambda e, osl=osl, qi=qi, bav=bav:
                                                  e.scalar_tensor_tensor(
                                                      oacc[:, osl], oacc[:, osl],
                                                      drow[:, qi:qi + 1],
                                                      bav.ap[:, 0:512],
                                                      ALU.mult, ALU.add)),
                                                 bav.rwaits() + [tdx], "VV")
                                bav.add_reader(tou)
                                if diag:
                                    oacc_done[qi].append(tou)
                        kprev[kt % 2] = (P.emit("vector", lambda e: e.engine_nop(),
                                                kread, "VV") if kread else None)
                        vprev[kt % 2] = (P.emit("vector", lambda e: e.engine_nop(),
                                                vread, "VV") if vread else None)
                    evt = []
                    prev_tt = []
                    for qi in range(4):
                        tri = P.emit("vector", (lambda e, qi=qi:
                                     e.reciprocal(linv[:, qi:qi + 1],
                                                  lrow[:, qi:qi + 1])),
                                     oacc_done[qi], "VV")
                        tob = P.emit("scalar", (lambda e, qi=qi:
                                     e.activation(ob16[:],
                                                  oacc[:, qi * D:(qi + 1) * D],
                                                  ACT.Identity,
                                                  scale=linv[:, qi:qi + 1])),
                                     [tri] + oacc_done[qi] + prev_tt, "SV")
                        prev_tt = []
                        for grp in range(4):
                            wfin = b_fin.wwaits() + [tob, tib]
                            tt = None
                            for k4 in range(4):
                                oc = grp * 4 + k4
                                tt = P.emit("tensor", (lambda e, oc=oc, k4=k4:
                                            e.matmul(b_fin.ap[:, k4 * 128:
                                                              (k4 + 1) * 128],
                                                     ob16[:, oc * 128:
                                                          (oc + 1) * 128],
                                                     IDB[:], start=True,
                                                     stop=True)),
                                            wfin if k4 == 0 else (),
                                            "TV" if k4 == 3 else None)
                            b_fin.set_writer(tt)
                            prev_tt.append(tt)
                            dst = otst[:].rearrange("p (c t) -> p c t", c=C)[
                                :, grp * 4:(grp + 1) * 4, qi * 128:(qi + 1) * 128]
                            tte = P.emit("scalar", (lambda e, dst=dst:
                                         e.copy(dst, b_fin.ap[:].rearrange(
                                             "p (c t) -> p c t", c=4))),
                                         b_fin.rwaits(), "SV")
                            b_fin.add_reader(tte)
                            evt.append(tte)
                    P.dma("gpsimd", fm_view(ot_s, 0, D, sti * 512, (sti + 1) * 512),
                          otst[:].rearrange("p (c t) -> p c t", c=C), evt)
                    P.barrier()

        _phase3()
        def weight_prep(ph, wdram, nrows, ncol, wq_t, widx):
            ntil = nrows // 128
            nch = ncol // 128
            NW = 4 if ncol <= 2048 else 2
            wt2 = [ph.enter_context(nc.sbuf_tensor(f"wp{widx}w{i}",
                   [128, ncol], F32)) for i in range(NW)]
            ab = None
            s1b = ph.enter_context(nc.sbuf_tensor(f"wp{widx}s1", [128, ncol],
                                                  F32))
            s2b = ph.enter_context(nc.sbuf_tensor(f"wp{widx}s2", [128, ncol],
                                                  F32))
            trn = ph.enter_context(nc.sbuf_tensor(f"wp{widx}t", [128, ncol],
                                                  BF16))
            rsum = ph.enter_context(nc.sbuf_tensor(f"wp{widx}r", [1, 1], F32))
            invc = ph.enter_context(nc.sbuf_tensor(f"wp{widx}i", [128, 1],
                                                   F32))
            accb = ph.enter_context(nc.sbuf_tensor(f"wp{widx}ac",
                                                    [128, 128], F32))
            t1 = ph.enter_context(nc.sbuf_tensor(f"wp{widx}t1", [128, 1], F32))
            tld = P.dma("sync", accb[:],
                        acc_s[:, widx * 128:(widx + 1) * 128])
            bsum = banks[0]
            wsum = bsum.wwaits()
            tr1 = P.emit("vector", (lambda e: e.reduce_sum(
                t1[:], accb[:, 0:128], axis=AX.X)), [tld], "VV")
            tsl = P.emit("tensor", (lambda e: e.matmul(
                bsum.ap[0:1, 0:1], t1[:], ONC[:], start=True, stop=True)),
                wsum + [tr1, tonc], "TV")
            bsum.set_writer(tsl)
            trs = P.emit("vector", lambda e: e.tensor_copy(
                rsum[:], bsum.ap[0:1, 0:1]), bsum.rwaits(), "VV")
            bsum.add_reader(trs)
            tms = P.emit("vector", (lambda e: e.tensor_scalar(
                wrow[:, 2 * widx:2 * widx + 1], rsum[:], 1.0 / (1 << 23), EPS,
                ALU.mult, ALU.max)), [trs], "VV")
            tiv = P.emit("vector", (lambda e: e.reciprocal(
                wrow[:, 2 * widx + 1:2 * widx + 2],
                wrow[:, 2 * widx:2 * widx + 1])), [tms], "VV")
            tkr = P.emit("vector", (lambda e: e.tensor_scalar_mul(
                rsum[:], wrow[:, 2 * widx:2 * widx + 1], 0.5 / QB)),
                [tms, trs], "VV")
            binv = banks[1]
            wbi = binv.wwaits()
            tbi = P.emit("tensor", (lambda e: e.matmul(
                binv.ap[:, 0:1], ONR[:], wrow[:, 2 * widx + 1:2 * widx + 2],
                start=True, stop=True)), wbi + [tiv, tonr], "TV")
            tbk = P.emit("tensor", (lambda e: e.matmul(
                binv.ap[:, 1:2], ONR[:], rsum[:], start=True, stop=True)),
                [tkr], "TV")
            binv.set_writer(tbk)
            tin = P.emit("vector", lambda e: e.tensor_copy(invc[:],
                         binv.ap[:, 0:1]), [tbi], "VV")
            tkc = P.emit("vector", (lambda e: e.tensor_copy(
                WCOL[:, widx:widx + 1], binv.ap[:, 1:2])), [tbk], "VV")
            binv.add_reader(tin)
            binv.add_reader(tkc)
            wq_toks = []
            tprev2 = None
            tread2 = {}
            for ti in range(ntil):
                wt = wt2[ti % NW]
                eng = "sync" if ti % 2 == 0 else "gpsimd"
                tw = P.dma(eng, wt[:], wdram[ti * 128:(ti + 1) * 128, :],
                           [tread2.get(ti - NW, tin)])
                ts1 = P.emit("scalar", (lambda e, wt=wt: e.activation(
                    s1b[:], wt[:], ACT.Sign, bias=-0.5, scale=invc[:, 0:1])),
                    [tw, tin, tc, tprev2], "SV")
                ts2 = P.emit("scalar", (lambda e, wt=wt: e.activation(
                    s2b[:], wt[:], ACT.Sign, bias=0.5, scale=invc[:, 0:1])),
                    [ts1], "SV")
                tprev2 = ts2
                tread2[ti] = ts2
                tt3 = P.emit("vector", (lambda e: e.tensor_add(
                    trn[:], s1b[:], s2b[:])), [ts2], "VV")
                for grp in range((nch + 3) // 4):
                    bkt = banks[2 + (grp % 2)]
                    wt_ = bkt.wwaits() + [tt3, tib]
                    k4n = min(4, nch - grp * 4)
                    tt = None
                    for k4 in range(k4n):
                        cc = grp * 4 + k4
                        tt = P.emit("tensor", (lambda e, cc=cc, k4=k4, bkt=bkt:
                                    e.matmul(bkt.ap[:, k4 * 128:(k4 + 1) * 128],
                                             trn[:, cc * 128:(cc + 1) * 128],
                                             IDB[:], start=True, stop=True)),
                                    wt_ if k4 == 0 else (),
                                    "TV" if k4 == k4n - 1 else None)
                    bkt.set_writer(tt)
                    dst = wq_t[:].rearrange("p (c f) -> p c f", c=nch)[
                        :, grp * 4:grp * 4 + k4n, ti * 128:(ti + 1) * 128]
                    tte = P.emit("scalar", (lambda e, dst=dst, bkt=bkt,
                                 k4n=k4n: e.copy(dst,
                                     bkt.ap[:, 0:k4n * 128].rearrange(
                                         "p (c t) -> p c t", c=k4n))),
                                 bkt.rwaits(), "SV")
                    bkt.add_reader(tte)
                    wq_toks.append(tte)
            return wq_toks

        def quant_rounds(tag, nch, n, ubuf, utoks, qout, ftm, mstore,
                         qint, n0b, rbuf, s2b2, sc_r, sc_q, qguard,
                         rbuf2=None):
            nt = n // 128
            if rbuf2 is None:
                rbuf2 = rbuf
            cmul = C1 if tag == 1 else C2
            t = utoks
            tmax = P.emit("vector", (lambda e: e.tensor_copy(rbuf[:, 0:n],
                         ubuf[:, 0:n])), t + qguard, "VV")
            tmin = P.emit("vector", (lambda e: e.tensor_copy(rbuf[:, n:2 * n],
                         ubuf[:, 0:n])), t, "VV")
            for cc2 in range(1, nch):
                tmax = P.emit("vector", (lambda e, cc2=cc2: e.tensor_max(
                    rbuf[:, 0:n], rbuf[:, 0:n],
                    ubuf[:, cc2 * n:(cc2 + 1) * n])), [tmax], "VV")
                tmin = P.emit("vector", (lambda e, cc2=cc2: e.tensor_tensor(
                    rbuf[:, n:2 * n], rbuf[:, n:2 * n],
                    ubuf[:, cc2 * n:(cc2 + 1) * n], ALU.min)), [tmin], "VV")
            tam = P.emit("vector", (lambda e: e.scalar_tensor_tensor(
                rbuf[:, 0:n], rbuf[:, n:2 * n], -1.0, rbuf[:, 0:n],
                ALU.mult, ALU.max)), [tmax, tmin], "VV")
            btr = banks[5]
            wtr = btr.wwaits() + [tam, tif]
            tt = None
            for k4 in range(nt):
                tt = P.emit("tensor", (lambda e, k4=k4: e.matmul(
                    btr.ap[:, k4 * 128:(k4 + 1) * 128],
                    rbuf[:, k4 * 128:(k4 + 1) * 128], IDF[:],
                    start=True, stop=True)), wtr if k4 == 0 else (),
                    "TV" if k4 == nt - 1 else None)
            btr.set_writer(tt)
            ttm = []
            for k4 in range(nt):
                tr = P.emit("vector", (lambda e, k4=k4: e.reduce_max(
                    sc_r[:, k4:k4 + 1], btr.ap[:, k4 * 128:(k4 + 1) * 128],
                    axis=AX.X)), btr.rwaits(), "VV")
                btr.add_reader(tr)
                ttm.append(tr)
            if ftm is not None:
                tsc = P.emit("vector", (lambda e: e.tensor_mul(
                    sc_r[:, 0:nt], sc_r[:, 0:nt], ftm)), ttm, "VV")
            else:
                tsc = ttm[-1]
            tm_ = P.emit("vector", (lambda e: e.tensor_scalar(
                sc_r[:, 0:nt], sc_r[:, 0:nt], cmul, EPS, ALU.mult, ALU.max)),
                [tsc] + ttm, "VV")
            tok_m = None
            if mstore is not None:
                tok_m = P.emit("vector", (lambda e: e.tensor_copy(
                    mstore, sc_r[:, 0:nt])), [tm_], "VV")
            tqm = P.emit("vector", (lambda e: e.reciprocal(
                sc_q[:, 0:nt], sc_r[:, 0:nt])), [tm_], "VV")
            if ftm is not None:
                tqm = P.emit("vector", (lambda e: e.tensor_mul(
                    sc_q[:, 0:nt], sc_q[:, 0:nt], ftm)), [tqm], "VV")
            tqm = P.emit("vector", (lambda e: e.tensor_scalar_mul(
                sc_q[:, 0:nt], sc_q[:, 0:nt], QB * cmul)), [tqm], "VV")
            brow = banks[5]
            wrw = brow.wwaits() + [tqm, tif]
            tt = None
            for k4 in range(nt):
                tt = P.emit("tensor", (lambda e, k4=k4: e.matmul(
                    brow.ap[0:1, k4 * 128:(k4 + 1) * 128],
                    sc_q[:, k4:k4 + 1], IDF[:], start=True, stop=True)),
                    wrw if k4 == 0 else (), "TV" if k4 == nt - 1 else None)
            brow.set_writer(tt)
            trow = P.emit("vector", (lambda e: e.tensor_copy(
                rbuf[0:1, 0:n], brow.ap[0:1, 0:n])), brow.rwaits(), "VV")
            brow.add_reader(trow)
            brep = banks[6]
            wrep = brep.wwaits() + [trow, tonr]
            trep = P.emit("tensor", (lambda e: e.matmul(
                brep.ap[:, 0:n], ONR[:], rbuf[0:1, 0:n],
                start=True, stop=True)), wrep, "TV")
            brep.set_writer(trep)
            trepc = P.emit("vector", (lambda e: e.tensor_copy(
                rbuf[:, n:2 * n], brep.ap[:, 0:n])), brep.rwaits(), "VV")
            brep.add_reader(trepc)
            qtoks = []
            nb = min(nch, qint.shape[1] // n)
            for cb in range(0, nch, nb):
                nbb = min(nb, nch - cb)
                for cc2 in range(cb, cb + nbb):
                    sl = slice(cc2 * n, (cc2 + 1) * n)
                    P.emit("vector", (lambda e, sl=sl: e.tensor_mul(
                        ubuf[:, sl], ubuf[:, sl], rbuf[:, n:2 * n])),
                        [trepc], "VV")
                bsl = slice(cb * n, (cb + nbb) * n)
                m = nbb * n
                P.emit("vector", (lambda e, bsl=bsl, m=m: e.tensor_copy(
                    qint[:, 0:m], ubuf[:, bsl])), (), "VV")
                P.emit("vector", (lambda e, bsl=bsl, m=m:
                    e.scalar_tensor_tensor(
                        n0b[:, 0:m], qint[:, 0:m], 0.5, ubuf[:, bsl],
                        ALU.add, ALU.is_le)), (), "VV")
                P.emit("vector", (lambda e, bsl=bsl, m=m:
                    e.scalar_tensor_tensor(
                        s2b2[:, 0:m], qint[:, 0:m], -0.5, ubuf[:, bsl],
                        ALU.add, ALU.is_ge)), (), "VV")
                P.emit("vector", (lambda e, m=m: e.tensor_sub(
                    n0b[:, 0:m], n0b[:, 0:m], s2b2[:, 0:m])), (), "VV")
                tqf = P.emit("vector", (lambda e, bsl=bsl, m=m: e.tensor_tensor(
                    qout[:, bsl], qint[:, 0:m], n0b[:, 0:m], ALU.add)),
                    (), "VV")
                qtoks.append(tqf)
            return qtoks, tok_m

        def fwht_stages(nch, n, a, b, toks, gfrac=0):
            nst = int(math.log2(nch))
            cur, oth = a, b
            t = list(toks)
            for stg in range(nst):
                s = 1 << stg
                nblk = nch // (2 * s)
                av = cur[:].rearrange("p (blk two s t) -> p blk two s t",
                                      blk=nblk, two=2, s=s)
                ov = oth[:].rearrange("p (blk two s t) -> p blk two s t",
                                      blk=nblk, two=2, s=s)
                gb = int(nblk * gfrac)
                vb = nblk - gb
                tt = []
                tt.append(P.emit("vector", (lambda e, av=av, ov=ov, vb=vb:
                          e.tensor_add(ov[:, 0:vb, 0], av[:, 0:vb, 0],
                                       av[:, 0:vb, 1])), t, "VV"))
                tt.append(P.emit("vector", (lambda e, av=av, ov=ov, vb=vb:
                          e.tensor_sub(ov[:, 0:vb, 1], av[:, 0:vb, 0],
                                       av[:, 0:vb, 1])), t, "VV"))
                if gb:
                    tt.append(P.emit("gpsimd", (lambda e, av=av, ov=ov, vb=vb,
                              nblk=nblk: e.tensor_add(
                                  ov[:, vb:nblk, 0], av[:, vb:nblk, 0],
                                  av[:, vb:nblk, 1])), t, "GC"))
                    tt.append(P.emit("gpsimd", (lambda e, av=av, ov=ov, vb=vb,
                              nblk=nblk: e.tensor_sub(
                                  ov[:, vb:nblk, 1], av[:, vb:nblk, 0],
                                  av[:, vb:nblk, 1])), t, "GC"))
                t = tt
                cur, oth = oth, cur
            return cur, oth, t

        def _phase4():
            with ExitStack() as ph:
                wqt = ph.enter_context(nc.sbuf_tensor("p4wqt", [128, C * FI],
                                                      BF16))
                with ExitStack() as pp:
                    weight_prep(pp, wup, FI, D, wqt, 0)
                    P.barrier()
                n = 256
                ngr = OWN // n
                xnb = ph.enter_context(nc.sbuf_tensor("p4xn", [128, C * n], F32))
                pong = ph.enter_context(nc.sbuf_tensor("p4pg", [128, C * n], F32))
                s2b2 = ph.enter_context(nc.sbuf_tensor("p4s2", [128, n], F32))
                n0b = ph.enter_context(nc.sbuf_tensor("p4n0", [128, n], F32))
                rbuf = ph.enter_context(nc.sbuf_tensor("p4rb", [128, 2 * n], F32))
                qint = ph.enter_context(nc.sbuf_tensor("p4qi", [128, n], I32))
                xqb = ph.enter_context(nc.sbuf_tensor("p4xq", [128, C * n], BF16))
                hst2 = ph.enter_context(nc.sbuf_tensor("p4h2", [128, FC * n], F32))
                hst = n0b
                sc_r = ph.enter_context(nc.sbuf_tensor("p4sr", [128, 2], F32))
                sc_q = ph.enter_context(nc.sbuf_tensor("p4sq", [128, 2], F32))
                prevx = None
                prevh = None
                for gi in range(ngr):
                    tx = P.dma("sync", xnb[:].rearrange("p (c t) -> p c t", c=C),
                               fm_view(xn_s, 0, D, gi * n, (gi + 1) * n), [prevx])
                    fb, ob, ft = fwht_stages(C, n, xnb, pong, [tx])
                    utoks = []
                    for cc in range(C):
                        bk = banks[cc % 4]
                        w0 = bk.wwaits() + ft + [th]
                        tmm = P.emit("tensor", (lambda e, cc=cc, fb=fb, bk=bk, n=n:
                                     e.matmul(bk.ap[:, 0:n], H128[:],
                                              fb[:, cc * n:(cc + 1) * n],
                                              start=True, stop=True)), w0, "TV")
                        bk.set_writer(tmm)
                        te = P.emit("scalar", (lambda e, cc=cc, ob=ob, bk=bk, n=n:
                                    e.copy(ob[:, cc * n:(cc + 1) * n],
                                           bk.ap[:, 0:n])),
                                    bk.rwaits(), "SV")
                        bk.add_reader(te)
                        utoks.append(te)
                    qtoks, _ = quant_rounds(1, C, n, ob, [utoks[-1]], xqb, None,
                                            None, qint, n0b, rbuf, s2b2,
                                            sc_r, sc_q, [prevh] if prevh else [])
                    tf = P.emit("vector", (lambda e, gi=gi: e.tensor_scalar(
                        f2tm[:, 2 * gi:2 * gi + 2], sc_r[:, 0:2],
                        WCOL[:, 0:1], None, ALU.mult)), [qtoks[0]], "VV")
                    tf2 = P.emit("vector", (lambda e, gi=gi: e.tensor_mul(
                        f2tm[:, 2 * gi:2 * gi + 2], f2tm[:, 2 * gi:2 * gi + 2],
                        f2tm[:, 2 * gi:2 * gi + 2])), [tf], "VV")
                    sqtoks = []
                    for j in range(FC):
                        bk = banks[4 + (j % 2)]
                        w0 = bk.wwaits() + [qtoks[-1]]
                        tmm = None
                        for c in range(C):
                            tmm = P.emit("tensor", (lambda e, j=j, c=c, bk=bk, n=n:
                                         e.matmul(bk.ap[:, 0:n],
                                                  wqt[:, c * FI + j * 128:
                                                     c * FI + (j + 1) * 128],
                                                  xqb[:, c * n:(c + 1) * n],
                                                  start=(c == 0),
                                                  stop=(c == C - 1))),
                                         w0 if c == 0 else (),
                                         "TV" if c == C - 1 else None)
                        bk.set_writer(tmm)
                        trl = P.emit("scalar", (lambda e, bk=bk, j=j, n=n:
                                     e.activation(hst2[:, j * n:(j + 1) * n],
                                                  bk.ap[:, 0:n], ACT.Relu)),
                            bk.rwaits() + [prevh], "SV")
                        bk.add_reader(trl)
                        tsq2 = P.emit("vector", (lambda e, j=j, n=n: e.tensor_mul(
                            hst2[:, j * n:(j + 1) * n],
                            hst2[:, j * n:(j + 1) * n],
                            hst2[:, j * n:(j + 1) * n])),
                            [trl], "VV")
                        sqtoks.append(tsq2)
                    prevh = P.dma("gpsimd", fm_view(h_s, 0, FI, gi * n,
                                                    (gi + 1) * n),
                                  hst2[:].rearrange("p (c t) -> p c t", c=FC),
                                  sqtoks)
                    prevx = prevh
                P.barrier()

        _phase4()
        def _phase5():
            with ExitStack() as ph:
                n = 512
                NB = 4
                ngr = OWN // n
                hb = ph.enter_context(nc.sbuf_tensor("p5h", [128, FC * n], F32))
                pong = ph.enter_context(nc.sbuf_tensor("p5pg", [128, FC * n], F32))
                s2b2 = ph.enter_context(nc.sbuf_tensor("p5s2", [128, NB * n],
                                                       F32))
                n0b = ph.enter_context(nc.sbuf_tensor("p5n0", [128, NB * n],
                                                      F32))
                rbuf = ph.enter_context(nc.sbuf_tensor("p5rb", [128, 2 * n], F32))
                rbuf2 = ph.enter_context(nc.sbuf_tensor("p5rb2", [128, NB * n],
                                                        F32))
                qint = ph.enter_context(nc.sbuf_tensor("p5qi", [128, NB * n],
                                                       I32))
                hqb = ph.enter_context(nc.sbuf_tensor("p5hq", [128, FC * n], BF16))
                sc_r = ph.enter_context(nc.sbuf_tensor("p5sr", [128, 4], F32))
                sc_q = ph.enter_context(nc.sbuf_tensor("p5sq", [128, 4], F32))
                prevx = None
                prevq = None
                for gi in range(ngr):
                    tx = P.dma("sync", hb[:].rearrange("p (c t) -> p c t", c=FC),
                               fm_view(h_s, 0, FI, gi * n, (gi + 1) * n), [prevx])
                    fb, ob, ft = fwht_stages(FC, n, hb, pong, [tx],
                                             gfrac=0.25)
                    utoks = []
                    for cc in range(FC):
                        bk = banks[cc % 4]
                        w0 = bk.wwaits() + ft + [th]
                        tmm = P.emit("tensor", (lambda e, cc=cc, fb=fb, bk=bk, n=n:
                                     e.matmul(bk.ap[:, 0:n], H128[:],
                                              fb[:, cc * n:(cc + 1) * n],
                                              start=True, stop=True)), w0, "TV")
                        bk.set_writer(tmm)
                        te = P.emit("scalar", (lambda e, cc=cc, ob=ob, bk=bk, n=n:
                                    e.copy(ob[:, cc * n:(cc + 1) * n],
                                           bk.ap[:, 0:n])),
                                    bk.rwaits(), "SV")
                        bk.add_reader(te)
                        utoks.append(te)
                    nt = n // 128
                    qtoks, _ = quant_rounds(2, FC, n, ob, [utoks[-1]], hqb,
                                            f2tm[:, nt * gi:nt * gi + nt],
                                            m2tm[:, nt * gi:nt * gi + nt],
                                            qint, n0b, rbuf, s2b2, sc_r, sc_q,
                                            list(prevq or []), rbuf2)
                    sp_toks = []
                    for bi, cb in enumerate(range(0, FC, 4)):
                        tsp = P.dma("gpsimd",
                                    fm_view(hq_s, cb * 128, (cb + 4) * 128,
                                            gi * n, (gi + 1) * n),
                                    hqb[:, cb * n:(cb + 4) * n].rearrange(
                                        "p (c t) -> p c t", c=4),
                                    [qtoks[bi]])
                        sp_toks.append(tsp)
                    prevq = sp_toks
                    prevx = sp_toks[-1]
                P.barrier()

        _phase5()
        def _phase6():
            with ExitStack() as ph:
                wqd = ph.enter_context(nc.sbuf_tensor("p6wqd", [128, FC * D],
                                                      BF16))
                with ExitStack() as pp:
                    weight_prep(pp, wdn, D, FI, wqd, 1)
                    P.barrier()
                n = 512
                ngr = OWN // n
                nt = n // 128
                hqb = ph.enter_context(nc.sbuf_tensor("p6hq", [128, FC * n],
                                                      BF16))
                fdr = ph.enter_context(nc.sbuf_tensor("p6fd", [128, nt], F32))
                fdrep = ph.enter_context(nc.sbuf_tensor("p6fe", [128, n],
                                                        F32))
                frow = ph.enter_context(nc.sbuf_tensor("p6fr", [1, n], F32))
                o2b = ph.enter_context(nc.sbuf_tensor("p6o2", [128, 2 * n],
                                                       F32))
                gst = ph.enter_context(nc.sbuf_tensor("p6g", [128, C * n], F32))
                prevx = None
                prevg = None
                for gi in range(ngr):
                    tx = P.dma("sync", hqb[:].rearrange("p (c t) -> p c t", c=FC),
                               fm_view(hq_s, 0, FI, gi * n, (gi + 1) * n), [prevx])
                    tfd = P.emit("vector", (lambda e, gi=gi, nt=nt:
                        e.tensor_scalar(
                        fdr[:, 0:nt], m2tm[:, nt * gi:nt * gi + nt],
                        WCOL[:, 1:2],
                        None, ALU.mult)), [prevg], "VV")
                    btr = banks[5]
                    wtr = btr.wwaits() + [tfd, tif]
                    tt = None
                    for k4 in range(nt):
                        tt = P.emit("tensor", (lambda e, k4=k4: e.matmul(
                            btr.ap[0:1, k4 * 128:(k4 + 1) * 128],
                            fdr[:, k4:k4 + 1], IDF[:], start=True, stop=True)),
                            wtr if k4 == 0 else (),
                            "TV" if k4 == nt - 1 else None)
                    btr.set_writer(tt)
                    trw = P.emit("vector", (lambda e: e.tensor_copy(
                        frow[:], btr.ap[0:1, 0:n])), btr.rwaits(), "VV")
                    btr.add_reader(trw)
                    brep = banks[6]
                    wrep = brep.wwaits() + [trw, tonr]
                    trep = P.emit("tensor", (lambda e: e.matmul(
                        brep.ap[:, 0:n], ONR[:], frow[:], start=True, stop=True)),
                        wrep, "TV")
                    brep.set_writer(trep)
                    tfde = P.emit("vector", (lambda e: e.tensor_copy(
                        fdrep[:, 0:n], brep.ap[:, 0:n])),
                        brep.rwaits() + [prevg], "VV")
                    brep.add_reader(tfde)
                    gtoks = []
                    prev_sil = [None, None]
                    for j in range(C):
                        bk = banks[(j % 2)]
                        w0 = bk.wwaits() + [tx]
                        tmm = None
                        for c in range(FC):
                            tmm = P.emit("tensor", (lambda e, j=j, c=c, bk=bk, n=n:
                                         e.matmul(bk.ap[:, 0:n],
                                                  wqd[:, c * D + j * 128:
                                                     c * D + (j + 1) * 128],
                                                  hqb[:, c * n:(c + 1) * n],
                                                  start=(c == 0),
                                                  stop=(c == FC - 1))),
                                         w0 if c == 0 else (),
                                         "TV" if c == FC - 1 else None)
                        bk.set_writer(tmm)
                        osl = slice((j % 2) * n, (j % 2) * n + n)
                        tsc2 = P.emit("vector", (lambda e, bk=bk, osl=osl, n=n:
                                      e.tensor_mul(o2b[:, osl], bk.ap[:, 0:n],
                                                   fdrep[:, 0:n])),
                                      bk.rwaits() + [tfde, prev_sil[j % 2]],
                                      "VV")
                        bk.add_reader(tsc2)
                        tsil = P.emit("scalar", (lambda e, j=j, osl=osl, n=n:
                                      e.activation(gst[:, j * n:(j + 1) * n],
                                                   o2b[:, osl], ACT.Silu)),
                            [tsc2, prevg], "SV")
                        prev_sil[j % 2] = tsil
                        gtoks.append(tsil)
                    prevg = P.dma("gpsimd", fm_view(g_s, 0, D, gi * n,
                                                    (gi + 1) * n),
                                  gst[:].rearrange("p (c t) -> p c t", c=C),
                                  gtoks)
                    prevx = prevg
                P.barrier()

        _phase6()
        def _phase7():
            with ExitStack() as ph:
                n = 512
                ngr = OWN // n
                wob = ph.enter_context(nc.sbuf_tensor("p7wo", [128, C * D], BF16))
                otb = ph.enter_context(nc.sbuf_tensor("p7ot", [128, C * n], BF16))
                xnb = ph.enter_context(nc.sbuf_tensor("p7xn", [128, C * n], F32))
                gb = ph.enter_context(nc.sbuf_tensor("p7g", [128, C * n], F32))
                xb = ph.enter_context(nc.sbuf_tensor("p7x", [128, C * n], F32))
                two = P.dma("gpsimd", wob[:].rearrange("p (c j) -> p c j", c=C),
                            fm_view(wo_d, 0, D, 0, D))
                prevo = prevn = prevgl = prevxl = None
                for gi in range(ngr):
                    tot = P.dma("sync", otb[:].rearrange("p (c t) -> p c t", c=C),
                                fm_view(ot_s, 0, D, gi * n, (gi + 1) * n), [prevo])
                    txn = P.dma("sync", xnb[:].rearrange("p (c t) -> p c t", c=C),
                                fm_view(xn_s, 0, D, gi * n, (gi + 1) * n), [prevn])
                    tgl = P.dma("sync", gb[:].rearrange("p (c t) -> p c t", c=C),
                                fm_view(g_s, 0, D, gi * n, (gi + 1) * n), [prevgl])
                    txl = P.dma("sync", xb[:].rearrange("p (c t) -> p c t", c=C),
                                fm_view(xo, 0, D, gi * n, (gi + 1) * n), [prevxl])
                    otoks = []
                    mmtoks = []
                    for j in range(C):
                        bk = banks[(j % 2)]
                        w0 = bk.wwaits() + [tot, two]
                        tmm = None
                        for c in range(C):
                            tmm = P.emit("tensor", (lambda e, j=j, c=c, bk=bk, n=n:
                                         e.matmul(bk.ap[:, 0:n],
                                                  wob[:, c * D + j * 128:
                                                     c * D + (j + 1) * 128],
                                                  otb[:, c * n:(c + 1) * n],
                                                  start=(c == 0),
                                                  stop=(c == C - 1))),
                                         w0 if c == 0 else (),
                                         "TV" if c == C - 1 else None)
                        bk.set_writer(tmm)
                        mmtoks.append(tmm)
                        sl = slice(j * n, (j + 1) * n)
                        ts_ = P.emit("vector", (lambda e, sl=sl, bk=bk, n=n:
                                     e.tensor_add(xnb[:, sl], bk.ap[:, 0:n],
                                                  xnb[:, sl])),
                                     bk.rwaits() + [txn], "VV")
                        bk.add_reader(ts_)
                        tsg = P.emit("vector", (lambda e, sl=sl: e.tensor_mul(
                            gb[:, sl], xnb[:, sl], gb[:, sl])), [ts_, tgl], "VV")
                        tout = P.emit("vector", (lambda e, sl=sl: e.tensor_add(
                            xb[:, sl], xb[:, sl], gb[:, sl])), [tsg, txl], "VV")
                        otoks.append(tout)
                    tso = P.dma_out("sync", fm_view(out_d, 0, D, gi * n,
                                                    (gi + 1) * n),
                                    xb[:].rearrange("p (c t) -> p c t", c=C),
                                    otoks)
                    prevo = prevn = prevgl = prevxl = tso
                P.barrier()

        _phase7()
        if dbg:
            for name in dbg:
                src = scr[name]
                dst = dbg_outs[name]
                nr = src.shape[0]
                for rr in range(0, nr, 512):
                    P.dma_out("gpsimd", dst[rr:rr + 512, :],
                              src[rr:rr + 512, :])
            P.barrier()

        P.finalize(blk)
    return nc


_CACHE = {}


def _host_inputs(L0_attn_x, L0_attn_qkv, L0_attn_o_proj, L0_attn_gamma,
                 L0_attn_beta, L0_ffn_w_up, L0_ffn_w_down):
    x = np.asarray(L0_attn_x, dtype=np.float32)
    qkv = np.ascontiguousarray(np.asarray(L0_attn_qkv, dtype=np.float32))
    wo = np.ascontiguousarray(np.asarray(L0_attn_o_proj, dtype=np.float32))
    wup = np.ascontiguousarray(np.asarray(L0_ffn_w_up, dtype=np.float32))
    wdn = np.ascontiguousarray(np.asarray(L0_ffn_w_down, dtype=np.float32))
    gamC = np.ascontiguousarray(
        np.asarray(L0_attn_gamma, np.float32).reshape(C, 128).T)
    betC = np.ascontiguousarray(
        np.asarray(L0_attn_beta, np.float32).reshape(C, 128).T)
    ii = np.arange(128)
    pc = np.zeros((128, 128), dtype=np.int64)
    a = ii[:, None] & ii[None, :]
    for b in range(7):
        pc += (a >> b) & 1
    h128 = (1.0 - 2.0 * (pc & 1)).astype(np.float32)
    identf = np.eye(128, dtype=np.float32)
    identb = np.eye(128).astype(ml_dtypes.bfloat16)
    consts = np.zeros((128, 8), dtype=np.float32)
    consts[:, 0] = -0.5
    consts[:, 1] = 0.5
    consts[:, 2] = EPS
    consts[:, 3] = 1.0
    ones_row = np.ones((1, 128), dtype=np.float32)
    shared = dict(qkv=qkv, o_proj=wo, w_up=wup, w_down=wdn, gammaC=gamC,
                  betaC=betC, h128f=h128, identf=identf, identb=identb,
                  consts=consts, ones_row=ones_row)
    in_maps = []
    metas = []
    for core in range(8):
        b = core // 2
        role = core % 2
        tiles = TILES_ROLE[role]
        rows = np.concatenate([np.arange(g * 128, (g + 1) * 128)
                               for g in tiles])
        x_seq = np.ascontiguousarray(x[b].T)
        x_own = np.ascontiguousarray(x[b][rows].T)
        masks = np.zeros((16, 128, 512), dtype=np.float32)
        for i, g in enumerate(tiles):
            cgt = CNT[g]
            k0 = (cgt - 1) * 512
            q_glob = g * 128 + np.arange(128)[:, None]
            k_glob = k0 + np.arange(512)[None, :]
            masks[i] = np.where(k_glob <= q_glob, 0.0, NEG)
        m = dict(shared)
        m["x_seq"] = x_seq
        m["x_own"] = x_own
        m["masks"] = np.ascontiguousarray(masks.reshape(16 * 128, 512))
        in_maps.append(m)
        metas.append((b, rows))
    return in_maps, metas


def kernel(**inputs):
    if "nc" not in _CACHE:
        _CACHE["nc"] = build_program()
    nc = _CACHE["nc"]
    in_maps, metas = _host_inputs(**inputs)
    res = run_bass_kernel_spmd(nc, in_maps, core_ids=list(range(8)))
    out = np.empty((4, SEQ, D), dtype=np.float32)
    for core in range(8):
        b, rows = metas[core]
        out[b][rows] = np.ascontiguousarray(res.results[core]["outT"]).T
    return out


# version 24
# speedup vs baseline: 1.0126x; 1.0126x over previous
import math
import numpy as np
import ml_dtypes
import concourse.bass as bass
import concourse.mybir as mybir
from concourse.bass_utils import run_bass_kernel_spmd
from contextlib import ExitStack

F32 = mybir.dt.float32
BF16 = mybir.dt.bfloat16
I32 = mybir.dt.int32
ALU = mybir.AluOpType
ACT = mybir.ActivationFunctionType
AX = mybir.AxisListType

D = 2048
C = 16
SEQ = 4096
OWN = 2048
FI = 4096
FC = 32
EPS = 1e-5
C1 = 1.0 / math.sqrt(2048.0)
C2 = 1.0 / 64.0
CS = 1.0 / math.sqrt(2048.0)
QB = 127.0
NEG = -1.0e12

TILES_ROLE = [
    [0, 1, 4, 5, 8, 9, 12, 13, 16, 17, 20, 21, 24, 25, 28, 29],
    [2, 3, 6, 7, 10, 11, 14, 15, 18, 19, 22, 23, 26, 27, 30, 31],
]
CNT = [(g + 4) // 4 for g in range(32)]

ENGS = ("sync", "gpsimd", "vector", "scalar", "tensor")
OWN_SEM = {"sync": "LD", "gpsimd": "GD", "vector": "VV", "scalar": "SV",
           "tensor": "TV"}
SKIP_SELF = set()


class Prog:
    NCHAIN = 6

    def __init__(self, nc, stack):
        self.nc = nc
        self.items = {e: [] for e in ENGS}
        self.sems = {}
        self.cnt = {}
        names = ["TV", "VV", "SV", "GC", "BS", "BG", "BV", "BC", "BT"]
        for q in ("LD", "GD"):
            for r in range(self.NCHAIN):
                names.append(f"{q}{r}")
        for name in names:
            self.sems[name] = stack.enter_context(nc.semaphore(name))
            self.cnt[name] = 0
        self.qidx = {"sync": 0, "gpsimd": 0}
        self.qidx_b = {"sync": 0, "gpsimd": 0}

    def emit(self, eng, fn, waits=(), inc=None, amt=1):
        tok = None
        incs = []
        if inc is not None:
            self.cnt[inc] += amt
            tok = (inc, self.cnt[inc])
            incs.append((inc, amt))
        w = tuple(t for t in waits if t is not None)
        self.items[eng].append([fn, w, incs])
        return tok

    def dma(self, eng, out, in_, waits=(), lane="a"):
        q = "LD" if eng == "sync" else "GD"
        nch_a = 4
        if lane == "a":
            idx = self.qidx[eng]
            self.qidx[eng] = idx + 1
            r = idx % nch_a
            k = idx // nch_a
        else:
            idx = self.qidx_b[eng]
            self.qidx_b[eng] = idx + 1
            r = nch_a + idx % (self.NCHAIN - nch_a)
            k = idx // (self.NCHAIN - nch_a)
        sem = f"{q}{r}"
        w = list(waits)
        if k > 0:
            w.append((sem, 16 * k))
        return self.emit(eng, (lambda e, o=out, i=in_: e.dma_start(out=o, in_=i)),
                         w, sem, 16)

    def dma_out(self, eng, out, in_, waits=()):
        return self.dma(eng, out, in_, waits)

    def barrier(self):
        bar = {"sync": "BS", "gpsimd": "BG", "vector": "BV", "scalar": "BC",
               "tensor": "BT"}
        toks = []
        for eng in ENGS:
            b = bar[eng]
            if eng in ("sync", "gpsimd"):
                self.cnt[b] += 1
                toks.append((b, self.cnt[b]))
                q = "LD" if eng == "sync" else "GD"
                w = [(f"{q}{r}", self.cnt[f"{q}{r}"])
                     for r in range(self.NCHAIN)]
                self.items[eng].append([(lambda e: e.nop()), tuple(w),
                                        [(b, 1)]])
            else:
                toks.append(self.dummy[eng](b))
        for eng in ENGS:
            self.items[eng].append([None, tuple(toks), []])

    def finalize(self, block):
        for eng in ENGS:
            items = self.items[eng]

            def body(e, items=items, eng=eng):
                waited = {}
                for fn, waits, incs in items:
                    for (s, v) in waits:
                        if (eng, s) in SKIP_SELF:
                            continue
                        if waited.get(s, -1) >= v:
                            continue
                        e.wait_ge(self.sems[s], v)
                        waited[s] = v
                    if fn is None:
                        continue
                    inst = fn(e)
                    for (s, amt) in incs:
                        inst.then_inc(self.sems[s], amt)
            getattr(block, eng)(body)


class Bank:
    def __init__(self, ap):
        self.ap = ap
        self.writer = None
        self.readers = []

    def wwaits(self):
        w = [t for t in self.readers if t is not None]
        if self.writer is not None:
            w.append(self.writer)
        self.readers = []
        return w

    def set_writer(self, tok):
        self.writer = tok

    def rwaits(self):
        return [self.writer] if self.writer else []

    def add_reader(self, tok):
        if tok is not None:
            self.readers.append(tok)


def fm_view(dram_ap, r0, r1, c0, c1):
    v = dram_ap[r0:r1, c0:c1].rearrange("(c p) t -> c p t", p=128)
    return v.transpose([1, 0, 2])


def build_program(dbg=()):
    nc = bass.Bass("TRN2", target_bir_lowering=False,
                   detect_race_conditions=False)
    xs = nc.dram_tensor("x_seq", [D, SEQ], F32, kind="ExternalInput").ap()
    xo = nc.dram_tensor("x_own", [D, OWN], F32, kind="ExternalInput").ap()
    qkv = nc.dram_tensor("qkv", [D, 3 * D], F32, kind="ExternalInput").ap()
    wo_d = nc.dram_tensor("o_proj", [D, D], F32, kind="ExternalInput").ap()
    wup = nc.dram_tensor("w_up", [FI, D], F32, kind="ExternalInput").ap()
    wdn = nc.dram_tensor("w_down", [D, FI], F32, kind="ExternalInput").ap()
    gam = nc.dram_tensor("gammaC", [128, C], F32, kind="ExternalInput").ap()
    bet = nc.dram_tensor("betaC", [128, C], F32, kind="ExternalInput").ap()
    msk = nc.dram_tensor("masks", [16 * 128, 512], F32, kind="ExternalInput").ap()
    h128f = nc.dram_tensor("h128f", [128, 128], F32, kind="ExternalInput").ap()
    idf_d = nc.dram_tensor("identf", [128, 128], F32, kind="ExternalInput").ap()
    idb_d = nc.dram_tensor("identb", [128, 128], BF16, kind="ExternalInput").ap()
    cst_d = nc.dram_tensor("consts", [128, 8], F32, kind="ExternalInput").ap()
    onr_d = nc.dram_tensor("ones_row", [1, 128], F32, kind="ExternalInput").ap()
    out_d = nc.dram_tensor("outT", [D, OWN], F32, kind="ExternalOutput").ap()

    xn_s = nc.dram_tensor("s_xn", [D, OWN], F32).ap()
    xn2_s = nc.dram_tensor("s_xn2", [D, SEQ], BF16).ap()
    qt_s = nc.dram_tensor("s_qt", [D, OWN], BF16).ap()
    kt_s = nc.dram_tensor("s_kt", [D, SEQ], BF16).ap()
    v_s = nc.dram_tensor("s_v", [SEQ, D], BF16).ap()
    ot_s = nc.dram_tensor("s_ot", [D, OWN], BF16).ap()
    h_s = nc.dram_tensor("s_h", [FI, OWN], F32).ap()
    hq_s = nc.dram_tensor("s_hq", [FI, OWN], BF16).ap()
    g_s = nc.dram_tensor("s_g", [D, OWN], F32).ap()
    acc_s = nc.dram_tensor("s_acc", [128, 256], F32).ap()
    scr = {"s_xn": xn_s, "s_xn2": xn2_s, "s_qt": qt_s, "s_kt": kt_s,
           "s_v": v_s, "s_ot": ot_s, "s_h": h_s, "s_hq": hq_s, "s_g": g_s}
    dbg_outs = {}
    for name in dbg:
        t = scr[name]
        dbg_outs[name] = nc.dram_tensor(
            "dbg_" + name, list(t.shape), F32, kind="ExternalOutput").ap()

    with ExitStack() as st:
        en = st.enter_context
        P = Prog(nc, st)
        gamC = en(nc.sbuf_tensor("gamC", [128, C], F32))
        betC = en(nc.sbuf_tensor("betC", [128, C], F32))
        H128 = en(nc.sbuf_tensor("H128", [128, 128], F32))
        IDF = en(nc.sbuf_tensor("IDF", [128, 128], F32))
        IDB = en(nc.sbuf_tensor("IDB", [128, 128], BF16))
        CSTS = en(nc.sbuf_tensor("CSTS", [128, 8], F32))
        ONR = en(nc.sbuf_tensor("ONR", [1, 128], F32))
        ONC = en(nc.sbuf_tensor("ONC", [128, 1], F32))
        ONCB = en(nc.sbuf_tensor("ONCB", [128, 1], BF16))
        f2tm = en(nc.sbuf_tensor("f2tm", [128, 16], F32))
        m2tm = en(nc.sbuf_tensor("m2tm", [128, 16], F32))
        wrow = en(nc.sbuf_tensor("wrow", [1, 4], F32))
        WCOL = en(nc.sbuf_tensor("WCOL", [128, 2], F32))
        nc.const_aps.register(F32, -0.5, CSTS[:, 0:1])
        nc.const_aps.register(F32, 0.5, CSTS[:, 1:2])
        nc.const_aps.register(F32, EPS, CSTS[:, 2:3])
        banks = [Bank(en(nc.psum_tensor(f"pb{i}", [128, 512], F32))[:])
                 for i in range(8)]
        DUMS = en(nc.sbuf_tensor("DUMS", [1, 1], F32))

        def _dummy_vector(b):
            return P.emit("vector", lambda e: e.engine_nop(), (), b)

        def _dummy_scalar(b):
            return P.emit("scalar", (lambda e: e.activation(
                DUMS[:], CSTS[0:1, 3:4], ACT.Identity)), (), b)

        def _dummy_tensor(b):
            bk = banks[7]
            w = bk.wwaits()
            tok = P.emit("tensor", (lambda e: e.matmul(
                bk.ap[0:1, 0:1], ONC[0:1, 0:1], ONC[0:1, 0:1],
                start=True, stop=True)), w, b)
            bk.set_writer(tok)
            return tok
        P.dummy = {"vector": _dummy_vector, "scalar": _dummy_scalar,
                   "tensor": _dummy_tensor}
        blk = en(nc.Block())

        tg = P.dma("sync", gamC[:], gam[:, :])
        tbq = P.dma("sync", betC[:], bet[:, :])
        th = P.dma("sync", H128[:], h128f[:, :])
        tif = P.dma("sync", IDF[:], idf_d[:, :])
        tib = P.dma("sync", IDB[:], idb_d[:, :])
        tc = P.dma("sync", CSTS[:], cst_d[:, :])
        tonr = P.dma("sync", ONR[:], onr_d[:, :])
        tonc = P.emit("vector", lambda e: e.tensor_copy(ONC[:], CSTS[:, 3:4]),
                      [tc], "VV")
        toncb = P.emit("vector", lambda e: e.tensor_copy(ONCB[:], CSTS[:, 3:4]),
                       [tc], "VV")

        def ln_group(n, xbuf, x_tok, xb16, xb16_tok, sqbuf, xn2buf, xnbuf,
                     rows, want_xn, gd):
            b_sx, b_sq, b_mu, b_r12, b_r1 = (banks[0], banks[1], banks[2],
                                             banks[3], banks[4])
            tsq = P.emit("scalar", (lambda e, xbuf=xbuf, sqbuf=sqbuf, n=n:
                         e.activation(sqbuf[:, 0:C * n], xbuf[:, 0:C * n],
                                      ACT.Square)),
                         [x_tok] + gd.get("sq", []), "SV")

            def mk_sums(bank, src, tok):
                w = bank.wwaits() + [tok, toncb]
                t = None
                for c in range(C):
                    last = (c == C - 1)
                    t = P.emit("tensor", (lambda e, c=c, src=src, bank=bank,
                               n=n, last=last:
                               e.matmul(bank.ap[0:1, 0:n], ONCB[:],
                                        src[:, c * n:(c + 1) * n],
                                        start=(c == 0), stop=last)),
                               w if c == 0 else (), "TV" if last else None)
                bank.set_writer(t)
                return t
            t1 = mk_sums(b_sx, xb16, xb16_tok)
            t2 = mk_sums(b_sq, sqbuf, tsq)
            r = rows
            tmu = P.emit("vector", (lambda e, n=n: e.tensor_scalar_mul(
                r["mu"][:, 0:n], b_sx.ap[0:1, 0:n], 1.0 / D)),
                [t1] + gd.get("rows", []), "VV")
            b_sx.add_reader(tmu)
            tm2 = P.emit("vector", (lambda e, n=n: e.tensor_mul(
                r["tmp"][:, 0:n], r["mu"][:, 0:n], r["mu"][:, 0:n])), [tmu], "VV")
            tvar = P.emit("vector", (lambda e, n=n: e.scalar_tensor_tensor(
                r["var"][:, 0:n], b_sq.ap[0:1, 0:n], 1.0 / D, r["tmp"][:, 0:n],
                ALU.mult, ALU.subtract)), [t2, tm2], "VV")
            b_sq.add_reader(tvar)
            tl1 = P.emit("scalar", (lambda e, n=n: e.activation(
                r["tmp"][:, 0:n], r["var"][:, 0:n], ACT.Ln, bias=EPS)),
                [tvar, tc], "SV")
            tr1 = P.emit("scalar", (lambda e, n=n: e.activation(
                r["r1"][:, 0:n], r["tmp"][:, 0:n], ACT.Exp, scale=-0.5)),
                [tl1], "SV")
            ta = P.emit("vector", (lambda e, n=n: e.tensor_mul(
                r["tmp"][:, 0:n], r["var"][:, 0:n], r["r1"][:, 0:n])), [tr1], "VV")
            tb2 = P.emit("vector", (lambda e, n=n: e.tensor_mul(
                r["tmp"][:, 0:n], r["tmp"][:, 0:n], r["r1"][:, 0:n])), [ta], "VV")
            tl2 = P.emit("scalar", (lambda e, n=n: e.activation(
                r["s2r"][:, 0:n], r["tmp"][:, 0:n], ACT.Ln, bias=EPS)), [tb2], "SV")
            tr2 = P.emit("scalar", (lambda e, n=n: e.activation(
                r["s2r"][:, 0:n], r["s2r"][:, 0:n], ACT.Exp, scale=-0.5)),
                [tl2], "SV")
            tr12 = P.emit("vector", (lambda e, n=n: e.tensor_mul(
                r["r12"][:, 0:n], r["r1"][:, 0:n], r["s2r"][:, 0:n])), [tr2], "VV")

            def bcast(bank, row, tok):
                w = bank.wwaits() + [tok, tonr]
                t = P.emit("tensor", (lambda e, bank=bank, row=row, n=n:
                           e.matmul(bank.ap[:, 0:n], ONR[:], row[:, 0:n],
                                    start=True, stop=True)), w, "TV")
                bank.set_writer(t)
                return t
            tmub = bcast(b_mu, r["mu"], tmu)
            tr12b = bcast(b_r12, r["r12"], tr12)
            tr1b = bcast(b_r1, r["r1"], tr1) if want_xn else None
            tmus = P.emit("vector", (lambda e, n=n: e.tensor_copy(
                r["mub"][:, 0:n], b_mu.ap[:, 0:n])),
                [tmub] + gd.get("rowsb", []), "VV")
            b_mu.add_reader(tmus)
            tr12s = P.emit("vector", (lambda e, n=n: e.tensor_copy(
                r["r12b"][:, 0:n], b_r12.ap[:, 0:n])), [tr12b], "VV")
            b_r12.add_reader(tr12s)
            if want_xn:
                tr1s = P.emit("vector", (lambda e, n=n: e.tensor_copy(
                    r["r1b"][:, 0:n], b_r1.ap[:, 0:n])), [tr1b], "VV")
                b_r1.add_reader(tr1s)
            toks_xn2, toks_xn = [], []
            for c in range(C):
                sl = slice(c * n, (c + 1) * n)
                tt1 = P.emit("vector", (lambda e, sl=sl, xbuf=xbuf, n=n:
                             e.tensor_sub(xbuf[:, sl], xbuf[:, sl],
                                          r["mub"][:, 0:n])),
                             [tmus, tsq, t1], "VV")
                if want_xn:
                    txn = P.emit("vector", (lambda e, sl=sl, xbuf=xbuf,
                                 xnbuf=xnbuf, n=n: e.tensor_mul(
                                     xnbuf[:, sl], xbuf[:, sl],
                                     r["r1b"][:, 0:n])),
                                 [tt1, tr1s, t2] + gd.get("xn", []), "VV")
                    toks_xn.append(txn)
                tt2 = P.emit("vector", (lambda e, sl=sl, xbuf=xbuf, n=n:
                             e.tensor_mul(xbuf[:, sl], xbuf[:, sl],
                                          r["r12b"][:, 0:n])),
                             [tt1, tr12s], "VV")
                t2f = P.emit("scalar", (lambda e, sl=sl, c=c, xbuf=xbuf,
                             xn2buf=xn2buf: e.activation(
                                 xn2buf[:, sl], xbuf[:, sl], ACT.Identity,
                                 bias=betC[:, c:c + 1], scale=gamC[:, c:c + 1])),
                             [tt2, tg, tbq, t1] + gd.get("x2", []), "SV")
                toks_xn2.append(t2f)
            new_guard = {"rows": [tmub, tr12b] + ([tr1b] if tr1b else []),
                         "rowsb": toks_xn2, "sq": [t2]}
            return toks_xn2, toks_xn, new_guard

        WPHW = 512
        WPT = []
        for (_wd, _hw) in ((wup, WPHW), (wdn, WPHW)):
            _per = _wd.shape[1] // _hw
            for _ti in range((_wd.shape[0] // 128) * _per):
                WPT.append((_wd, _hw, _per, _ti))

        def wp_make(ph, lo, hi, nbuf=8):
            hw = WPHW
            wt2 = [ph.enter_context(nc.sbuf_tensor(f"wm{lo}_{i}",
                   [128, hw], F32)) for i in range(nbuf)]
            abf = ph.enter_context(nc.sbuf_tensor(f"wm{lo}a", [128, hw], F32))
            accv = ph.enter_context(nc.sbuf_tensor(f"wm{lo}c",
                                                   [128, hi - lo], F32))
            st = {"pend": [], "pos": lo, "tprev": None, "tread": {}, "cnt": 0}

            def step(k):
                for (tw, wt, gc, cnt) in st["pend"]:
                    ta = P.emit("scalar", (lambda e, wt=wt, gc=gc:
                                e.activation(abf[:], wt[:], ACT.Abs,
                                             accum_out=accv[:, gc:gc + 1])),
                                [tw, st["tprev"]], "SV")
                    st["tprev"] = ta
                    st["tread"][cnt] = ta
                st["pend"] = []
                for _ in range(min(k, hi - st["pos"])):
                    gidx = st["pos"]
                    wdram, hw_, per, ti = WPT[gidx]
                    cnt = st["cnt"]
                    wt = wt2[cnt % nbuf]
                    rt, hf = ti // per, ti % per
                    eng = "sync" if cnt % 2 == 0 else "gpsimd"
                    tw = P.dma(eng, wt[:],
                               wdram[rt * 128:(rt + 1) * 128,
                                     hf * hw_:(hf + 1) * hw_],
                               [st["tread"].get(cnt - nbuf)], lane="b")
                    st["pend"].append((tw, wt, gidx - lo, cnt))
                    st["pos"] += 1
                    st["cnt"] = cnt + 1

            def flush():
                while st["pend"] or st["pos"] < hi:
                    step(nbuf)
                P.dma("gpsimd", acc_s[:, lo:hi], accv[:],
                      [st["tprev"]], lane="b")
            return step, flush

        def _phase0():
            with ExitStack() as ph:
                n = 512
                ngr = OWN // n
                xb = [ph.enter_context(nc.sbuf_tensor("p1x0", [128, C * n],
                                                      F32))] * 2
                xnb = ph.enter_context(nc.sbuf_tensor("p1xn", [128, C * n], F32))
                x2b2 = [ph.enter_context(nc.sbuf_tensor(f"p1x2{i}",
                        [128, C * n], BF16)) for i in range(2)]
                wq = ph.enter_context(nc.sbuf_tensor("p1wq", [128, C * D], BF16))
                qst2 = [ph.enter_context(nc.sbuf_tensor("p1qs",
                        [128, C * n], BF16))] * 2
                rows = {k: ph.enter_context(nc.sbuf_tensor(f"p1r{k}", [1, n], F32))
                        for k in ("mu", "var", "r1", "r12", "tmp", "s2r")}
                for k in ("mub", "r12b", "r1b"):
                    rows[k] = ph.enter_context(
                        nc.sbuf_tensor(f"p1r{k}", [128, n], F32))
                twq = P.dma("gpsimd", wq[:].rearrange("p (c j) -> p c j", c=C),
                            fm_view(qkv, 0, D, 0, D))
                wp1s, wp1f = wp_make(ph, 0, 64, nbuf=4)
                prev = [None, None]
                guard = {}
                guards = [{}, {}]
                xguard = []
                for gi in range(ngr):
                    buf = xb[gi % 2]
                    x2b = x2b2[gi % 2]
                    qst = qst2[gi % 2]
                    guard = guards[gi % 2]
                    oth = guards[(gi + 1) % 2]
                    for key in ("rows", "rowsb", "xn", "sq"):
                        guard[key] = oth.get(key, [])
                    tx = P.dma("sync", buf[:].rearrange("p (c t) -> p c t", c=C),
                               fm_view(xo, 0, D, gi * n, (gi + 1) * n),
                               [prev[gi % 2]] + guard.get("x2", []) + xguard)
                    txc = P.dma("gpsimd",
                                x2b[:].rearrange("p (c t) -> p c t", c=C),
                                fm_view(xo, 0, D, gi * n, (gi + 1) * n),
                                guard.get("x2", []))
                    t2, txn, ng = ln_group(n, buf, tx, x2b, txc, qst, x2b,
                                           xnb, rows, True, guard)
                    guard.update(ng)
                    xguard = t2
                    tspill = P.dma("gpsimd",
                                   fm_view(xn_s, 0, D, gi * n, (gi + 1) * n),
                                   xnb[:].rearrange("p (c t) -> p c t", c=C), txn)
                    guard["xn"] = [tspill]
                    evtoks = []
                    for j in range(C):
                        wp1s(1)
                        bk = banks[5 + (j % 2)]
                        w0 = bk.wwaits() + t2 + [twq]
                        tmm = None
                        for c in range(C):
                            tmm = P.emit("tensor", (lambda e, j=j, c=c, bk=bk,
                                         n=n, x2b=x2b:
                                         e.matmul(bk.ap[:, 0:n],
                                                  wq[:, c * D + j * 128:
                                                     c * D + (j + 1) * 128],
                                                  x2b[:, c * n:(c + 1) * n],
                                                  start=(c == 0), stop=(c == C - 1))),
                                         w0 if c == 0 else (),
                                         "TV" if c == C - 1 else None)
                        bk.set_writer(tmm)
                        tev = P.emit("scalar", (lambda e, j=j, bk=bk, n=n,
                            qst=qst: e.copy(
                            qst[:, j * n:(j + 1) * n], bk.ap[:, 0:n])),
                            bk.rwaits() + [t2[0]], "SV")
                        bk.add_reader(tev)
                        evtoks.append(tev)
                    tq = P.dma("gpsimd", fm_view(qt_s, 0, D, gi * n, (gi + 1) * n),
                               qst[:].rearrange("p (c t) -> p c t", c=C), evtoks)
                    prev[gi % 2] = tq
                    guard["x2"] = [tq]
                    guard["sq"] = guard.get("sq", []) + [tq]
                    guards[gi % 2] = guard
                wp1f()
                P.barrier()

        _phase0()
        def _phase1():
            with ExitStack() as ph:
                n = 512
                ngr = SEQ // n
                xb = [ph.enter_context(nc.sbuf_tensor(f"p2x{i}", [128, C * n], F32))
                      for i in range(2)]
                x2b2 = [ph.enter_context(nc.sbuf_tensor(f"p2x2{i}",
                        [128, C * n], BF16)) for i in range(2)]
                wk = ph.enter_context(nc.sbuf_tensor("p2wk", [128, C * D], BF16))
                kst2 = [ph.enter_context(nc.sbuf_tensor("p2ks",
                        [128, C * n], BF16))] * 2
                rows = {k: ph.enter_context(nc.sbuf_tensor(f"p2r{k}", [1, n], F32))
                        for k in ("mu", "var", "r1", "r12", "tmp", "s2r")}
                for k in ("mub", "r12b", "r1b"):
                    rows[k] = ph.enter_context(
                        nc.sbuf_tensor(f"p2r{k}", [128, n], F32))
                twk = P.dma("gpsimd", wk[:].rearrange("p (c j) -> p c j", c=C),
                            fm_view(qkv, 0, D, D, 2 * D))
                wp2s, wp2f = wp_make(ph, 64, 256, nbuf=4)
                prev = [None, None]
                guard = {}
                guards = [{}, {}]
                for gi in range(ngr):
                    buf = xb[gi % 2]
                    x2b = x2b2[gi % 2]
                    kst = kst2[gi % 2]
                    guard = guards[gi % 2]
                    oth = guards[(gi + 1) % 2]
                    for key in ("rows", "rowsb", "xn", "sq"):
                        guard[key] = oth.get(key, [])
                    tx = P.dma("sync", buf[:].rearrange("p (c t) -> p c t", c=C),
                               fm_view(xs, 0, D, gi * n, (gi + 1) * n),
                               [prev[gi % 2]])
                    txc = P.dma("gpsimd",
                                x2b[:].rearrange("p (c t) -> p c t", c=C),
                                fm_view(xs, 0, D, gi * n, (gi + 1) * n),
                                guard.get("x2", []))
                    t2, _, ng = ln_group(n, buf, tx, x2b, txc, kst, x2b,
                                         None, rows, False, guard)
                    guard.update(ng)
                    tx2 = P.dma("gpsimd",
                                fm_view(xn2_s, 0, D, gi * n, (gi + 1) * n),
                                x2b[:].rearrange("p (c t) -> p c t", c=C), t2)
                    evtoks = []
                    for j in range(C):
                        wp2s(2)
                        bk = banks[5 + (j % 2)]
                        w0 = bk.wwaits() + t2 + [twk]
                        tmm = None
                        for c in range(C):
                            tmm = P.emit("tensor", (lambda e, j=j, c=c, bk=bk,
                                         n=n, x2b=x2b:
                                         e.matmul(bk.ap[:, 0:n],
                                                  wk[:, c * D + j * 128:
                                                     c * D + (j + 1) * 128],
                                                  x2b[:, c * n:(c + 1) * n],
                                                  start=(c == 0), stop=(c == C - 1))),
                                         w0 if c == 0 else (),
                                         "TV" if c == C - 1 else None)
                        bk.set_writer(tmm)
                        tev = P.emit("scalar", (lambda e, j=j, bk=bk, n=n,
                            kst=kst: e.copy(
                            kst[:, j * n:(j + 1) * n], bk.ap[:, 0:n])),
                            bk.rwaits() + [t2[0]], "SV")
                        bk.add_reader(tev)
                        evtoks.append(tev)
                    tk = P.dma("gpsimd", fm_view(kt_s, 0, D, gi * n, (gi + 1) * n),
                               kst[:].rearrange("p (c t) -> p c t", c=C), evtoks)
                    prev[gi % 2] = tk
                    guard["x2"] = [tk, tx2]
                    guard["sq"] = guard.get("sq", []) + [tk]
                    guards[gi % 2] = guard
                wp2f()
                P.barrier()

        _phase1()
        def _phase2():
            with ExitStack() as ph:
                n = 512
                ngr = SEQ // n
                x2b = [ph.enter_context(nc.sbuf_tensor(f"p2bx{i}", [128, C * n],
                                                       BF16)) for i in range(2)]
                wv = ph.enter_context(nc.sbuf_tensor("p2bwv", [128, C * D], BF16))
                vst = ph.enter_context(nc.sbuf_tensor("p2bvs", [128, 4 * D], BF16))
                twv = P.dma("gpsimd", wv[:].rearrange("p (c j) -> p c j", c=C),
                            fm_view(qkv, 0, D, 2 * D, 3 * D))
                prev = [None, None]
                vspill = None
                for gi in range(ngr):
                    buf = x2b[gi % 2]
                    tx = P.dma("sync", buf[:].rearrange("p (c t) -> p c t", c=C),
                               fm_view(xn2_s, 0, D, gi * n, (gi + 1) * n),
                               [prev[gi % 2]])
                    evtoks = []
                    rdtoks = []
                    for ts in range(4):
                        for vn in range(4):
                            bk = banks[(ts * 4 + vn) % 4]
                            w0 = bk.wwaits() + [tx, twv]
                            tmm = None
                            for c in range(C):
                                tmm = P.emit("tensor",
                                             (lambda e, ts=ts, vn=vn, c=c, bk=bk,
                                              buf=buf, n=n:
                                              e.matmul(bk.ap[:, 0:512],
                                                       buf[:, c * n + ts * 128:
                                                          c * n + (ts + 1) * 128],
                                                       wv[:, c * D + vn * 512:
                                                          c * D + (vn + 1) * 512],
                                                       start=(c == 0),
                                                       stop=(c == C - 1))),
                                             w0 if c == 0 else (),
                                             "TV" if c == C - 1 else None)
                            bk.set_writer(tmm)
                            rdtoks.append(tmm)
                            tev = P.emit("scalar", (lambda e, ts=ts, vn=vn, bk=bk:
                                         e.copy(vst[:, ts * D + vn * 512:
                                                    ts * D + (vn + 1) * 512],
                                                bk.ap[:, 0:512])),
                                         bk.rwaits() + ([vspill] if vspill else []),
                                         "SV")
                            bk.add_reader(tev)
                            evtoks.append(tev)
                    vspill = P.dma("gpsimd",
                                   v_s[gi * n:(gi + 1) * n, :]
                                   .rearrange("(s p) d -> s p d", p=128)
                                   .transpose([1, 0, 2]),
                                   vst[:].rearrange("p (s d) -> p s d", s=4),
                                   evtoks)
                    prev[gi % 2] = P.emit("vector", lambda e: e.engine_nop(),
                                          rdtoks, "VV")
                P.barrier()

        _phase2()
        def _phase3():
            with ExitStack() as ph:
                qsb = ph.enter_context(nc.sbuf_tensor("p3q", [128, C * 512], BF16))
                ksb = [ph.enter_context(nc.sbuf_tensor(f"p3k{i}", [128, C * 512],
                                                       BF16)) for i in range(2)]
                vsb = [ph.enter_context(nc.sbuf_tensor(f"p3v{i}", [128, 4 * D],
                                                       BF16)) for i in range(2)]
                oacc = ph.enter_context(nc.sbuf_tensor("p3o", [128, 4 * D], F32))
                mrow = [ph.enter_context(nc.sbuf_tensor(f"p3m{i}", [128, 4], F32))
                        for i in range(2)]
                lrow = ph.enter_context(nc.sbuf_tensor("p3l", [128, 4], F32))
                srow = ph.enter_context(nc.sbuf_tensor("p3s", [128, 4], F32))
                drow = ph.enter_context(nc.sbuf_tensor("p3d", [128, 4], F32))
                nbr = ph.enter_context(nc.sbuf_tensor("p3nb", [128, 4], F32))
                rmax = ph.enter_context(nc.sbuf_tensor("p3rm", [128, 4], F32))
                masks = ph.enter_context(nc.sbuf_tensor("p3msk", [128, 16 * 512],
                                                        F32))
                smsk = ph.enter_context(nc.sbuf_tensor("p3sm", [128, 4 * 512],
                                                       F32))
                pb16 = ph.enter_context(nc.sbuf_tensor("p3p", [128, 4 * 512],
                                                       BF16))
                ptb = ph.enter_context(nc.sbuf_tensor("p3pt", [128, 512], BF16))
                ob16 = ph.enter_context(nc.sbuf_tensor("p3ob", [128, D], BF16))
                otst = ph.enter_context(nc.sbuf_tensor("p3ot", [128, C * 512],
                                                       BF16))
                linv = ph.enter_context(nc.sbuf_tensor("p3li", [128, 4], F32))
                tmsk = P.dma("sync",
                             masks[:].rearrange("p (q t) -> p q t", q=16),
                             msk[:, :].rearrange("(q p) t -> q p t", p=128)
                             .transpose([1, 0, 2]))
                b_sc4 = [banks[0], banks[1], banks[2], banks[3]]
                b_pt = banks[4]
                b_av2 = [banks[5], banks[6]]
                b_fin = banks[7]
                for sti in range(4):
                    kmax = CNT[TILES_ROLE[0][sti * 4 + 3]]
                    tq = P.dma("sync", qsb[:].rearrange("p (c t) -> p c t", c=C),
                               fm_view(qt_s, 0, D, sti * 512, (sti + 1) * 512))
                    oacc_done = {qi: [] for qi in range(4)}
                    prev_tex = {}
                    prev_tl = {}
                    prev_tdx = {}
                    prev_smsk = {}
                    prev_ptm = {}
                    kprev = [None, None]
                    vprev = [None, None]
                    for kt in range(kmax):
                        kb = ksb[kt % 2]
                        vb = vsb[kt % 2]
                        tkl = P.dma("sync", kb[:].rearrange("p (c t) -> p c t", c=C),
                                    fm_view(kt_s, 0, D, kt * 512, (kt + 1) * 512),
                                    [kprev[kt % 2]])
                        tvl = P.dma("sync",
                                    vb[:].rearrange("p (s d) -> p s d", s=4),
                                    v_s[kt * 512:(kt + 1) * 512, :]
                                    .rearrange("(s p) d -> s p d", p=128)
                                    .transpose([1, 0, 2]), [vprev[kt % 2]])
                        kread, vread = [], []
                        active = [qi for qi in range(4)
                                  if kt < CNT[TILES_ROLE[0][sti * 4 + qi]]]
                        texs, tdxs = {}, {}
                        for qi in active:
                            g_cnt = CNT[TILES_ROLE[0][sti * 4 + qi]]
                            first = (kt == 0)
                            diag = (kt == g_cnt - 1)
                            bsc = b_sc4[qi]
                            w0 = bsc.wwaits() + [tq, tkl]
                            tmm = None
                            for c in range(C):
                                tmm = P.emit("tensor",
                                             (lambda e, qi=qi, c=c, bsc=bsc, kb=kb:
                                              e.matmul(bsc.ap[:, 0:512],
                                                       qsb[:, c * 512 + qi * 128:
                                                          c * 512 + (qi + 1) * 128],
                                                       kb[:, c * 512:(c + 1) * 512],
                                                       start=(c == 0),
                                                       stop=(c == C - 1))),
                                             w0 if c == 0 else (),
                                             "TV" if c == C - 1 else None)
                            bsc.set_writer(tmm)
                            kread.append(tmm)
                            if diag:
                                qidx = sti * 4 + qi
                                msl = slice(qi * 512, (qi + 1) * 512)
                                tmadd = P.emit("vector",
                                               (lambda e, qidx=qidx, bsc=bsc,
                                                msl=msl:
                                                e.tensor_add(
                                                    smsk[:, msl], bsc.ap[:, 0:512],
                                                    masks[:, qidx * 512:
                                                          (qidx + 1) * 512])),
                                               bsc.rwaits() + [tmsk,
                                               prev_smsk.get(qi)], "VV")
                                bsc.add_reader(tmadd)
                                sc_src = smsk[:, msl]
                                tprep = [tmadd]
                            else:
                                sc_src = bsc.ap[:, 0:512]
                                tprep = bsc.rwaits()
                            trm = P.emit("vector", (lambda e, qi=qi, s=sc_src:
                                         e.reduce_max(rmax[:, qi:qi + 1], s,
                                                      axis=AX.X)), tprep, "VV")
                            if not diag:
                                bsc.add_reader(trm)
                            mo, mn = mrow[kt % 2], mrow[(kt + 1) % 2]
                            tdx = None
                            if first:
                                tmn = P.emit("vector", (lambda e, qi=qi, mn=mn:
                                             e.tensor_copy(mn[:, qi:qi + 1],
                                                           rmax[:, qi:qi + 1])),
                                             [trm], "VV")
                            else:
                                tmn = P.emit("vector", (lambda e, qi=qi, mo=mo, mn=mn:
                                             e.tensor_max(mn[:, qi:qi + 1],
                                                          mo[:, qi:qi + 1],
                                                          rmax[:, qi:qi + 1])),
                                             [trm], "VV")
                                tda = P.emit("vector", (lambda e, qi=qi, mo=mo, mn=mn:
                                             e.tensor_sub(drow[:, qi:qi + 1],
                                                          mo[:, qi:qi + 1],
                                                          mn[:, qi:qi + 1])),
                                             [tmn, prev_tdx.get(qi)], "VV")
                                tdx = P.emit("scalar", (lambda e, qi=qi:
                                             e.activation(drow[:, qi:qi + 1],
                                                          drow[:, qi:qi + 1],
                                                          ACT.Exp, scale=CS)),
                                             [tda], "SV")
                            tnb = P.emit("vector", (lambda e, qi=qi, mn=mn:
                                         e.tensor_scalar_mul(nbr[:, qi:qi + 1],
                                                             mn[:, qi:qi + 1],
                                                             -CS)),
                                         [tmn, prev_tex.get(qi)], "VV")
                            psl = slice(qi * 512, (qi + 1) * 512)
                            tex = P.emit("scalar", (lambda e, qi=qi, s=sc_src,
                                         psl=psl:
                                         e.activation(pb16[:, psl], s, ACT.Exp,
                                                      bias=nbr[:, qi:qi + 1],
                                                      scale=CS,
                                                      accum_out=srow[:, qi:qi + 1])),
                                         [tnb, prev_tl.get(qi),
                                          prev_ptm.get(qi)] + tprep, "SV")
                            prev_tex[qi] = tex
                            texs[qi] = tex
                            tdxs[qi] = tdx
                            if not diag:
                                bsc.add_reader(tex)
                            if diag:
                                prev_smsk[qi] = tex
                            if first:
                                tl = P.emit("vector", (lambda e, qi=qi:
                                            e.tensor_copy(lrow[:, qi:qi + 1],
                                                          srow[:, qi:qi + 1])),
                                            [tex], "VV")
                            else:
                                tl = P.emit("vector", (lambda e, qi=qi:
                                            e.scalar_tensor_tensor(
                                                lrow[:, qi:qi + 1],
                                                lrow[:, qi:qi + 1],
                                                drow[:, qi:qi + 1],
                                                srow[:, qi:qi + 1],
                                                ALU.mult, ALU.add)),
                                            [tex, tdx], "VV")
                            prev_tl[qi] = tl
                        for qi in active:
                            g_cnt = CNT[TILES_ROLE[0][sti * 4 + qi]]
                            first = (kt == 0)
                            diag = (kt == g_cnt - 1)
                            tdx = tdxs[qi]
                            psl = slice(qi * 512, (qi + 1) * 512)
                            wpt = b_pt.wwaits() + [texs[qi], tib]
                            tptm = None
                            for k4 in range(4):
                                tptm = P.emit("tensor", (lambda e, k4=k4, psl=psl,
                                              qi=qi:
                                              e.matmul(b_pt.ap[:, k4 * 128:
                                                               (k4 + 1) * 128],
                                                       pb16[:, qi * 512 + k4 * 128:
                                                            qi * 512 +
                                                            (k4 + 1) * 128],
                                                       IDB[:], start=True,
                                                       stop=True)),
                                              wpt if k4 == 0 else (),
                                              "TV" if k4 == 3 else None)
                            b_pt.set_writer(tptm)
                            prev_ptm[qi] = tptm
                            tpte = P.emit("scalar", lambda e: e.copy(ptb[:],
                                          b_pt.ap[:]), b_pt.rwaits(), "SV")
                            b_pt.add_reader(tpte)
                            for vn in range(4):
                                bav = b_av2[vn % 2]
                                wav = bav.wwaits() + [tpte, tvl]
                                tav = None
                                for k4 in range(4):
                                    tav = P.emit("tensor",
                                                 (lambda e, vn=vn, k4=k4, bav=bav,
                                                  vb=vb:
                                                  e.matmul(bav.ap[:, 0:512],
                                                           ptb[:, k4 * 128:
                                                               (k4 + 1) * 128],
                                                           vb[:, k4 * D + vn * 512:
                                                              k4 * D +
                                                              (vn + 1) * 512],
                                                           start=(k4 == 0),
                                                           stop=(k4 == 3))),
                                                 wav if k4 == 0 else (),
                                                 "TV" if k4 == 3 else None)
                                bav.set_writer(tav)
                                vread.append(tav)
                                osl = slice(qi * D + vn * 512,
                                            qi * D + (vn + 1) * 512)
                                if first:
                                    tou = P.emit("vector",
                                                 (lambda e, osl=osl, bav=bav:
                                                  e.tensor_copy(oacc[:, osl],
                                                                bav.ap[:, 0:512])),
                                                 bav.rwaits(), "VV")
                                else:
                                    tou = P.emit("vector",
                                                 (lambda e, osl=osl, qi=qi, bav=bav:
                                                  e.scalar_tensor_tensor(
                                                      oacc[:, osl], oacc[:, osl],
                                                      drow[:, qi:qi + 1],
                                                      bav.ap[:, 0:512],
                                                      ALU.mult, ALU.add)),
                                                 bav.rwaits() + [tdx], "VV")
                                bav.add_reader(tou)
                                if diag:
                                    oacc_done[qi].append(tou)
                        kprev[kt % 2] = (P.emit("vector", lambda e: e.engine_nop(),
                                                kread, "VV") if kread else None)
                        vprev[kt % 2] = (P.emit("vector", lambda e: e.engine_nop(),
                                                vread, "VV") if vread else None)
                    evt = []
                    prev_tt = []
                    for qi in range(4):
                        tri = P.emit("vector", (lambda e, qi=qi:
                                     e.reciprocal(linv[:, qi:qi + 1],
                                                  lrow[:, qi:qi + 1])),
                                     oacc_done[qi], "VV")
                        tob = P.emit("scalar", (lambda e, qi=qi:
                                     e.activation(ob16[:],
                                                  oacc[:, qi * D:(qi + 1) * D],
                                                  ACT.Identity,
                                                  scale=linv[:, qi:qi + 1])),
                                     [tri] + oacc_done[qi] + prev_tt, "SV")
                        prev_tt = []
                        for grp in range(4):
                            wfin = b_fin.wwaits() + [tob, tib]
                            tt = None
                            for k4 in range(4):
                                oc = grp * 4 + k4
                                tt = P.emit("tensor", (lambda e, oc=oc, k4=k4:
                                            e.matmul(b_fin.ap[:, k4 * 128:
                                                              (k4 + 1) * 128],
                                                     ob16[:, oc * 128:
                                                          (oc + 1) * 128],
                                                     IDB[:], start=True,
                                                     stop=True)),
                                            wfin if k4 == 0 else (),
                                            "TV" if k4 == 3 else None)
                            b_fin.set_writer(tt)
                            prev_tt.append(tt)
                            dst = otst[:].rearrange("p (c t) -> p c t", c=C)[
                                :, grp * 4:(grp + 1) * 4, qi * 128:(qi + 1) * 128]
                            tte = P.emit("scalar", (lambda e, dst=dst:
                                         e.copy(dst, b_fin.ap[:].rearrange(
                                             "p (c t) -> p c t", c=4))),
                                         b_fin.rwaits(), "SV")
                            b_fin.add_reader(tte)
                            evt.append(tte)
                    P.dma("gpsimd", fm_view(ot_s, 0, D, sti * 512, (sti + 1) * 512),
                          otst[:].rearrange("p (c t) -> p c t", c=C), evt)
                    P.barrier()

        _phase3()
        def weight_prep(ph, wdram, nrows, ncol, wq_t, widx):
            ntil = nrows // 128
            nch = ncol // 128
            NW = 4 if ncol <= 2048 else 2
            wt2 = [ph.enter_context(nc.sbuf_tensor(f"wp{widx}w{i}",
                   [128, ncol], F32)) for i in range(NW)]
            ab = None
            s1b = ph.enter_context(nc.sbuf_tensor(f"wp{widx}s1", [128, ncol],
                                                  F32))
            s2b = ph.enter_context(nc.sbuf_tensor(f"wp{widx}s2", [128, ncol],
                                                  F32))
            trn = ph.enter_context(nc.sbuf_tensor(f"wp{widx}t", [128, ncol],
                                                  BF16))
            rsum = ph.enter_context(nc.sbuf_tensor(f"wp{widx}r", [1, 1], F32))
            invc = ph.enter_context(nc.sbuf_tensor(f"wp{widx}i", [128, 1],
                                                   F32))
            accb = ph.enter_context(nc.sbuf_tensor(f"wp{widx}ac",
                                                    [128, 128], F32))
            t1 = ph.enter_context(nc.sbuf_tensor(f"wp{widx}t1", [128, 1], F32))
            tld = P.dma("sync", accb[:],
                        acc_s[:, widx * 128:(widx + 1) * 128])
            bsum = banks[0]
            wsum = bsum.wwaits()
            tr1 = P.emit("vector", (lambda e: e.reduce_sum(
                t1[:], accb[:, 0:128], axis=AX.X)), [tld], "VV")
            tsl = P.emit("tensor", (lambda e: e.matmul(
                bsum.ap[0:1, 0:1], t1[:], ONC[:], start=True, stop=True)),
                wsum + [tr1, tonc], "TV")
            bsum.set_writer(tsl)
            trs = P.emit("vector", lambda e: e.tensor_copy(
                rsum[:], bsum.ap[0:1, 0:1]), bsum.rwaits(), "VV")
            bsum.add_reader(trs)
            tms = P.emit("vector", (lambda e: e.tensor_scalar(
                wrow[:, 2 * widx:2 * widx + 1], rsum[:], 1.0 / (1 << 23), EPS,
                ALU.mult, ALU.max)), [trs], "VV")
            tiv = P.emit("vector", (lambda e: e.reciprocal(
                wrow[:, 2 * widx + 1:2 * widx + 2],
                wrow[:, 2 * widx:2 * widx + 1])), [tms], "VV")
            tkr = P.emit("vector", (lambda e: e.tensor_scalar_mul(
                rsum[:], wrow[:, 2 * widx:2 * widx + 1], 0.5 / QB)),
                [tms, trs], "VV")
            binv = banks[1]
            wbi = binv.wwaits()
            tbi = P.emit("tensor", (lambda e: e.matmul(
                binv.ap[:, 0:1], ONR[:], wrow[:, 2 * widx + 1:2 * widx + 2],
                start=True, stop=True)), wbi + [tiv, tonr], "TV")
            tbk = P.emit("tensor", (lambda e: e.matmul(
                binv.ap[:, 1:2], ONR[:], rsum[:], start=True, stop=True)),
                [tkr], "TV")
            binv.set_writer(tbk)
            tin = P.emit("vector", lambda e: e.tensor_copy(invc[:],
                         binv.ap[:, 0:1]), [tbi], "VV")
            tkc = P.emit("vector", (lambda e: e.tensor_copy(
                WCOL[:, widx:widx + 1], binv.ap[:, 1:2])), [tbk], "VV")
            binv.add_reader(tin)
            binv.add_reader(tkc)
            wq_toks = []
            tprev2 = None
            tread2 = {}
            for ti in range(ntil):
                wt = wt2[ti % NW]
                eng = "sync" if ti % 2 == 0 else "gpsimd"
                tw = P.dma(eng, wt[:], wdram[ti * 128:(ti + 1) * 128, :],
                           [tread2.get(ti - NW, tin)])
                ts1 = P.emit("scalar", (lambda e, wt=wt: e.activation(
                    s1b[:], wt[:], ACT.Sign, bias=-0.5, scale=invc[:, 0:1])),
                    [tw, tin, tc, tprev2], "SV")
                ts2 = P.emit("scalar", (lambda e, wt=wt: e.activation(
                    s2b[:], wt[:], ACT.Sign, bias=0.5, scale=invc[:, 0:1])),
                    [ts1], "SV")
                tprev2 = ts2
                tread2[ti] = ts2
                tt3 = P.emit("vector", (lambda e: e.tensor_add(
                    trn[:], s1b[:], s2b[:])), [ts2], "VV")
                for grp in range((nch + 3) // 4):
                    bkt = banks[2 + (grp % 2)]
                    wt_ = bkt.wwaits() + [tt3, tib]
                    k4n = min(4, nch - grp * 4)
                    tt = None
                    for k4 in range(k4n):
                        cc = grp * 4 + k4
                        tt = P.emit("tensor", (lambda e, cc=cc, k4=k4, bkt=bkt:
                                    e.matmul(bkt.ap[:, k4 * 128:(k4 + 1) * 128],
                                             trn[:, cc * 128:(cc + 1) * 128],
                                             IDB[:], start=True, stop=True)),
                                    wt_ if k4 == 0 else (),
                                    "TV" if k4 == k4n - 1 else None)
                    bkt.set_writer(tt)
                    dst = wq_t[:].rearrange("p (c f) -> p c f", c=nch)[
                        :, grp * 4:grp * 4 + k4n, ti * 128:(ti + 1) * 128]
                    tte = P.emit("scalar", (lambda e, dst=dst, bkt=bkt,
                                 k4n=k4n: e.copy(dst,
                                     bkt.ap[:, 0:k4n * 128].rearrange(
                                         "p (c t) -> p c t", c=k4n))),
                                 bkt.rwaits(), "SV")
                    bkt.add_reader(tte)
                    wq_toks.append(tte)
            return wq_toks

        def quant_rounds(tag, nch, n, ubuf, utoks, qout, ftm, mstore,
                         qint, n0b, rbuf, s2b2, sc_r, sc_q, qguard,
                         rbuf2=None):
            nt = n // 128
            if rbuf2 is None:
                rbuf2 = rbuf
            cmul = C1 if tag == 1 else C2
            t = utoks
            tmax = P.emit("vector", (lambda e: e.tensor_copy(rbuf[:, 0:n],
                         ubuf[:, 0:n])), t + qguard, "VV")
            tmin = P.emit("vector", (lambda e: e.tensor_copy(rbuf[:, n:2 * n],
                         ubuf[:, 0:n])), t, "VV")
            for cc2 in range(1, nch):
                tmax = P.emit("vector", (lambda e, cc2=cc2: e.tensor_max(
                    rbuf[:, 0:n], rbuf[:, 0:n],
                    ubuf[:, cc2 * n:(cc2 + 1) * n])), [tmax], "VV")
                tmin = P.emit("vector", (lambda e, cc2=cc2: e.tensor_tensor(
                    rbuf[:, n:2 * n], rbuf[:, n:2 * n],
                    ubuf[:, cc2 * n:(cc2 + 1) * n], ALU.min)), [tmin], "VV")
            tam = P.emit("vector", (lambda e: e.scalar_tensor_tensor(
                rbuf[:, 0:n], rbuf[:, n:2 * n], -1.0, rbuf[:, 0:n],
                ALU.mult, ALU.max)), [tmax, tmin], "VV")
            btr = banks[5]
            wtr = btr.wwaits() + [tam, tif]
            tt = None
            for k4 in range(nt):
                tt = P.emit("tensor", (lambda e, k4=k4: e.matmul(
                    btr.ap[:, k4 * 128:(k4 + 1) * 128],
                    rbuf[:, k4 * 128:(k4 + 1) * 128], IDF[:],
                    start=True, stop=True)), wtr if k4 == 0 else (),
                    "TV" if k4 == nt - 1 else None)
            btr.set_writer(tt)
            ttm = []
            for k4 in range(nt):
                tr = P.emit("vector", (lambda e, k4=k4: e.reduce_max(
                    sc_r[:, k4:k4 + 1], btr.ap[:, k4 * 128:(k4 + 1) * 128],
                    axis=AX.X)), btr.rwaits(), "VV")
                btr.add_reader(tr)
                ttm.append(tr)
            if ftm is not None:
                tsc = P.emit("vector", (lambda e: e.tensor_mul(
                    sc_r[:, 0:nt], sc_r[:, 0:nt], ftm)), ttm, "VV")
            else:
                tsc = ttm[-1]
            tm_ = P.emit("vector", (lambda e: e.tensor_scalar(
                sc_r[:, 0:nt], sc_r[:, 0:nt], cmul, EPS, ALU.mult, ALU.max)),
                [tsc] + ttm, "VV")
            tok_m = None
            if mstore is not None:
                tok_m = P.emit("vector", (lambda e: e.tensor_copy(
                    mstore, sc_r[:, 0:nt])), [tm_], "VV")
            tqm = P.emit("vector", (lambda e: e.reciprocal(
                sc_q[:, 0:nt], sc_r[:, 0:nt])), [tm_], "VV")
            if ftm is not None:
                tqm = P.emit("vector", (lambda e: e.tensor_mul(
                    sc_q[:, 0:nt], sc_q[:, 0:nt], ftm)), [tqm], "VV")
            tqm = P.emit("vector", (lambda e: e.tensor_scalar_mul(
                sc_q[:, 0:nt], sc_q[:, 0:nt], QB * cmul)), [tqm], "VV")
            brow = banks[5]
            wrw = brow.wwaits() + [tqm, tif]
            tt = None
            for k4 in range(nt):
                tt = P.emit("tensor", (lambda e, k4=k4: e.matmul(
                    brow.ap[0:1, k4 * 128:(k4 + 1) * 128],
                    sc_q[:, k4:k4 + 1], IDF[:], start=True, stop=True)),
                    wrw if k4 == 0 else (), "TV" if k4 == nt - 1 else None)
            brow.set_writer(tt)
            trow = P.emit("vector", (lambda e: e.tensor_copy(
                rbuf[0:1, 0:n], brow.ap[0:1, 0:n])), brow.rwaits(), "VV")
            brow.add_reader(trow)
            brep = banks[6]
            wrep = brep.wwaits() + [trow, tonr]
            trep = P.emit("tensor", (lambda e: e.matmul(
                brep.ap[:, 0:n], ONR[:], rbuf[0:1, 0:n],
                start=True, stop=True)), wrep, "TV")
            brep.set_writer(trep)
            trepc = P.emit("vector", (lambda e: e.tensor_copy(
                rbuf[:, n:2 * n], brep.ap[:, 0:n])), brep.rwaits(), "VV")
            brep.add_reader(trepc)
            qtoks = []
            nb = min(nch, qint.shape[1] // n)
            for cb in range(0, nch, nb):
                nbb = min(nb, nch - cb)
                for cc2 in range(cb, cb + nbb):
                    sl = slice(cc2 * n, (cc2 + 1) * n)
                    P.emit("vector", (lambda e, sl=sl: e.tensor_mul(
                        ubuf[:, sl], ubuf[:, sl], rbuf[:, n:2 * n])),
                        [trepc], "VV")
                bsl = slice(cb * n, (cb + nbb) * n)
                m = nbb * n
                P.emit("vector", (lambda e, bsl=bsl, m=m: e.tensor_copy(
                    qint[:, 0:m], ubuf[:, bsl])), (), "VV")
                P.emit("vector", (lambda e, bsl=bsl, m=m:
                    e.scalar_tensor_tensor(
                        n0b[:, 0:m], qint[:, 0:m], 0.5, ubuf[:, bsl],
                        ALU.add, ALU.is_le)), (), "VV")
                P.emit("vector", (lambda e, bsl=bsl, m=m:
                    e.scalar_tensor_tensor(
                        s2b2[:, 0:m], qint[:, 0:m], -0.5, ubuf[:, bsl],
                        ALU.add, ALU.is_ge)), (), "VV")
                P.emit("vector", (lambda e, m=m: e.tensor_sub(
                    n0b[:, 0:m], n0b[:, 0:m], s2b2[:, 0:m])), (), "VV")
                tqf = P.emit("vector", (lambda e, bsl=bsl, m=m: e.tensor_tensor(
                    qout[:, bsl], qint[:, 0:m], n0b[:, 0:m], ALU.add)),
                    (), "VV")
                qtoks.append(tqf)
            return qtoks, tok_m

        def fwht_stages(nch, n, a, b, toks, split0=None):
            nst = int(math.log2(nch))
            cur, oth = a, b
            t = list(toks)
            for stg in range(nst):
                s = 1 << stg
                nblk = nch // (2 * s)
                av = cur[:].rearrange("p (blk two s t) -> p blk two s t",
                                      blk=nblk, two=2, s=s)
                ov = oth[:].rearrange("p (blk two s t) -> p blk two s t",
                                      blk=nblk, two=2, s=s)
                gb = 0
                vb = nblk - gb
                if stg == 0 and split0 is not None:
                    hb_ = nblk // 2
                    tt = []
                    for hi_, tk in ((0, split0[0]), (1, split0[1])):
                        b0, b1 = hi_ * hb_, (hi_ + 1) * hb_
                        tt.append(P.emit("vector", (lambda e, av=av, ov=ov,
                                  b0=b0, b1=b1: e.tensor_add(
                                      ov[:, b0:b1, 0], av[:, b0:b1, 0],
                                      av[:, b0:b1, 1])), t + [tk], "VV"))
                        tt.append(P.emit("vector", (lambda e, av=av, ov=ov,
                                  b0=b0, b1=b1: e.tensor_sub(
                                      ov[:, b0:b1, 1], av[:, b0:b1, 0],
                                      av[:, b0:b1, 1])), t + [tk], "VV"))
                    t = tt
                    cur, oth = oth, cur
                    continue
                tt = []
                tt.append(P.emit("vector", (lambda e, av=av, ov=ov, vb=vb:
                          e.tensor_add(ov[:, 0:vb, 0], av[:, 0:vb, 0],
                                       av[:, 0:vb, 1])), t, "VV"))
                tt.append(P.emit("vector", (lambda e, av=av, ov=ov, vb=vb:
                          e.tensor_sub(ov[:, 0:vb, 1], av[:, 0:vb, 0],
                                       av[:, 0:vb, 1])), t, "VV"))
                if gb:
                    tt.append(P.emit("gpsimd", (lambda e, av=av, ov=ov, vb=vb,
                              nblk=nblk: e.tensor_add(
                                  ov[:, vb:nblk, 0], av[:, vb:nblk, 0],
                                  av[:, vb:nblk, 1])), t, "GC"))
                    tt.append(P.emit("gpsimd", (lambda e, av=av, ov=ov, vb=vb,
                              nblk=nblk: e.tensor_sub(
                                  ov[:, vb:nblk, 1], av[:, vb:nblk, 0],
                                  av[:, vb:nblk, 1])), t, "GC"))
                t = tt
                cur, oth = oth, cur
            return cur, oth, t

        def _phase4():
            with ExitStack() as ph:
                wqt = ph.enter_context(nc.sbuf_tensor("p4wqt", [128, C * FI],
                                                      BF16))
                with ExitStack() as pp:
                    weight_prep(pp, wup, FI, D, wqt, 0)
                    P.barrier()
                n = 256
                ngr = OWN // n
                xnb = ph.enter_context(nc.sbuf_tensor("p4xn", [128, C * n], F32))
                pong = ph.enter_context(nc.sbuf_tensor("p4pg", [128, C * n], F32))
                s2b2 = ph.enter_context(nc.sbuf_tensor("p4s2", [128, n], F32))
                n0b = ph.enter_context(nc.sbuf_tensor("p4n0", [128, n], F32))
                rbuf = ph.enter_context(nc.sbuf_tensor("p4rb", [128, 2 * n], F32))
                qint = ph.enter_context(nc.sbuf_tensor("p4qi", [128, n], I32))
                xqb = ph.enter_context(nc.sbuf_tensor("p4xq", [128, C * n], BF16))
                hst2 = ph.enter_context(nc.sbuf_tensor("p4h2", [128, FC * n], F32))
                hst = n0b
                sc_r = ph.enter_context(nc.sbuf_tensor("p4sr", [128, 2], F32))
                sc_q = ph.enter_context(nc.sbuf_tensor("p4sq", [128, 2], F32))
                prevx = None
                prevh = None
                for gi in range(ngr):
                    tx = P.dma("sync", xnb[:].rearrange("p (c t) -> p c t", c=C),
                               fm_view(xn_s, 0, D, gi * n, (gi + 1) * n), [prevx])
                    fb, ob, ft = fwht_stages(C, n, xnb, pong, [tx])
                    utoks = []
                    for cc in range(C):
                        bk = banks[cc % 4]
                        w0 = bk.wwaits() + ft + [th]
                        tmm = P.emit("tensor", (lambda e, cc=cc, fb=fb, bk=bk, n=n:
                                     e.matmul(bk.ap[:, 0:n], H128[:],
                                              fb[:, cc * n:(cc + 1) * n],
                                              start=True, stop=True)), w0, "TV")
                        bk.set_writer(tmm)
                        te = P.emit("scalar", (lambda e, cc=cc, ob=ob, bk=bk, n=n:
                                    e.copy(ob[:, cc * n:(cc + 1) * n],
                                           bk.ap[:, 0:n])),
                                    bk.rwaits(), "SV")
                        bk.add_reader(te)
                        utoks.append(te)
                    qtoks, _ = quant_rounds(1, C, n, ob, [utoks[-1]], xqb, None,
                                            None, qint, n0b, rbuf, s2b2,
                                            sc_r, sc_q, [prevh] if prevh else [])
                    tf = P.emit("vector", (lambda e, gi=gi: e.tensor_scalar(
                        f2tm[:, 2 * gi:2 * gi + 2], sc_r[:, 0:2],
                        WCOL[:, 0:1], None, ALU.mult)), [qtoks[0]], "VV")
                    tf2 = P.emit("vector", (lambda e, gi=gi: e.tensor_mul(
                        f2tm[:, 2 * gi:2 * gi + 2], f2tm[:, 2 * gi:2 * gi + 2],
                        f2tm[:, 2 * gi:2 * gi + 2])), [tf], "VV")
                    sqtoks = []
                    for j in range(FC):
                        bk = banks[4 + (j % 2)]
                        w0 = bk.wwaits() + [qtoks[-1]]
                        tmm = None
                        for c in range(C):
                            tmm = P.emit("tensor", (lambda e, j=j, c=c, bk=bk, n=n:
                                         e.matmul(bk.ap[:, 0:n],
                                                  wqt[:, c * FI + j * 128:
                                                     c * FI + (j + 1) * 128],
                                                  xqb[:, c * n:(c + 1) * n],
                                                  start=(c == 0),
                                                  stop=(c == C - 1))),
                                         w0 if c == 0 else (),
                                         "TV" if c == C - 1 else None)
                        bk.set_writer(tmm)
                        trl = P.emit("scalar", (lambda e, bk=bk, j=j, n=n:
                                     e.activation(hst2[:, j * n:(j + 1) * n],
                                                  bk.ap[:, 0:n], ACT.Relu)),
                            bk.rwaits() + [prevh], "SV")
                        bk.add_reader(trl)
                        tsq2 = P.emit("vector", (lambda e, j=j, n=n: e.tensor_mul(
                            hst2[:, j * n:(j + 1) * n],
                            hst2[:, j * n:(j + 1) * n],
                            hst2[:, j * n:(j + 1) * n])),
                            [trl], "VV")
                        sqtoks.append(tsq2)
                    prevh = P.dma("gpsimd", fm_view(h_s, 0, FI, gi * n,
                                                    (gi + 1) * n),
                                  hst2[:].rearrange("p (c t) -> p c t", c=FC),
                                  sqtoks)
                    prevx = prevh
                P.barrier()

        _phase4()
        def _phase5():
            with ExitStack() as ph:
                n = 512
                NB = 4
                ngr = OWN // n
                hb = ph.enter_context(nc.sbuf_tensor("p5h", [128, FC * n], F32))
                pong = ph.enter_context(nc.sbuf_tensor("p5pg", [128, FC * n], F32))
                s2b2 = ph.enter_context(nc.sbuf_tensor("p5s2", [128, NB * n],
                                                       F32))
                n0b = ph.enter_context(nc.sbuf_tensor("p5n0", [128, NB * n],
                                                      F32))
                rbuf = ph.enter_context(nc.sbuf_tensor("p5rb", [128, 2 * n], F32))
                rbuf2 = ph.enter_context(nc.sbuf_tensor("p5rb2", [128, NB * n],
                                                        F32))
                qint = ph.enter_context(nc.sbuf_tensor("p5qi", [128, NB * n],
                                                       I32))
                hqb = ph.enter_context(nc.sbuf_tensor("p5hq", [128, FC * n], BF16))
                sc_r = ph.enter_context(nc.sbuf_tensor("p5sr", [128, 4], F32))
                sc_q = ph.enter_context(nc.sbuf_tensor("p5sq", [128, 4], F32))
                prevx = None
                prevq = None
                for gi in range(ngr):
                    hc = FC // 2
                    tx0 = P.dma("sync",
                                hb[:, 0:hc * n].rearrange("p (c t) -> p c t",
                                                          c=hc),
                                fm_view(h_s, 0, hc * 128, gi * n, (gi + 1) * n),
                                [prevx])
                    tx1 = P.dma("gpsimd",
                                hb[:, hc * n:FC * n].rearrange(
                                    "p (c t) -> p c t", c=hc),
                                fm_view(h_s, hc * 128, FI, gi * n,
                                        (gi + 1) * n), [prevx])
                    fb, ob, ft = fwht_stages(FC, n, hb, pong, [],
                                             split0=(tx0, tx1))
                    utoks = []
                    for cc in range(FC):
                        bk = banks[cc % 4]
                        w0 = bk.wwaits() + ft + [th]
                        tmm = P.emit("tensor", (lambda e, cc=cc, fb=fb, bk=bk, n=n:
                                     e.matmul(bk.ap[:, 0:n], H128[:],
                                              fb[:, cc * n:(cc + 1) * n],
                                              start=True, stop=True)), w0, "TV")
                        bk.set_writer(tmm)
                        te = P.emit("scalar", (lambda e, cc=cc, ob=ob, bk=bk, n=n:
                                    e.copy(ob[:, cc * n:(cc + 1) * n],
                                           bk.ap[:, 0:n])),
                                    bk.rwaits(), "SV")
                        bk.add_reader(te)
                        utoks.append(te)
                    nt = n // 128
                    qtoks, _ = quant_rounds(2, FC, n, ob, [utoks[-1]], hqb,
                                            f2tm[:, nt * gi:nt * gi + nt],
                                            m2tm[:, nt * gi:nt * gi + nt],
                                            qint, n0b, rbuf, s2b2, sc_r, sc_q,
                                            list(prevq or []), rbuf2)
                    sp_toks = []
                    for bi, cb in enumerate(range(0, FC, 4)):
                        tsp = P.dma("gpsimd",
                                    fm_view(hq_s, cb * 128, (cb + 4) * 128,
                                            gi * n, (gi + 1) * n),
                                    hqb[:, cb * n:(cb + 4) * n].rearrange(
                                        "p (c t) -> p c t", c=4),
                                    [qtoks[bi]])
                        sp_toks.append(tsp)
                    prevq = sp_toks
                    prevx = sp_toks[-1]
                P.barrier()

        _phase5()
        def _phase6():
            with ExitStack() as ph:
                wqd = ph.enter_context(nc.sbuf_tensor("p6wqd", [128, FC * D],
                                                      BF16))
                with ExitStack() as pp:
                    weight_prep(pp, wdn, D, FI, wqd, 1)
                    P.barrier()
                n = 512
                ngr = OWN // n
                nt = n // 128
                hqb = ph.enter_context(nc.sbuf_tensor("p6hq", [128, FC * n],
                                                      BF16))
                fdr = ph.enter_context(nc.sbuf_tensor("p6fd", [128, nt], F32))
                fdrep = ph.enter_context(nc.sbuf_tensor("p6fe", [128, n],
                                                        F32))
                frow = ph.enter_context(nc.sbuf_tensor("p6fr", [1, n], F32))
                o2b = ph.enter_context(nc.sbuf_tensor("p6o2", [128, 2 * n],
                                                       F32))
                gst = ph.enter_context(nc.sbuf_tensor("p6g", [128, C * n], F32))
                prevx = None
                prevg = None
                for gi in range(ngr):
                    tx = P.dma("sync", hqb[:].rearrange("p (c t) -> p c t", c=FC),
                               fm_view(hq_s, 0, FI, gi * n, (gi + 1) * n), [prevx])
                    tfd = P.emit("vector", (lambda e, gi=gi, nt=nt:
                        e.tensor_scalar(
                        fdr[:, 0:nt], m2tm[:, nt * gi:nt * gi + nt],
                        WCOL[:, 1:2],
                        None, ALU.mult)), [prevg], "VV")
                    btr = banks[5]
                    wtr = btr.wwaits() + [tfd, tif]
                    tt = None
                    for k4 in range(nt):
                        tt = P.emit("tensor", (lambda e, k4=k4: e.matmul(
                            btr.ap[0:1, k4 * 128:(k4 + 1) * 128],
                            fdr[:, k4:k4 + 1], IDF[:], start=True, stop=True)),
                            wtr if k4 == 0 else (),
                            "TV" if k4 == nt - 1 else None)
                    btr.set_writer(tt)
                    trw = P.emit("vector", (lambda e: e.tensor_copy(
                        frow[:], btr.ap[0:1, 0:n])), btr.rwaits(), "VV")
                    btr.add_reader(trw)
                    brep = banks[6]
                    wrep = brep.wwaits() + [trw, tonr]
                    trep = P.emit("tensor", (lambda e: e.matmul(
                        brep.ap[:, 0:n], ONR[:], frow[:], start=True, stop=True)),
                        wrep, "TV")
                    brep.set_writer(trep)
                    tfde = P.emit("vector", (lambda e: e.tensor_copy(
                        fdrep[:, 0:n], brep.ap[:, 0:n])),
                        brep.rwaits() + [prevg], "VV")
                    brep.add_reader(tfde)
                    gtoks = []
                    prev_sil = [None, None]
                    for j in range(C):
                        bk = banks[(j % 2)]
                        w0 = bk.wwaits() + [tx]
                        tmm = None
                        for c in range(FC):
                            tmm = P.emit("tensor", (lambda e, j=j, c=c, bk=bk, n=n:
                                         e.matmul(bk.ap[:, 0:n],
                                                  wqd[:, c * D + j * 128:
                                                     c * D + (j + 1) * 128],
                                                  hqb[:, c * n:(c + 1) * n],
                                                  start=(c == 0),
                                                  stop=(c == FC - 1))),
                                         w0 if c == 0 else (),
                                         "TV" if c == FC - 1 else None)
                        bk.set_writer(tmm)
                        osl = slice((j % 2) * n, (j % 2) * n + n)
                        tsc2 = P.emit("vector", (lambda e, bk=bk, osl=osl, n=n:
                                      e.tensor_mul(o2b[:, osl], bk.ap[:, 0:n],
                                                   fdrep[:, 0:n])),
                                      bk.rwaits() + [tfde, prev_sil[j % 2]],
                                      "VV")
                        bk.add_reader(tsc2)
                        tsil = P.emit("scalar", (lambda e, j=j, osl=osl, n=n:
                                      e.activation(gst[:, j * n:(j + 1) * n],
                                                   o2b[:, osl], ACT.Silu)),
                            [tsc2, prevg], "SV")
                        prev_sil[j % 2] = tsil
                        gtoks.append(tsil)
                    prevg = P.dma("gpsimd", fm_view(g_s, 0, D, gi * n,
                                                    (gi + 1) * n),
                                  gst[:].rearrange("p (c t) -> p c t", c=C),
                                  gtoks)
                    prevx = prevg
                P.barrier()

        _phase6()
        def _phase7():
            with ExitStack() as ph:
                n = 512
                ngr = OWN // n
                wob = ph.enter_context(nc.sbuf_tensor("p7wo", [128, C * D], BF16))
                otb = ph.enter_context(nc.sbuf_tensor("p7ot", [128, C * n], BF16))
                xnb = ph.enter_context(nc.sbuf_tensor("p7xn", [128, C * n], F32))
                gb = ph.enter_context(nc.sbuf_tensor("p7g", [128, C * n], F32))
                xb = ph.enter_context(nc.sbuf_tensor("p7x", [128, C * n], F32))
                two = P.dma("gpsimd", wob[:].rearrange("p (c j) -> p c j", c=C),
                            fm_view(wo_d, 0, D, 0, D))
                prevo = prevn = prevgl = prevxl = None
                for gi in range(ngr):
                    tot = P.dma("sync", otb[:].rearrange("p (c t) -> p c t", c=C),
                                fm_view(ot_s, 0, D, gi * n, (gi + 1) * n), [prevo])
                    txn = P.dma("sync", xnb[:].rearrange("p (c t) -> p c t", c=C),
                                fm_view(xn_s, 0, D, gi * n, (gi + 1) * n), [prevn])
                    tgl = P.dma("sync", gb[:].rearrange("p (c t) -> p c t", c=C),
                                fm_view(g_s, 0, D, gi * n, (gi + 1) * n), [prevgl])
                    txl = P.dma("sync", xb[:].rearrange("p (c t) -> p c t", c=C),
                                fm_view(xo, 0, D, gi * n, (gi + 1) * n), [prevxl])
                    otoks = []
                    mmtoks = []
                    for j in range(C):
                        bk = banks[(j % 2)]
                        w0 = bk.wwaits() + [tot, two]
                        tmm = None
                        for c in range(C):
                            tmm = P.emit("tensor", (lambda e, j=j, c=c, bk=bk, n=n:
                                         e.matmul(bk.ap[:, 0:n],
                                                  wob[:, c * D + j * 128:
                                                     c * D + (j + 1) * 128],
                                                  otb[:, c * n:(c + 1) * n],
                                                  start=(c == 0),
                                                  stop=(c == C - 1))),
                                         w0 if c == 0 else (),
                                         "TV" if c == C - 1 else None)
                        bk.set_writer(tmm)
                        mmtoks.append(tmm)
                        sl = slice(j * n, (j + 1) * n)
                        ts_ = P.emit("vector", (lambda e, sl=sl, bk=bk, n=n:
                                     e.tensor_add(xnb[:, sl], bk.ap[:, 0:n],
                                                  xnb[:, sl])),
                                     bk.rwaits() + [txn], "VV")
                        bk.add_reader(ts_)
                        tsg = P.emit("vector", (lambda e, sl=sl: e.tensor_mul(
                            gb[:, sl], xnb[:, sl], gb[:, sl])), [ts_, tgl], "VV")
                        tout = P.emit("vector", (lambda e, sl=sl: e.tensor_add(
                            xb[:, sl], xb[:, sl], gb[:, sl])), [tsg, txl], "VV")
                        otoks.append(tout)
                    tso = P.dma_out("sync", fm_view(out_d, 0, D, gi * n,
                                                    (gi + 1) * n),
                                    xb[:].rearrange("p (c t) -> p c t", c=C),
                                    otoks)
                    prevo = prevn = prevgl = prevxl = tso
                P.barrier()

        _phase7()
        if dbg:
            for name in dbg:
                src = scr[name]
                dst = dbg_outs[name]
                nr = src.shape[0]
                for rr in range(0, nr, 512):
                    P.dma_out("gpsimd", dst[rr:rr + 512, :],
                              src[rr:rr + 512, :])
            P.barrier()

        P.finalize(blk)
    return nc


_CACHE = {}


def _host_inputs(L0_attn_x, L0_attn_qkv, L0_attn_o_proj, L0_attn_gamma,
                 L0_attn_beta, L0_ffn_w_up, L0_ffn_w_down):
    x = np.asarray(L0_attn_x, dtype=np.float32)
    qkv = np.ascontiguousarray(np.asarray(L0_attn_qkv, dtype=np.float32))
    wo = np.ascontiguousarray(np.asarray(L0_attn_o_proj, dtype=np.float32))
    wup = np.ascontiguousarray(np.asarray(L0_ffn_w_up, dtype=np.float32))
    wdn = np.ascontiguousarray(np.asarray(L0_ffn_w_down, dtype=np.float32))
    gamC = np.ascontiguousarray(
        np.asarray(L0_attn_gamma, np.float32).reshape(C, 128).T)
    betC = np.ascontiguousarray(
        np.asarray(L0_attn_beta, np.float32).reshape(C, 128).T)
    ii = np.arange(128)
    pc = np.zeros((128, 128), dtype=np.int64)
    a = ii[:, None] & ii[None, :]
    for b in range(7):
        pc += (a >> b) & 1
    h128 = (1.0 - 2.0 * (pc & 1)).astype(np.float32)
    identf = np.eye(128, dtype=np.float32)
    identb = np.eye(128).astype(ml_dtypes.bfloat16)
    consts = np.zeros((128, 8), dtype=np.float32)
    consts[:, 0] = -0.5
    consts[:, 1] = 0.5
    consts[:, 2] = EPS
    consts[:, 3] = 1.0
    ones_row = np.ones((1, 128), dtype=np.float32)
    shared = dict(qkv=qkv, o_proj=wo, w_up=wup, w_down=wdn, gammaC=gamC,
                  betaC=betC, h128f=h128, identf=identf, identb=identb,
                  consts=consts, ones_row=ones_row)
    in_maps = []
    metas = []
    for core in range(8):
        b = core // 2
        role = core % 2
        tiles = TILES_ROLE[role]
        rows = np.concatenate([np.arange(g * 128, (g + 1) * 128)
                               for g in tiles])
        x_seq = np.ascontiguousarray(x[b].T)
        x_own = np.ascontiguousarray(x[b][rows].T)
        masks = np.zeros((16, 128, 512), dtype=np.float32)
        for i, g in enumerate(tiles):
            cgt = CNT[g]
            k0 = (cgt - 1) * 512
            q_glob = g * 128 + np.arange(128)[:, None]
            k_glob = k0 + np.arange(512)[None, :]
            masks[i] = np.where(k_glob <= q_glob, 0.0, NEG)
        m = dict(shared)
        m["x_seq"] = x_seq
        m["x_own"] = x_own
        m["masks"] = np.ascontiguousarray(masks.reshape(16 * 128, 512))
        in_maps.append(m)
        metas.append((b, rows))
    return in_maps, metas


def kernel(**inputs):
    if "nc" not in _CACHE:
        _CACHE["nc"] = build_program()
    nc = _CACHE["nc"]
    in_maps, metas = _host_inputs(**inputs)
    res = run_bass_kernel_spmd(nc, in_maps, core_ids=list(range(8)))
    out = np.empty((4, SEQ, D), dtype=np.float32)
    for core in range(8):
        b, rows = metas[core]
        out[b][rows] = np.ascontiguousarray(res.results[core]["outT"]).T
    return out


# version 25
# speedup vs baseline: 1.0277x; 1.0149x over previous
import math
import numpy as np
import ml_dtypes
import concourse.bass as bass
import concourse.mybir as mybir
from concourse.bass_utils import run_bass_kernel_spmd
from contextlib import ExitStack

F32 = mybir.dt.float32
BF16 = mybir.dt.bfloat16
I32 = mybir.dt.int32
ALU = mybir.AluOpType
ACT = mybir.ActivationFunctionType
AX = mybir.AxisListType

D = 2048
C = 16
SEQ = 4096
OWN = 2048
FI = 4096
FC = 32
EPS = 1e-5
C1 = 1.0 / math.sqrt(2048.0)
C2 = 1.0 / 64.0
CS = 1.0 / math.sqrt(2048.0)
QB = 127.0
NEG = -1.0e12

TILES_ROLE = [
    [0, 1, 4, 5, 8, 9, 12, 13, 16, 17, 20, 21, 24, 25, 28, 29],
    [2, 3, 6, 7, 10, 11, 14, 15, 18, 19, 22, 23, 26, 27, 30, 31],
]
CNT = [(g + 4) // 4 for g in range(32)]

ENGS = ("sync", "gpsimd", "vector", "scalar", "tensor")
OWN_SEM = {"sync": "LD", "gpsimd": "GD", "vector": "VV", "scalar": "SV",
           "tensor": "TV"}
SKIP_SELF = set()


class Prog:
    NCHAIN = 6

    def __init__(self, nc, stack):
        self.nc = nc
        self.items = {e: [] for e in ENGS}
        self.sems = {}
        self.cnt = {}
        names = ["TV", "VV", "SV", "GC", "BS", "BG", "BV", "BC", "BT"]
        for q in ("LD", "GD"):
            for r in range(self.NCHAIN):
                names.append(f"{q}{r}")
        for name in names:
            self.sems[name] = stack.enter_context(nc.semaphore(name))
            self.cnt[name] = 0
        self.qidx = {"sync": 0, "gpsimd": 0}
        self.qidx_b = {"sync": 0, "gpsimd": 0}

    def emit(self, eng, fn, waits=(), inc=None, amt=1):
        tok = None
        incs = []
        if inc is not None:
            self.cnt[inc] += amt
            tok = (inc, self.cnt[inc])
            incs.append((inc, amt))
        w = tuple(t for t in waits if t is not None)
        self.items[eng].append([fn, w, incs])
        return tok

    def dma(self, eng, out, in_, waits=(), lane="a"):
        q = "LD" if eng == "sync" else "GD"
        nch_a = 4
        if lane == "a":
            idx = self.qidx[eng]
            self.qidx[eng] = idx + 1
            r = idx % nch_a
            k = idx // nch_a
        else:
            idx = self.qidx_b[eng]
            self.qidx_b[eng] = idx + 1
            r = nch_a + idx % (self.NCHAIN - nch_a)
            k = idx // (self.NCHAIN - nch_a)
        sem = f"{q}{r}"
        w = list(waits)
        if k > 0:
            w.append((sem, 16 * k))
        return self.emit(eng, (lambda e, o=out, i=in_: e.dma_start(out=o, in_=i)),
                         w, sem, 16)

    def dma_out(self, eng, out, in_, waits=()):
        return self.dma(eng, out, in_, waits)

    def barrier(self):
        bar = {"sync": "BS", "gpsimd": "BG", "vector": "BV", "scalar": "BC",
               "tensor": "BT"}
        toks = []
        for eng in ENGS:
            b = bar[eng]
            if eng in ("sync", "gpsimd"):
                self.cnt[b] += 1
                toks.append((b, self.cnt[b]))
                q = "LD" if eng == "sync" else "GD"
                w = [(f"{q}{r}", self.cnt[f"{q}{r}"])
                     for r in range(self.NCHAIN)]
                self.items[eng].append([(lambda e: e.nop()), tuple(w),
                                        [(b, 1)]])
            else:
                toks.append(self.dummy[eng](b))
        for eng in ENGS:
            self.items[eng].append([None, tuple(toks), []])

    def finalize(self, block):
        for eng in ENGS:
            items = self.items[eng]

            def body(e, items=items, eng=eng):
                waited = {}
                for fn, waits, incs in items:
                    for (s, v) in waits:
                        if (eng, s) in SKIP_SELF:
                            continue
                        if waited.get(s, -1) >= v:
                            continue
                        e.wait_ge(self.sems[s], v)
                        waited[s] = v
                    if fn is None:
                        continue
                    inst = fn(e)
                    for (s, amt) in incs:
                        inst.then_inc(self.sems[s], amt)
            getattr(block, eng)(body)


class Bank:
    def __init__(self, ap):
        self.ap = ap
        self.writer = None
        self.readers = []

    def wwaits(self):
        w = [t for t in self.readers if t is not None]
        if self.writer is not None:
            w.append(self.writer)
        self.readers = []
        return w

    def set_writer(self, tok):
        self.writer = tok

    def rwaits(self):
        return [self.writer] if self.writer else []

    def add_reader(self, tok):
        if tok is not None:
            self.readers.append(tok)


def fm_view(dram_ap, r0, r1, c0, c1):
    v = dram_ap[r0:r1, c0:c1].rearrange("(c p) t -> c p t", p=128)
    return v.transpose([1, 0, 2])


def build_program(dbg=()):
    nc = bass.Bass("TRN2", target_bir_lowering=False,
                   detect_race_conditions=False)
    xs = nc.dram_tensor("x_seq", [D, SEQ], F32, kind="ExternalInput").ap()
    xo = nc.dram_tensor("x_own", [D, OWN], F32, kind="ExternalInput").ap()
    qkv = nc.dram_tensor("qkv", [D, 3 * D], F32, kind="ExternalInput").ap()
    wo_d = nc.dram_tensor("o_proj", [D, D], F32, kind="ExternalInput").ap()
    wup = nc.dram_tensor("w_up", [FI, D], F32, kind="ExternalInput").ap()
    wdn = nc.dram_tensor("w_down", [D, FI], F32, kind="ExternalInput").ap()
    gam = nc.dram_tensor("gammaC", [128, C], F32, kind="ExternalInput").ap()
    bet = nc.dram_tensor("betaC", [128, C], F32, kind="ExternalInput").ap()
    msk = nc.dram_tensor("masks", [16 * 128, 512], F32, kind="ExternalInput").ap()
    h128f = nc.dram_tensor("h128f", [128, 128], F32, kind="ExternalInput").ap()
    idf_d = nc.dram_tensor("identf", [128, 128], F32, kind="ExternalInput").ap()
    idb_d = nc.dram_tensor("identb", [128, 128], BF16, kind="ExternalInput").ap()
    cst_d = nc.dram_tensor("consts", [128, 8], F32, kind="ExternalInput").ap()
    onr_d = nc.dram_tensor("ones_row", [1, 128], F32, kind="ExternalInput").ap()
    out_d = nc.dram_tensor("outT", [D, OWN], F32, kind="ExternalOutput").ap()

    xn_s = nc.dram_tensor("s_xn", [D, OWN], F32).ap()
    xn2_s = nc.dram_tensor("s_xn2", [D, SEQ], BF16).ap()
    qt_s = nc.dram_tensor("s_qt", [D, OWN], BF16).ap()
    kt_s = nc.dram_tensor("s_kt", [D, SEQ], BF16).ap()
    v_s = nc.dram_tensor("s_v", [SEQ, D], BF16).ap()
    ot_s = nc.dram_tensor("s_ot", [D, OWN], BF16).ap()
    h_s = nc.dram_tensor("s_h", [FI, OWN], F32).ap()
    hq_s = nc.dram_tensor("s_hq", [FI, OWN], BF16).ap()
    g_s = nc.dram_tensor("s_g", [D, OWN], F32).ap()
    acc_s = nc.dram_tensor("s_acc", [128, 256], F32).ap()
    scr = {"s_xn": xn_s, "s_xn2": xn2_s, "s_qt": qt_s, "s_kt": kt_s,
           "s_v": v_s, "s_ot": ot_s, "s_h": h_s, "s_hq": hq_s, "s_g": g_s}
    dbg_outs = {}
    for name in dbg:
        t = scr[name]
        dbg_outs[name] = nc.dram_tensor(
            "dbg_" + name, list(t.shape), F32, kind="ExternalOutput").ap()

    with ExitStack() as st:
        en = st.enter_context
        P = Prog(nc, st)
        gamC = en(nc.sbuf_tensor("gamC", [128, C], F32))
        betC = en(nc.sbuf_tensor("betC", [128, C], F32))
        H128 = en(nc.sbuf_tensor("H128", [128, 128], F32))
        IDF = en(nc.sbuf_tensor("IDF", [128, 128], F32))
        IDB = en(nc.sbuf_tensor("IDB", [128, 128], BF16))
        CSTS = en(nc.sbuf_tensor("CSTS", [128, 8], F32))
        ONR = en(nc.sbuf_tensor("ONR", [1, 128], F32))
        ONC = en(nc.sbuf_tensor("ONC", [128, 1], F32))
        ONCB = en(nc.sbuf_tensor("ONCB", [128, 1], BF16))
        f2tm = en(nc.sbuf_tensor("f2tm", [128, 16], F32))
        m2tm = en(nc.sbuf_tensor("m2tm", [128, 16], F32))
        wrow = en(nc.sbuf_tensor("wrow", [1, 4], F32))
        WCOL = en(nc.sbuf_tensor("WCOL", [128, 2], F32))
        nc.const_aps.register(F32, -0.5, CSTS[:, 0:1])
        nc.const_aps.register(F32, 0.5, CSTS[:, 1:2])
        nc.const_aps.register(F32, EPS, CSTS[:, 2:3])
        banks = [Bank(en(nc.psum_tensor(f"pb{i}", [128, 512], F32))[:])
                 for i in range(8)]
        DUMS = en(nc.sbuf_tensor("DUMS", [1, 1], F32))

        def _dummy_vector(b):
            return P.emit("vector", lambda e: e.engine_nop(), (), b)

        def _dummy_scalar(b):
            return P.emit("scalar", (lambda e: e.activation(
                DUMS[:], CSTS[0:1, 3:4], ACT.Identity)), (), b)

        def _dummy_tensor(b):
            bk = banks[7]
            w = bk.wwaits()
            tok = P.emit("tensor", (lambda e: e.matmul(
                bk.ap[0:1, 0:1], ONC[0:1, 0:1], ONC[0:1, 0:1],
                start=True, stop=True)), w, b)
            bk.set_writer(tok)
            return tok
        P.dummy = {"vector": _dummy_vector, "scalar": _dummy_scalar,
                   "tensor": _dummy_tensor}
        blk = en(nc.Block())

        tg = P.dma("sync", gamC[:], gam[:, :])
        tbq = P.dma("sync", betC[:], bet[:, :])
        th = P.dma("sync", H128[:], h128f[:, :])
        tif = P.dma("sync", IDF[:], idf_d[:, :])
        tib = P.dma("sync", IDB[:], idb_d[:, :])
        tc = P.dma("sync", CSTS[:], cst_d[:, :])
        tonr = P.dma("sync", ONR[:], onr_d[:, :])
        tonc = P.emit("vector", lambda e: e.tensor_copy(ONC[:], CSTS[:, 3:4]),
                      [tc], "VV")
        toncb = P.emit("vector", lambda e: e.tensor_copy(ONCB[:], CSTS[:, 3:4]),
                       [tc], "VV")

        def ln_group(n, xbuf, x_tok, xb16, xb16_tok, sqbuf, xn2buf, xnbuf,
                     rows, want_xn, gd):
            b_sx, b_sq, b_mu, b_r12, b_r1 = (banks[0], banks[1], banks[2],
                                             banks[3], banks[4])
            tsq = P.emit("scalar", (lambda e, xbuf=xbuf, sqbuf=sqbuf, n=n:
                         e.activation(sqbuf[:, 0:C * n], xbuf[:, 0:C * n],
                                      ACT.Square)),
                         [x_tok] + gd.get("sq", []), "SV")

            def mk_sums(bank, src, tok):
                w = bank.wwaits() + [tok, toncb]
                t = None
                for c in range(C):
                    last = (c == C - 1)
                    t = P.emit("tensor", (lambda e, c=c, src=src, bank=bank,
                               n=n, last=last:
                               e.matmul(bank.ap[0:1, 0:n], ONCB[:],
                                        src[:, c * n:(c + 1) * n],
                                        start=(c == 0), stop=last)),
                               w if c == 0 else (), "TV" if last else None)
                bank.set_writer(t)
                return t
            t1 = mk_sums(b_sx, xb16, xb16_tok)
            t2 = mk_sums(b_sq, sqbuf, tsq)
            r = rows
            tmu = P.emit("vector", (lambda e, n=n: e.tensor_scalar_mul(
                r["mu"][:, 0:n], b_sx.ap[0:1, 0:n], 1.0 / D)),
                [t1] + gd.get("rows", []), "VV")
            b_sx.add_reader(tmu)
            tm2 = P.emit("vector", (lambda e, n=n: e.tensor_mul(
                r["tmp"][:, 0:n], r["mu"][:, 0:n], r["mu"][:, 0:n])), [tmu], "VV")
            tvar = P.emit("vector", (lambda e, n=n: e.scalar_tensor_tensor(
                r["var"][:, 0:n], b_sq.ap[0:1, 0:n], 1.0 / D, r["tmp"][:, 0:n],
                ALU.mult, ALU.subtract)), [t2, tm2], "VV")
            b_sq.add_reader(tvar)
            tl1 = P.emit("scalar", (lambda e, n=n: e.activation(
                r["tmp"][:, 0:n], r["var"][:, 0:n], ACT.Ln, bias=EPS)),
                [tvar, tc], "SV")
            tr1 = P.emit("scalar", (lambda e, n=n: e.activation(
                r["r1"][:, 0:n], r["tmp"][:, 0:n], ACT.Exp, scale=-0.5)),
                [tl1], "SV")
            ta = P.emit("vector", (lambda e, n=n: e.tensor_mul(
                r["tmp"][:, 0:n], r["var"][:, 0:n], r["r1"][:, 0:n])), [tr1], "VV")
            tb2 = P.emit("vector", (lambda e, n=n: e.tensor_mul(
                r["tmp"][:, 0:n], r["tmp"][:, 0:n], r["r1"][:, 0:n])), [ta], "VV")
            tl2 = P.emit("scalar", (lambda e, n=n: e.activation(
                r["s2r"][:, 0:n], r["tmp"][:, 0:n], ACT.Ln, bias=EPS)), [tb2], "SV")
            tr2 = P.emit("scalar", (lambda e, n=n: e.activation(
                r["s2r"][:, 0:n], r["s2r"][:, 0:n], ACT.Exp, scale=-0.5)),
                [tl2], "SV")
            tr12 = P.emit("vector", (lambda e, n=n: e.tensor_mul(
                r["r12"][:, 0:n], r["r1"][:, 0:n], r["s2r"][:, 0:n])), [tr2], "VV")

            def bcast(bank, row, tok):
                w = bank.wwaits() + [tok, tonr]
                t = P.emit("tensor", (lambda e, bank=bank, row=row, n=n:
                           e.matmul(bank.ap[:, 0:n], ONR[:], row[:, 0:n],
                                    start=True, stop=True)), w, "TV")
                bank.set_writer(t)
                return t
            tmub = bcast(b_mu, r["mu"], tmu)
            tr12b = bcast(b_r12, r["r12"], tr12)
            tr1b = bcast(b_r1, r["r1"], tr1) if want_xn else None
            tmus = P.emit("vector", (lambda e, n=n: e.tensor_copy(
                r["mub"][:, 0:n], b_mu.ap[:, 0:n])),
                [tmub] + gd.get("rowsb", []), "VV")
            b_mu.add_reader(tmus)
            tr12s = P.emit("vector", (lambda e, n=n: e.tensor_copy(
                r["r12b"][:, 0:n], b_r12.ap[:, 0:n])), [tr12b], "VV")
            b_r12.add_reader(tr12s)
            if want_xn:
                tr1s = P.emit("vector", (lambda e, n=n: e.tensor_copy(
                    r["r1b"][:, 0:n], b_r1.ap[:, 0:n])), [tr1b], "VV")
                b_r1.add_reader(tr1s)
            toks_xn2, toks_xn = [], []
            for c in range(C):
                sl = slice(c * n, (c + 1) * n)
                tt1 = P.emit("vector", (lambda e, sl=sl, xbuf=xbuf, n=n:
                             e.tensor_sub(xbuf[:, sl], xbuf[:, sl],
                                          r["mub"][:, 0:n])),
                             [tmus, tsq, t1], "VV")
                if want_xn:
                    txn = P.emit("vector", (lambda e, sl=sl, xbuf=xbuf,
                                 xnbuf=xnbuf, n=n: e.tensor_mul(
                                     xnbuf[:, sl], xbuf[:, sl],
                                     r["r1b"][:, 0:n])),
                                 [tt1, tr1s, t2] + gd.get("xn", []), "VV")
                    toks_xn.append(txn)
                tt2 = P.emit("vector", (lambda e, sl=sl, xbuf=xbuf, n=n:
                             e.tensor_mul(xbuf[:, sl], xbuf[:, sl],
                                          r["r12b"][:, 0:n])),
                             [tt1, tr12s], "VV")
                t2f = P.emit("scalar", (lambda e, sl=sl, c=c, xbuf=xbuf,
                             xn2buf=xn2buf: e.activation(
                                 xn2buf[:, sl], xbuf[:, sl], ACT.Identity,
                                 bias=betC[:, c:c + 1], scale=gamC[:, c:c + 1])),
                             [tt2, tg, tbq, t1] + gd.get("x2", []), "SV")
                toks_xn2.append(t2f)
            new_guard = {"rows": [tmub, tr12b] + ([tr1b] if tr1b else []),
                         "rowsb": toks_xn2, "sq": [t2]}
            return toks_xn2, toks_xn, new_guard

        WPHW = 512
        WPT = []
        for (_wd, _hw) in ((wup, WPHW), (wdn, WPHW)):
            _per = _wd.shape[1] // _hw
            for _ti in range((_wd.shape[0] // 128) * _per):
                WPT.append((_wd, _hw, _per, _ti))

        def wp_make(ph, lo, hi, nbuf=8):
            hw = WPHW
            wt2 = [ph.enter_context(nc.sbuf_tensor(f"wm{lo}_{i}",
                   [128, hw], F32)) for i in range(nbuf)]
            abf = ph.enter_context(nc.sbuf_tensor(f"wm{lo}a", [128, hw], F32))
            accv = ph.enter_context(nc.sbuf_tensor(f"wm{lo}c",
                                                   [128, hi - lo], F32))
            st = {"pend": [], "pos": lo, "tprev": None, "tread": {}, "cnt": 0}

            def step(k):
                for (tw, wt, gc, cnt) in st["pend"]:
                    ta = P.emit("scalar", (lambda e, wt=wt, gc=gc:
                                e.activation(abf[:], wt[:], ACT.Abs,
                                             accum_out=accv[:, gc:gc + 1])),
                                [tw, st["tprev"]], "SV")
                    st["tprev"] = ta
                    st["tread"][cnt] = ta
                st["pend"] = []
                for _ in range(min(k, hi - st["pos"])):
                    gidx = st["pos"]
                    wdram, hw_, per, ti = WPT[gidx]
                    cnt = st["cnt"]
                    wt = wt2[cnt % nbuf]
                    rt, hf = ti // per, ti % per
                    eng = "sync" if cnt % 2 == 0 else "gpsimd"
                    tw = P.dma(eng, wt[:],
                               wdram[rt * 128:(rt + 1) * 128,
                                     hf * hw_:(hf + 1) * hw_],
                               [st["tread"].get(cnt - nbuf)], lane="b")
                    st["pend"].append((tw, wt, gidx - lo, cnt))
                    st["pos"] += 1
                    st["cnt"] = cnt + 1

            def flush():
                while st["pend"] or st["pos"] < hi:
                    step(nbuf)
                P.dma("gpsimd", acc_s[:, lo:hi], accv[:],
                      [st["tprev"]], lane="b")
            return step, flush

        def _phase0():
            with ExitStack() as ph:
                n = 512
                ngr = OWN // n
                xb = [ph.enter_context(nc.sbuf_tensor("p1x0", [128, C * n],
                                                      F32))] * 2
                xnb = ph.enter_context(nc.sbuf_tensor("p1xn", [128, C * n], F32))
                x2b2 = [ph.enter_context(nc.sbuf_tensor(f"p1x2{i}",
                        [128, C * n], BF16)) for i in range(2)]
                wq = ph.enter_context(nc.sbuf_tensor("p1wq", [128, C * D], BF16))
                qst2 = [ph.enter_context(nc.sbuf_tensor("p1qs",
                        [128, C * n], BF16))] * 2
                rows = {k: ph.enter_context(nc.sbuf_tensor(f"p1r{k}", [1, n], F32))
                        for k in ("mu", "var", "r1", "r12", "tmp", "s2r")}
                for k in ("mub", "r12b", "r1b"):
                    rows[k] = ph.enter_context(
                        nc.sbuf_tensor(f"p1r{k}", [128, n], F32))
                twq = P.dma("gpsimd", wq[:].rearrange("p (c j) -> p c j", c=C),
                            fm_view(qkv, 0, D, 0, D))
                wp1s, wp1f = wp_make(ph, 0, 64, nbuf=4)
                prev = [None, None]
                guard = {}
                guards = [{}, {}]
                xguard = []
                for gi in range(ngr):
                    buf = xb[gi % 2]
                    x2b = x2b2[gi % 2]
                    qst = qst2[gi % 2]
                    guard = guards[gi % 2]
                    oth = guards[(gi + 1) % 2]
                    for key in ("rows", "rowsb", "xn", "sq"):
                        guard[key] = oth.get(key, [])
                    tx = P.dma("sync", buf[:].rearrange("p (c t) -> p c t", c=C),
                               fm_view(xo, 0, D, gi * n, (gi + 1) * n),
                               [prev[gi % 2]] + guard.get("x2", []) + xguard)
                    txc = P.dma("gpsimd",
                                x2b[:].rearrange("p (c t) -> p c t", c=C),
                                fm_view(xo, 0, D, gi * n, (gi + 1) * n),
                                guard.get("x2", []))
                    t2, txn, ng = ln_group(n, buf, tx, x2b, txc, qst, x2b,
                                           xnb, rows, True, guard)
                    guard.update(ng)
                    xguard = t2
                    tspill = P.dma("gpsimd",
                                   fm_view(xn_s, 0, D, gi * n, (gi + 1) * n),
                                   xnb[:].rearrange("p (c t) -> p c t", c=C), txn)
                    guard["xn"] = [tspill]
                    evtoks = []
                    for j in range(C):
                        wp1s(1)
                        bk = banks[5 + (j % 2)]
                        w0 = bk.wwaits() + t2 + [twq]
                        tmm = None
                        for c in range(C):
                            tmm = P.emit("tensor", (lambda e, j=j, c=c, bk=bk,
                                         n=n, x2b=x2b:
                                         e.matmul(bk.ap[:, 0:n],
                                                  wq[:, c * D + j * 128:
                                                     c * D + (j + 1) * 128],
                                                  x2b[:, c * n:(c + 1) * n],
                                                  start=(c == 0), stop=(c == C - 1))),
                                         w0 if c == 0 else (),
                                         "TV" if c == C - 1 else None)
                        bk.set_writer(tmm)
                        tev = P.emit("scalar", (lambda e, j=j, bk=bk, n=n,
                            qst=qst: e.copy(
                            qst[:, j * n:(j + 1) * n], bk.ap[:, 0:n])),
                            bk.rwaits() + [t2[0]], "SV")
                        bk.add_reader(tev)
                        evtoks.append(tev)
                    tq = P.dma("gpsimd", fm_view(qt_s, 0, D, gi * n, (gi + 1) * n),
                               qst[:].rearrange("p (c t) -> p c t", c=C), evtoks)
                    prev[gi % 2] = tq
                    guard["x2"] = [tq]
                    guard["sq"] = guard.get("sq", []) + [tq]
                    guards[gi % 2] = guard
                wp1f()
                P.barrier()

        _phase0()
        def _phase1():
            with ExitStack() as ph:
                n = 512
                ngr = SEQ // n
                xb = [ph.enter_context(nc.sbuf_tensor(f"p2x{i}", [128, C * n], F32))
                      for i in range(2)]
                x2b2 = [ph.enter_context(nc.sbuf_tensor(f"p2x2{i}",
                        [128, C * n], BF16)) for i in range(2)]
                wk = ph.enter_context(nc.sbuf_tensor("p2wk", [128, C * D], BF16))
                kst2 = [ph.enter_context(nc.sbuf_tensor("p2ks",
                        [128, C * n], BF16))] * 2
                rows = {k: ph.enter_context(nc.sbuf_tensor(f"p2r{k}", [1, n], F32))
                        for k in ("mu", "var", "r1", "r12", "tmp", "s2r")}
                for k in ("mub", "r12b", "r1b"):
                    rows[k] = ph.enter_context(
                        nc.sbuf_tensor(f"p2r{k}", [128, n], F32))
                twk = P.dma("gpsimd", wk[:].rearrange("p (c j) -> p c j", c=C),
                            fm_view(qkv, 0, D, D, 2 * D))
                wp2s, wp2f = wp_make(ph, 64, 256, nbuf=4)
                prev = [None, None]
                guard = {}
                guards = [{}, {}]
                for gi in range(ngr):
                    buf = xb[gi % 2]
                    x2b = x2b2[gi % 2]
                    kst = kst2[gi % 2]
                    guard = guards[gi % 2]
                    oth = guards[(gi + 1) % 2]
                    for key in ("rows", "rowsb", "xn", "sq"):
                        guard[key] = oth.get(key, [])
                    tx = P.dma("sync", buf[:].rearrange("p (c t) -> p c t", c=C),
                               fm_view(xs, 0, D, gi * n, (gi + 1) * n),
                               [prev[gi % 2]])
                    txc = P.dma("gpsimd",
                                x2b[:].rearrange("p (c t) -> p c t", c=C),
                                fm_view(xs, 0, D, gi * n, (gi + 1) * n),
                                guard.get("x2", []))
                    t2, _, ng = ln_group(n, buf, tx, x2b, txc, kst, x2b,
                                         None, rows, False, guard)
                    guard.update(ng)
                    tx2 = P.dma("gpsimd",
                                fm_view(xn2_s, 0, D, gi * n, (gi + 1) * n),
                                x2b[:].rearrange("p (c t) -> p c t", c=C), t2)
                    evtoks = []
                    for j in range(C):
                        wp2s(2)
                        bk = banks[5 + (j % 2)]
                        w0 = bk.wwaits() + t2 + [twk]
                        tmm = None
                        for c in range(C):
                            tmm = P.emit("tensor", (lambda e, j=j, c=c, bk=bk,
                                         n=n, x2b=x2b:
                                         e.matmul(bk.ap[:, 0:n],
                                                  wk[:, c * D + j * 128:
                                                     c * D + (j + 1) * 128],
                                                  x2b[:, c * n:(c + 1) * n],
                                                  start=(c == 0), stop=(c == C - 1))),
                                         w0 if c == 0 else (),
                                         "TV" if c == C - 1 else None)
                        bk.set_writer(tmm)
                        tev = P.emit("scalar", (lambda e, j=j, bk=bk, n=n,
                            kst=kst: e.copy(
                            kst[:, j * n:(j + 1) * n], bk.ap[:, 0:n])),
                            bk.rwaits() + [t2[0]], "SV")
                        bk.add_reader(tev)
                        evtoks.append(tev)
                    tk = P.dma("gpsimd", fm_view(kt_s, 0, D, gi * n, (gi + 1) * n),
                               kst[:].rearrange("p (c t) -> p c t", c=C), evtoks)
                    prev[gi % 2] = tk
                    guard["x2"] = [tk, tx2]
                    guard["sq"] = guard.get("sq", []) + [tk]
                    guards[gi % 2] = guard
                wp2f()
                P.barrier()

        _phase1()
        def _phase2():
            with ExitStack() as ph:
                n = 512
                ngr = SEQ // n
                x2b = [ph.enter_context(nc.sbuf_tensor(f"p2bx{i}", [128, C * n],
                                                       BF16)) for i in range(2)]
                wv = ph.enter_context(nc.sbuf_tensor("p2bwv", [128, C * D], BF16))
                vst = ph.enter_context(nc.sbuf_tensor("p2bvs", [128, 4 * D], BF16))
                twv = P.dma("gpsimd", wv[:].rearrange("p (c j) -> p c j", c=C),
                            fm_view(qkv, 0, D, 2 * D, 3 * D))
                prev = [None, None]
                vspill = None
                for gi in range(ngr):
                    buf = x2b[gi % 2]
                    tx = P.dma("sync", buf[:].rearrange("p (c t) -> p c t", c=C),
                               fm_view(xn2_s, 0, D, gi * n, (gi + 1) * n),
                               [prev[gi % 2]])
                    evtoks = []
                    rdtoks = []
                    for ts in range(4):
                        for vn in range(4):
                            bk = banks[(ts * 4 + vn) % 4]
                            w0 = bk.wwaits() + [tx, twv]
                            tmm = None
                            for c in range(C):
                                tmm = P.emit("tensor",
                                             (lambda e, ts=ts, vn=vn, c=c, bk=bk,
                                              buf=buf, n=n:
                                              e.matmul(bk.ap[:, 0:512],
                                                       buf[:, c * n + ts * 128:
                                                          c * n + (ts + 1) * 128],
                                                       wv[:, c * D + vn * 512:
                                                          c * D + (vn + 1) * 512],
                                                       start=(c == 0),
                                                       stop=(c == C - 1))),
                                             w0 if c == 0 else (),
                                             "TV" if c == C - 1 else None)
                            bk.set_writer(tmm)
                            rdtoks.append(tmm)
                            tev = P.emit("scalar", (lambda e, ts=ts, vn=vn, bk=bk:
                                         e.copy(vst[:, ts * D + vn * 512:
                                                    ts * D + (vn + 1) * 512],
                                                bk.ap[:, 0:512])),
                                         bk.rwaits() + ([vspill] if vspill else []),
                                         "SV")
                            bk.add_reader(tev)
                            evtoks.append(tev)
                    vspill = P.dma("gpsimd",
                                   v_s[gi * n:(gi + 1) * n, :]
                                   .rearrange("(s p) d -> s p d", p=128)
                                   .transpose([1, 0, 2]),
                                   vst[:].rearrange("p (s d) -> p s d", s=4),
                                   evtoks)
                    prev[gi % 2] = P.emit("vector", lambda e: e.engine_nop(),
                                          rdtoks, "VV")
                P.barrier()

        _phase2()
        def _phase3():
            with ExitStack() as ph:
                qsb = ph.enter_context(nc.sbuf_tensor("p3q", [128, C * 512], BF16))
                ksb = [ph.enter_context(nc.sbuf_tensor(f"p3k{i}", [128, C * 512],
                                                       BF16)) for i in range(2)]
                vsb = [ph.enter_context(nc.sbuf_tensor(f"p3v{i}", [128, 4 * D],
                                                       BF16)) for i in range(2)]
                oacc = ph.enter_context(nc.sbuf_tensor("p3o", [128, 4 * D], F32))
                mrow = [ph.enter_context(nc.sbuf_tensor(f"p3m{i}", [128, 4], F32))
                        for i in range(2)]
                lrow = ph.enter_context(nc.sbuf_tensor("p3l", [128, 4], F32))
                srow = ph.enter_context(nc.sbuf_tensor("p3s", [128, 4], F32))
                drow = ph.enter_context(nc.sbuf_tensor("p3d", [128, 4], F32))
                nbr = ph.enter_context(nc.sbuf_tensor("p3nb", [128, 4], F32))
                rmax = ph.enter_context(nc.sbuf_tensor("p3rm", [128, 4], F32))
                masks = ph.enter_context(nc.sbuf_tensor("p3msk", [128, 16 * 512],
                                                        F32))
                smsk = ph.enter_context(nc.sbuf_tensor("p3sm", [128, 4 * 512],
                                                       F32))
                pb16 = ph.enter_context(nc.sbuf_tensor("p3p", [128, 4 * 512],
                                                       BF16))
                ptb = ph.enter_context(nc.sbuf_tensor("p3pt", [128, 512], BF16))
                ob16 = ph.enter_context(nc.sbuf_tensor("p3ob", [128, D], BF16))
                otst = ph.enter_context(nc.sbuf_tensor("p3ot", [128, C * 512],
                                                       BF16))
                linv = ph.enter_context(nc.sbuf_tensor("p3li", [128, 4], F32))
                tmsk = P.dma("sync",
                             masks[:].rearrange("p (q t) -> p q t", q=16),
                             msk[:, :].rearrange("(q p) t -> q p t", p=128)
                             .transpose([1, 0, 2]))
                b_sc4 = [banks[0], banks[1], banks[2], banks[3]]
                b_pt = banks[4]
                b_av2 = [banks[5], banks[6]]
                b_fin = banks[7]
                for sti in range(4):
                    kmax = CNT[TILES_ROLE[0][sti * 4 + 3]]
                    tq = P.dma("sync", qsb[:].rearrange("p (c t) -> p c t", c=C),
                               fm_view(qt_s, 0, D, sti * 512, (sti + 1) * 512))
                    oacc_done = {qi: [] for qi in range(4)}
                    prev_tex = {}
                    prev_tl = {}
                    prev_tdx = {}
                    prev_smsk = {}
                    prev_ptm = {}
                    kprev = [None, None]
                    vprev = [None, None]
                    for kt in range(kmax):
                        kb = ksb[kt % 2]
                        vb = vsb[kt % 2]
                        tkl = P.dma("sync", kb[:].rearrange("p (c t) -> p c t", c=C),
                                    fm_view(kt_s, 0, D, kt * 512, (kt + 1) * 512),
                                    [kprev[kt % 2]])
                        tvl = P.dma("sync",
                                    vb[:].rearrange("p (s d) -> p s d", s=4),
                                    v_s[kt * 512:(kt + 1) * 512, :]
                                    .rearrange("(s p) d -> s p d", p=128)
                                    .transpose([1, 0, 2]), [vprev[kt % 2]])
                        kread, vread = [], []
                        active = [qi for qi in range(4)
                                  if kt < CNT[TILES_ROLE[0][sti * 4 + qi]]]
                        texs, tdxs = {}, {}
                        for qi in active:
                            g_cnt = CNT[TILES_ROLE[0][sti * 4 + qi]]
                            first = (kt == 0)
                            diag = (kt == g_cnt - 1)
                            bsc = b_sc4[qi]
                            w0 = bsc.wwaits() + [tq, tkl]
                            tmm = None
                            for c in range(C):
                                tmm = P.emit("tensor",
                                             (lambda e, qi=qi, c=c, bsc=bsc, kb=kb:
                                              e.matmul(bsc.ap[:, 0:512],
                                                       qsb[:, c * 512 + qi * 128:
                                                          c * 512 + (qi + 1) * 128],
                                                       kb[:, c * 512:(c + 1) * 512],
                                                       start=(c == 0),
                                                       stop=(c == C - 1))),
                                             w0 if c == 0 else (),
                                             "TV" if c == C - 1 else None)
                            bsc.set_writer(tmm)
                            kread.append(tmm)
                            if diag:
                                qidx = sti * 4 + qi
                                msl = slice(qi * 512, (qi + 1) * 512)
                                tmadd = P.emit("vector",
                                               (lambda e, qidx=qidx, bsc=bsc,
                                                msl=msl:
                                                e.tensor_add(
                                                    smsk[:, msl], bsc.ap[:, 0:512],
                                                    masks[:, qidx * 512:
                                                          (qidx + 1) * 512])),
                                               bsc.rwaits() + [tmsk,
                                               prev_smsk.get(qi)], "VV")
                                bsc.add_reader(tmadd)
                                sc_src = smsk[:, msl]
                                tprep = [tmadd]
                            else:
                                sc_src = bsc.ap[:, 0:512]
                                tprep = bsc.rwaits()
                            trm = P.emit("vector", (lambda e, qi=qi, s=sc_src:
                                         e.reduce_max(rmax[:, qi:qi + 1], s,
                                                      axis=AX.X)), tprep, "VV")
                            if not diag:
                                bsc.add_reader(trm)
                            mo, mn = mrow[kt % 2], mrow[(kt + 1) % 2]
                            tdx = None
                            if first:
                                tmn = P.emit("vector", (lambda e, qi=qi, mn=mn:
                                             e.tensor_copy(mn[:, qi:qi + 1],
                                                           rmax[:, qi:qi + 1])),
                                             [trm], "VV")
                            else:
                                tmn = P.emit("vector", (lambda e, qi=qi, mo=mo, mn=mn:
                                             e.tensor_max(mn[:, qi:qi + 1],
                                                          mo[:, qi:qi + 1],
                                                          rmax[:, qi:qi + 1])),
                                             [trm], "VV")
                                tda = P.emit("vector", (lambda e, qi=qi, mo=mo, mn=mn:
                                             e.tensor_sub(drow[:, qi:qi + 1],
                                                          mo[:, qi:qi + 1],
                                                          mn[:, qi:qi + 1])),
                                             [tmn, prev_tdx.get(qi)], "VV")
                                tdx = P.emit("scalar", (lambda e, qi=qi:
                                             e.activation(drow[:, qi:qi + 1],
                                                          drow[:, qi:qi + 1],
                                                          ACT.Exp, scale=CS)),
                                             [tda], "SV")
                            tnb = P.emit("vector", (lambda e, qi=qi, mn=mn:
                                         e.tensor_scalar_mul(nbr[:, qi:qi + 1],
                                                             mn[:, qi:qi + 1],
                                                             -CS)),
                                         [tmn, prev_tex.get(qi)], "VV")
                            psl = slice(qi * 512, (qi + 1) * 512)
                            tex = P.emit("scalar", (lambda e, qi=qi, s=sc_src,
                                         psl=psl:
                                         e.activation(pb16[:, psl], s, ACT.Exp,
                                                      bias=nbr[:, qi:qi + 1],
                                                      scale=CS,
                                                      accum_out=srow[:, qi:qi + 1])),
                                         [tnb, prev_tl.get(qi),
                                          prev_ptm.get(qi)] + tprep, "SV")
                            prev_tex[qi] = tex
                            texs[qi] = tex
                            tdxs[qi] = tdx
                            if not diag:
                                bsc.add_reader(tex)
                            if diag:
                                prev_smsk[qi] = tex
                            if first:
                                tl = P.emit("vector", (lambda e, qi=qi:
                                            e.tensor_copy(lrow[:, qi:qi + 1],
                                                          srow[:, qi:qi + 1])),
                                            [tex], "VV")
                            else:
                                tl = P.emit("vector", (lambda e, qi=qi:
                                            e.scalar_tensor_tensor(
                                                lrow[:, qi:qi + 1],
                                                lrow[:, qi:qi + 1],
                                                drow[:, qi:qi + 1],
                                                srow[:, qi:qi + 1],
                                                ALU.mult, ALU.add)),
                                            [tex, tdx], "VV")
                            prev_tl[qi] = tl
                        for qi in active:
                            g_cnt = CNT[TILES_ROLE[0][sti * 4 + qi]]
                            first = (kt == 0)
                            diag = (kt == g_cnt - 1)
                            tdx = tdxs[qi]
                            psl = slice(qi * 512, (qi + 1) * 512)
                            wpt = b_pt.wwaits() + [texs[qi], tib]
                            tptm = None
                            for k4 in range(4):
                                tptm = P.emit("tensor", (lambda e, k4=k4, psl=psl,
                                              qi=qi:
                                              e.matmul(b_pt.ap[:, k4 * 128:
                                                               (k4 + 1) * 128],
                                                       pb16[:, qi * 512 + k4 * 128:
                                                            qi * 512 +
                                                            (k4 + 1) * 128],
                                                       IDB[:], start=True,
                                                       stop=True)),
                                              wpt if k4 == 0 else (),
                                              "TV" if k4 == 3 else None)
                            b_pt.set_writer(tptm)
                            prev_ptm[qi] = tptm
                            tpte = P.emit("scalar", lambda e: e.copy(ptb[:],
                                          b_pt.ap[:]), b_pt.rwaits(), "SV")
                            b_pt.add_reader(tpte)
                            for vn in range(4):
                                bav = b_av2[vn % 2]
                                wav = bav.wwaits() + [tpte, tvl]
                                tav = None
                                for k4 in range(4):
                                    tav = P.emit("tensor",
                                                 (lambda e, vn=vn, k4=k4, bav=bav,
                                                  vb=vb:
                                                  e.matmul(bav.ap[:, 0:512],
                                                           ptb[:, k4 * 128:
                                                               (k4 + 1) * 128],
                                                           vb[:, k4 * D + vn * 512:
                                                              k4 * D +
                                                              (vn + 1) * 512],
                                                           start=(k4 == 0),
                                                           stop=(k4 == 3))),
                                                 wav if k4 == 0 else (),
                                                 "TV" if k4 == 3 else None)
                                bav.set_writer(tav)
                                vread.append(tav)
                                osl = slice(qi * D + vn * 512,
                                            qi * D + (vn + 1) * 512)
                                if first:
                                    tou = P.emit("vector",
                                                 (lambda e, osl=osl, bav=bav:
                                                  e.tensor_copy(oacc[:, osl],
                                                                bav.ap[:, 0:512])),
                                                 bav.rwaits(), "VV")
                                else:
                                    tou = P.emit("vector",
                                                 (lambda e, osl=osl, qi=qi, bav=bav:
                                                  e.scalar_tensor_tensor(
                                                      oacc[:, osl], oacc[:, osl],
                                                      drow[:, qi:qi + 1],
                                                      bav.ap[:, 0:512],
                                                      ALU.mult, ALU.add)),
                                                 bav.rwaits() + [tdx], "VV")
                                bav.add_reader(tou)
                                if diag:
                                    oacc_done[qi].append(tou)
                        kprev[kt % 2] = (P.emit("vector", lambda e: e.engine_nop(),
                                                kread, "VV") if kread else None)
                        vprev[kt % 2] = (P.emit("vector", lambda e: e.engine_nop(),
                                                vread, "VV") if vread else None)
                    evt = []
                    prev_tt = []
                    for qi in range(4):
                        tri = P.emit("vector", (lambda e, qi=qi:
                                     e.reciprocal(linv[:, qi:qi + 1],
                                                  lrow[:, qi:qi + 1])),
                                     oacc_done[qi], "VV")
                        tob = P.emit("scalar", (lambda e, qi=qi:
                                     e.activation(ob16[:],
                                                  oacc[:, qi * D:(qi + 1) * D],
                                                  ACT.Identity,
                                                  scale=linv[:, qi:qi + 1])),
                                     [tri] + oacc_done[qi] + prev_tt, "SV")
                        prev_tt = []
                        for grp in range(4):
                            wfin = b_fin.wwaits() + [tob, tib]
                            tt = None
                            for k4 in range(4):
                                oc = grp * 4 + k4
                                tt = P.emit("tensor", (lambda e, oc=oc, k4=k4:
                                            e.matmul(b_fin.ap[:, k4 * 128:
                                                              (k4 + 1) * 128],
                                                     ob16[:, oc * 128:
                                                          (oc + 1) * 128],
                                                     IDB[:], start=True,
                                                     stop=True)),
                                            wfin if k4 == 0 else (),
                                            "TV" if k4 == 3 else None)
                            b_fin.set_writer(tt)
                            prev_tt.append(tt)
                            dst = otst[:].rearrange("p (c t) -> p c t", c=C)[
                                :, grp * 4:(grp + 1) * 4, qi * 128:(qi + 1) * 128]
                            tte = P.emit("scalar", (lambda e, dst=dst:
                                         e.copy(dst, b_fin.ap[:].rearrange(
                                             "p (c t) -> p c t", c=4))),
                                         b_fin.rwaits(), "SV")
                            b_fin.add_reader(tte)
                            evt.append(tte)
                    P.dma("gpsimd", fm_view(ot_s, 0, D, sti * 512, (sti + 1) * 512),
                          otst[:].rearrange("p (c t) -> p c t", c=C), evt)
                    P.barrier()

        _phase3()
        def weight_prep(ph, wdram, nrows, ncol, wq_t, widx):
            ntil = nrows // 128
            nch = ncol // 128
            NW = 4 if ncol <= 2048 else 2
            wt2 = [ph.enter_context(nc.sbuf_tensor(f"wp{widx}w{i}",
                   [128, ncol], F32)) for i in range(NW)]
            ab = None
            s1b = ph.enter_context(nc.sbuf_tensor(f"wp{widx}s1", [128, ncol],
                                                  F32))
            s2b = ph.enter_context(nc.sbuf_tensor(f"wp{widx}s2", [128, ncol],
                                                  F32))
            trn = ph.enter_context(nc.sbuf_tensor(f"wp{widx}t", [128, ncol],
                                                  BF16))
            rsum = ph.enter_context(nc.sbuf_tensor(f"wp{widx}r", [1, 1], F32))
            invc = ph.enter_context(nc.sbuf_tensor(f"wp{widx}i", [128, 1],
                                                   F32))
            accb = ph.enter_context(nc.sbuf_tensor(f"wp{widx}ac",
                                                    [128, 128], F32))
            t1 = ph.enter_context(nc.sbuf_tensor(f"wp{widx}t1", [128, 1], F32))
            tld = P.dma("sync", accb[:],
                        acc_s[:, widx * 128:(widx + 1) * 128])
            bsum = banks[0]
            wsum = bsum.wwaits()
            tr1 = P.emit("vector", (lambda e: e.reduce_sum(
                t1[:], accb[:, 0:128], axis=AX.X)), [tld], "VV")
            tsl = P.emit("tensor", (lambda e: e.matmul(
                bsum.ap[0:1, 0:1], t1[:], ONC[:], start=True, stop=True)),
                wsum + [tr1, tonc], "TV")
            bsum.set_writer(tsl)
            trs = P.emit("vector", lambda e: e.tensor_copy(
                rsum[:], bsum.ap[0:1, 0:1]), bsum.rwaits(), "VV")
            bsum.add_reader(trs)
            tms = P.emit("vector", (lambda e: e.tensor_scalar(
                wrow[:, 2 * widx:2 * widx + 1], rsum[:], 1.0 / (1 << 23), EPS,
                ALU.mult, ALU.max)), [trs], "VV")
            tiv = P.emit("vector", (lambda e: e.reciprocal(
                wrow[:, 2 * widx + 1:2 * widx + 2],
                wrow[:, 2 * widx:2 * widx + 1])), [tms], "VV")
            tkr = P.emit("vector", (lambda e: e.tensor_scalar_mul(
                rsum[:], wrow[:, 2 * widx:2 * widx + 1], 0.5 / QB)),
                [tms, trs], "VV")
            binv = banks[1]
            wbi = binv.wwaits()
            tbi = P.emit("tensor", (lambda e: e.matmul(
                binv.ap[:, 0:1], ONR[:], wrow[:, 2 * widx + 1:2 * widx + 2],
                start=True, stop=True)), wbi + [tiv, tonr], "TV")
            tbk = P.emit("tensor", (lambda e: e.matmul(
                binv.ap[:, 1:2], ONR[:], rsum[:], start=True, stop=True)),
                [tkr], "TV")
            binv.set_writer(tbk)
            tin = P.emit("vector", lambda e: e.tensor_copy(invc[:],
                         binv.ap[:, 0:1]), [tbi], "VV")
            tkc = P.emit("vector", (lambda e: e.tensor_copy(
                WCOL[:, widx:widx + 1], binv.ap[:, 1:2])), [tbk], "VV")
            binv.add_reader(tin)
            binv.add_reader(tkc)
            wq_toks = []
            tprev2 = None
            tread2 = {}
            for ti in range(ntil):
                wt = wt2[ti % NW]
                eng = "sync" if ti % 2 == 0 else "gpsimd"
                tw = P.dma(eng, wt[:], wdram[ti * 128:(ti + 1) * 128, :],
                           [tread2.get(ti - NW, tin)])
                ts1 = P.emit("scalar", (lambda e, wt=wt: e.activation(
                    s1b[:], wt[:], ACT.Sign, bias=-0.5, scale=invc[:, 0:1])),
                    [tw, tin, tc, tprev2], "SV")
                ts2 = P.emit("scalar", (lambda e, wt=wt: e.activation(
                    s2b[:], wt[:], ACT.Sign, bias=0.5, scale=invc[:, 0:1])),
                    [ts1], "SV")
                tprev2 = ts2
                tread2[ti] = ts2
                tt3 = P.emit("vector", (lambda e: e.tensor_add(
                    trn[:], s1b[:], s2b[:])), [ts2], "VV")
                for grp in range((nch + 3) // 4):
                    bkt = banks[2 + (grp % 2)]
                    wt_ = bkt.wwaits() + [tt3, tib]
                    k4n = min(4, nch - grp * 4)
                    tt = None
                    for k4 in range(k4n):
                        cc = grp * 4 + k4
                        tt = P.emit("tensor", (lambda e, cc=cc, k4=k4, bkt=bkt:
                                    e.matmul(bkt.ap[:, k4 * 128:(k4 + 1) * 128],
                                             trn[:, cc * 128:(cc + 1) * 128],
                                             IDB[:], start=True, stop=True)),
                                    wt_ if k4 == 0 else (),
                                    "TV" if k4 == k4n - 1 else None)
                    bkt.set_writer(tt)
                    dst = wq_t[:].rearrange("p (c f) -> p c f", c=nch)[
                        :, grp * 4:grp * 4 + k4n, ti * 128:(ti + 1) * 128]
                    tte = P.emit("scalar", (lambda e, dst=dst, bkt=bkt,
                                 k4n=k4n: e.copy(dst,
                                     bkt.ap[:, 0:k4n * 128].rearrange(
                                         "p (c t) -> p c t", c=k4n))),
                                 bkt.rwaits(), "SV")
                    bkt.add_reader(tte)
                    wq_toks.append(tte)
            return wq_toks

        def quant_rounds(tag, nch, n, ubuf, utoks, qout, ftm, mstore,
                         qint, n0b, rbuf, s2b2, sc_r, sc_q, qguard,
                         rbuf2=None):
            nt = n // 128
            if rbuf2 is None:
                rbuf2 = rbuf
            cmul = C1 if tag == 1 else C2
            t = utoks
            tmax = P.emit("vector", (lambda e: e.tensor_copy(rbuf[:, 0:n],
                         ubuf[:, 0:n])), t + qguard, "VV")
            tmin = P.emit("vector", (lambda e: e.tensor_copy(rbuf[:, n:2 * n],
                         ubuf[:, 0:n])), t, "VV")
            for cc2 in range(1, nch):
                tmax = P.emit("vector", (lambda e, cc2=cc2: e.tensor_max(
                    rbuf[:, 0:n], rbuf[:, 0:n],
                    ubuf[:, cc2 * n:(cc2 + 1) * n])), [tmax], "VV")
                tmin = P.emit("vector", (lambda e, cc2=cc2: e.tensor_tensor(
                    rbuf[:, n:2 * n], rbuf[:, n:2 * n],
                    ubuf[:, cc2 * n:(cc2 + 1) * n], ALU.min)), [tmin], "VV")
            tam = P.emit("vector", (lambda e: e.scalar_tensor_tensor(
                rbuf[:, 0:n], rbuf[:, n:2 * n], -1.0, rbuf[:, 0:n],
                ALU.mult, ALU.max)), [tmax, tmin], "VV")
            btr = banks[5]
            wtr = btr.wwaits() + [tam, tif]
            tt = None
            for k4 in range(nt):
                tt = P.emit("tensor", (lambda e, k4=k4: e.matmul(
                    btr.ap[:, k4 * 128:(k4 + 1) * 128],
                    rbuf[:, k4 * 128:(k4 + 1) * 128], IDF[:],
                    start=True, stop=True)), wtr if k4 == 0 else (),
                    "TV" if k4 == nt - 1 else None)
            btr.set_writer(tt)
            ttm = []
            for k4 in range(nt):
                tr = P.emit("vector", (lambda e, k4=k4: e.reduce_max(
                    sc_r[:, k4:k4 + 1], btr.ap[:, k4 * 128:(k4 + 1) * 128],
                    axis=AX.X)), btr.rwaits(), "VV")
                btr.add_reader(tr)
                ttm.append(tr)
            if ftm is not None:
                tsc = P.emit("vector", (lambda e: e.tensor_mul(
                    sc_r[:, 0:nt], sc_r[:, 0:nt], ftm)), ttm, "VV")
            else:
                tsc = ttm[-1]
            tm_ = P.emit("vector", (lambda e: e.tensor_scalar(
                sc_r[:, 0:nt], sc_r[:, 0:nt], cmul, EPS, ALU.mult, ALU.max)),
                [tsc] + ttm, "VV")
            tok_m = None
            if mstore is not None:
                tok_m = P.emit("vector", (lambda e: e.tensor_copy(
                    mstore, sc_r[:, 0:nt])), [tm_], "VV")
            tqm = P.emit("vector", (lambda e: e.reciprocal(
                sc_q[:, 0:nt], sc_r[:, 0:nt])), [tm_], "VV")
            if ftm is not None:
                tqm = P.emit("vector", (lambda e: e.tensor_mul(
                    sc_q[:, 0:nt], sc_q[:, 0:nt], ftm)), [tqm], "VV")
            tqm = P.emit("vector", (lambda e: e.tensor_scalar_mul(
                sc_q[:, 0:nt], sc_q[:, 0:nt], QB * cmul)), [tqm], "VV")
            brow = banks[5]
            wrw = brow.wwaits() + [tqm, tif]
            tt = None
            for k4 in range(nt):
                tt = P.emit("tensor", (lambda e, k4=k4: e.matmul(
                    brow.ap[0:1, k4 * 128:(k4 + 1) * 128],
                    sc_q[:, k4:k4 + 1], IDF[:], start=True, stop=True)),
                    wrw if k4 == 0 else (), "TV" if k4 == nt - 1 else None)
            brow.set_writer(tt)
            trow = P.emit("vector", (lambda e: e.tensor_copy(
                rbuf[0:1, 0:n], brow.ap[0:1, 0:n])), brow.rwaits(), "VV")
            brow.add_reader(trow)
            brep = banks[6]
            wrep = brep.wwaits() + [trow, tonr]
            trep = P.emit("tensor", (lambda e: e.matmul(
                brep.ap[:, 0:n], ONR[:], rbuf[0:1, 0:n],
                start=True, stop=True)), wrep, "TV")
            brep.set_writer(trep)
            trepc = P.emit("vector", (lambda e: e.tensor_copy(
                rbuf[:, n:2 * n], brep.ap[:, 0:n])), brep.rwaits(), "VV")
            brep.add_reader(trepc)
            qtoks = []
            nb = min(nch, qint.shape[1] // n)
            for cb in range(0, nch, nb):
                nbb = min(nb, nch - cb)
                for cc2 in range(cb, cb + nbb):
                    sl = slice(cc2 * n, (cc2 + 1) * n)
                    P.emit("vector", (lambda e, sl=sl: e.tensor_mul(
                        ubuf[:, sl], ubuf[:, sl], rbuf[:, n:2 * n])),
                        [trepc], "VV")
                bsl = slice(cb * n, (cb + nbb) * n)
                m = nbb * n
                P.emit("vector", (lambda e, bsl=bsl, m=m: e.tensor_copy(
                    qint[:, 0:m], ubuf[:, bsl])), (), "VV")
                P.emit("vector", (lambda e, bsl=bsl, m=m:
                    e.scalar_tensor_tensor(
                        n0b[:, 0:m], qint[:, 0:m], 0.5, ubuf[:, bsl],
                        ALU.add, ALU.is_le)), (), "VV")
                P.emit("vector", (lambda e, bsl=bsl, m=m:
                    e.scalar_tensor_tensor(
                        s2b2[:, 0:m], qint[:, 0:m], -0.5, ubuf[:, bsl],
                        ALU.add, ALU.is_ge)), (), "VV")
                P.emit("vector", (lambda e, m=m: e.tensor_sub(
                    n0b[:, 0:m], n0b[:, 0:m], s2b2[:, 0:m])), (), "VV")
                tqf = P.emit("vector", (lambda e, bsl=bsl, m=m: e.tensor_tensor(
                    qout[:, bsl], qint[:, 0:m], n0b[:, 0:m], ALU.add)),
                    (), "VV")
                qtoks.append(tqf)
            return qtoks, tok_m

        def fwht_stages(nch, n, a, b, toks):
            nst = int(math.log2(nch))
            cur, oth = a, b
            t = list(toks)
            for stg in range(nst):
                s = 1 << stg
                nblk = nch // (2 * s)
                av = cur[:].rearrange("p (blk two s t) -> p blk two s t",
                                      blk=nblk, two=2, s=s)
                ov = oth[:].rearrange("p (blk two s t) -> p blk two s t",
                                      blk=nblk, two=2, s=s)
                gb = 0
                vb = nblk - gb
                tt = []
                tt.append(P.emit("vector", (lambda e, av=av, ov=ov, vb=vb:
                          e.tensor_add(ov[:, 0:vb, 0], av[:, 0:vb, 0],
                                       av[:, 0:vb, 1])), t, "VV"))
                tt.append(P.emit("vector", (lambda e, av=av, ov=ov, vb=vb:
                          e.tensor_sub(ov[:, 0:vb, 1], av[:, 0:vb, 0],
                                       av[:, 0:vb, 1])), t, "VV"))
                if gb:
                    tt.append(P.emit("gpsimd", (lambda e, av=av, ov=ov, vb=vb,
                              nblk=nblk: e.tensor_add(
                                  ov[:, vb:nblk, 0], av[:, vb:nblk, 0],
                                  av[:, vb:nblk, 1])), t, "GC"))
                    tt.append(P.emit("gpsimd", (lambda e, av=av, ov=ov, vb=vb,
                              nblk=nblk: e.tensor_sub(
                                  ov[:, vb:nblk, 1], av[:, vb:nblk, 0],
                                  av[:, vb:nblk, 1])), t, "GC"))
                t = tt
                cur, oth = oth, cur
            return cur, oth, t

        def _phase4():
            with ExitStack() as ph:
                wqt = ph.enter_context(nc.sbuf_tensor("p4wqt", [128, C * FI],
                                                      BF16))
                with ExitStack() as pp:
                    weight_prep(pp, wup, FI, D, wqt, 0)
                    P.barrier()
                n = 256
                ngr = OWN // n
                xnb = ph.enter_context(nc.sbuf_tensor("p4xn", [128, C * n], F32))
                pong = ph.enter_context(nc.sbuf_tensor("p4pg", [128, C * n], F32))
                s2b2 = ph.enter_context(nc.sbuf_tensor("p4s2", [128, n], F32))
                n0b = ph.enter_context(nc.sbuf_tensor("p4n0", [128, n], F32))
                rbuf = ph.enter_context(nc.sbuf_tensor("p4rb", [128, 2 * n], F32))
                qint = ph.enter_context(nc.sbuf_tensor("p4qi", [128, n], I32))
                xqb = ph.enter_context(nc.sbuf_tensor("p4xq", [128, C * n], BF16))
                hst2 = ph.enter_context(nc.sbuf_tensor("p4h2", [128, FC * n], F32))
                hst = n0b
                sc_r = ph.enter_context(nc.sbuf_tensor("p4sr", [128, 2], F32))
                sc_q = ph.enter_context(nc.sbuf_tensor("p4sq", [128, 2], F32))
                prevx = None
                prevh = None
                for gi in range(ngr):
                    tx = P.dma("sync", xnb[:].rearrange("p (c t) -> p c t", c=C),
                               fm_view(xn_s, 0, D, gi * n, (gi + 1) * n), [prevx])
                    fb, ob, ft = fwht_stages(C, n, xnb, pong, [tx])
                    utoks = []
                    for cc in range(C):
                        bk = banks[cc % 4]
                        w0 = bk.wwaits() + ft + [th]
                        tmm = P.emit("tensor", (lambda e, cc=cc, fb=fb, bk=bk, n=n:
                                     e.matmul(bk.ap[:, 0:n], H128[:],
                                              fb[:, cc * n:(cc + 1) * n],
                                              start=True, stop=True)), w0, "TV")
                        bk.set_writer(tmm)
                        te = P.emit("scalar", (lambda e, cc=cc, ob=ob, bk=bk, n=n:
                                    e.copy(ob[:, cc * n:(cc + 1) * n],
                                           bk.ap[:, 0:n])),
                                    bk.rwaits(), "SV")
                        bk.add_reader(te)
                        utoks.append(te)
                    qtoks, _ = quant_rounds(1, C, n, ob, [utoks[-1]], xqb, None,
                                            None, qint, n0b, rbuf, s2b2,
                                            sc_r, sc_q, [prevh] if prevh else [])
                    tf = P.emit("vector", (lambda e, gi=gi: e.tensor_scalar(
                        f2tm[:, 2 * gi:2 * gi + 2], sc_r[:, 0:2],
                        WCOL[:, 0:1], None, ALU.mult)), [qtoks[0]], "VV")
                    tf2 = P.emit("vector", (lambda e, gi=gi: e.tensor_mul(
                        f2tm[:, 2 * gi:2 * gi + 2], f2tm[:, 2 * gi:2 * gi + 2],
                        f2tm[:, 2 * gi:2 * gi + 2])), [tf], "VV")
                    sqtoks = []
                    for j in range(FC):
                        bk = banks[4 + (j % 2)]
                        w0 = bk.wwaits() + [qtoks[-1]]
                        tmm = None
                        for c in range(C):
                            tmm = P.emit("tensor", (lambda e, j=j, c=c, bk=bk, n=n:
                                         e.matmul(bk.ap[:, 0:n],
                                                  wqt[:, c * FI + j * 128:
                                                     c * FI + (j + 1) * 128],
                                                  xqb[:, c * n:(c + 1) * n],
                                                  start=(c == 0),
                                                  stop=(c == C - 1))),
                                         w0 if c == 0 else (),
                                         "TV" if c == C - 1 else None)
                        bk.set_writer(tmm)
                        trl = P.emit("scalar", (lambda e, bk=bk, j=j, n=n:
                                     e.activation(hst2[:, j * n:(j + 1) * n],
                                                  bk.ap[:, 0:n], ACT.Relu)),
                            bk.rwaits() + [prevh], "SV")
                        bk.add_reader(trl)
                        tsq2 = P.emit("vector", (lambda e, j=j, n=n: e.tensor_mul(
                            hst2[:, j * n:(j + 1) * n],
                            hst2[:, j * n:(j + 1) * n],
                            hst2[:, j * n:(j + 1) * n])),
                            [trl], "VV")
                        sqtoks.append(tsq2)
                    prevh = P.dma("gpsimd", fm_view(h_s, 0, FI, gi * n,
                                                    (gi + 1) * n),
                                  hst2[:].rearrange("p (c t) -> p c t", c=FC),
                                  sqtoks)
                    prevx = prevh
                P.barrier()

        _phase4()
        def _phase5():
            with ExitStack() as ph:
                n = 512
                NB = 4
                ngr = OWN // n
                hb = ph.enter_context(nc.sbuf_tensor("p5h", [128, FC * n], F32))
                pong = ph.enter_context(nc.sbuf_tensor("p5pg", [128, FC * n], F32))
                s2b2 = ph.enter_context(nc.sbuf_tensor("p5s2", [128, NB * n],
                                                       F32))
                n0b = ph.enter_context(nc.sbuf_tensor("p5n0", [128, NB * n],
                                                      F32))
                rbuf = ph.enter_context(nc.sbuf_tensor("p5rb", [128, 2 * n], F32))
                rbuf2 = ph.enter_context(nc.sbuf_tensor("p5rb2", [128, NB * n],
                                                        F32))
                qint = ph.enter_context(nc.sbuf_tensor("p5qi", [128, NB * n],
                                                       I32))
                hqb = ph.enter_context(nc.sbuf_tensor("p5hq", [128, FC * n], BF16))
                sc_r = ph.enter_context(nc.sbuf_tensor("p5sr", [128, 4], F32))
                sc_q = ph.enter_context(nc.sbuf_tensor("p5sq", [128, 4], F32))
                prevx = None
                prevq = None
                for gi in range(ngr):
                    tx = P.dma("sync", hb[:].rearrange("p (c t) -> p c t", c=FC),
                               fm_view(h_s, 0, FI, gi * n, (gi + 1) * n), [prevx])
                    fb, ob, ft = fwht_stages(FC, n, hb, pong, [tx])
                    utoks = []
                    for cc in range(FC):
                        bk = banks[cc % 4]
                        w0 = bk.wwaits() + ft + [th]
                        tmm = P.emit("tensor", (lambda e, cc=cc, fb=fb, bk=bk, n=n:
                                     e.matmul(bk.ap[:, 0:n], H128[:],
                                              fb[:, cc * n:(cc + 1) * n],
                                              start=True, stop=True)), w0, "TV")
                        bk.set_writer(tmm)
                        te = P.emit("scalar", (lambda e, cc=cc, ob=ob, bk=bk, n=n:
                                    e.copy(ob[:, cc * n:(cc + 1) * n],
                                           bk.ap[:, 0:n])),
                                    bk.rwaits(), "SV")
                        bk.add_reader(te)
                        utoks.append(te)
                    nt = n // 128
                    qtoks, _ = quant_rounds(2, FC, n, ob, [utoks[-1]], hqb,
                                            f2tm[:, nt * gi:nt * gi + nt],
                                            m2tm[:, nt * gi:nt * gi + nt],
                                            qint, n0b, rbuf, s2b2, sc_r, sc_q,
                                            list(prevq or []), rbuf2)
                    sp_toks = []
                    for bi, cb in enumerate(range(0, FC, 4)):
                        tsp = P.dma("gpsimd",
                                    fm_view(hq_s, cb * 128, (cb + 4) * 128,
                                            gi * n, (gi + 1) * n),
                                    hqb[:, cb * n:(cb + 4) * n].rearrange(
                                        "p (c t) -> p c t", c=4),
                                    [qtoks[bi]])
                        sp_toks.append(tsp)
                    prevq = sp_toks
                    prevx = sp_toks[-1]
                P.barrier()

        _phase5()
        def _phase6():
            with ExitStack() as ph:
                wqd = ph.enter_context(nc.sbuf_tensor("p6wqd", [128, FC * D],
                                                      BF16))
                with ExitStack() as pp:
                    weight_prep(pp, wdn, D, FI, wqd, 1)
                    P.barrier()
                n = 512
                ngr = OWN // n
                nt = n // 128
                hqb = ph.enter_context(nc.sbuf_tensor("p6hq", [128, FC * n],
                                                      BF16))
                fdr = ph.enter_context(nc.sbuf_tensor("p6fd", [128, nt], F32))
                fdrep = ph.enter_context(nc.sbuf_tensor("p6fe", [128, n],
                                                        F32))
                frow = ph.enter_context(nc.sbuf_tensor("p6fr", [1, n], F32))
                o2b = ph.enter_context(nc.sbuf_tensor("p6o2", [128, 2 * n],
                                                       F32))
                gst = ph.enter_context(nc.sbuf_tensor("p6g", [128, C * n], F32))
                prevx = None
                prevg = None
                for gi in range(ngr):
                    hc = FC // 2
                    tx = P.dma("sync",
                               hqb[:, 0:hc * n].rearrange("p (c t) -> p c t",
                                                          c=hc),
                               fm_view(hq_s, 0, hc * 128, gi * n, (gi + 1) * n),
                               [prevx])
                    tx1 = P.dma("gpsimd",
                                hqb[:, hc * n:FC * n].rearrange(
                                    "p (c t) -> p c t", c=hc),
                                fm_view(hq_s, hc * 128, FI, gi * n,
                                        (gi + 1) * n), [prevx])
                    tfd = P.emit("vector", (lambda e, gi=gi, nt=nt:
                        e.tensor_scalar(
                        fdr[:, 0:nt], m2tm[:, nt * gi:nt * gi + nt],
                        WCOL[:, 1:2],
                        None, ALU.mult)), [prevg], "VV")
                    btr = banks[5]
                    wtr = btr.wwaits() + [tfd, tif]
                    tt = None
                    for k4 in range(nt):
                        tt = P.emit("tensor", (lambda e, k4=k4: e.matmul(
                            btr.ap[0:1, k4 * 128:(k4 + 1) * 128],
                            fdr[:, k4:k4 + 1], IDF[:], start=True, stop=True)),
                            wtr if k4 == 0 else (),
                            "TV" if k4 == nt - 1 else None)
                    btr.set_writer(tt)
                    trw = P.emit("vector", (lambda e: e.tensor_copy(
                        frow[:], btr.ap[0:1, 0:n])), btr.rwaits(), "VV")
                    btr.add_reader(trw)
                    brep = banks[6]
                    wrep = brep.wwaits() + [trw, tonr]
                    trep = P.emit("tensor", (lambda e: e.matmul(
                        brep.ap[:, 0:n], ONR[:], frow[:], start=True, stop=True)),
                        wrep, "TV")
                    brep.set_writer(trep)
                    tfde = P.emit("vector", (lambda e: e.tensor_copy(
                        fdrep[:, 0:n], brep.ap[:, 0:n])),
                        brep.rwaits() + [prevg], "VV")
                    brep.add_reader(tfde)
                    gtoks = []
                    prev_sil = [None, None]
                    for j in range(C):
                        bk = banks[(j % 2)]
                        w0 = bk.wwaits() + [tx]
                        tmm = None
                        for c in range(FC):
                            cw = w0 if c == 0 else ([tx1] if c == FC // 2
                                                    else ())
                            tmm = P.emit("tensor", (lambda e, j=j, c=c, bk=bk, n=n:
                                         e.matmul(bk.ap[:, 0:n],
                                                  wqd[:, c * D + j * 128:
                                                     c * D + (j + 1) * 128],
                                                  hqb[:, c * n:(c + 1) * n],
                                                  start=(c == 0),
                                                  stop=(c == FC - 1))),
                                         cw,
                                         "TV" if c == FC - 1 else None)
                        bk.set_writer(tmm)
                        osl = slice((j % 2) * n, (j % 2) * n + n)
                        tsc2 = P.emit("vector", (lambda e, bk=bk, osl=osl, n=n:
                                      e.tensor_mul(o2b[:, osl], bk.ap[:, 0:n],
                                                   fdrep[:, 0:n])),
                                      bk.rwaits() + [tfde, prev_sil[j % 2]],
                                      "VV")
                        bk.add_reader(tsc2)
                        tsil = P.emit("scalar", (lambda e, j=j, osl=osl, n=n:
                                      e.activation(gst[:, j * n:(j + 1) * n],
                                                   o2b[:, osl], ACT.Silu)),
                            [tsc2, prevg], "SV")
                        prev_sil[j % 2] = tsil
                        gtoks.append(tsil)
                    prevg = P.dma("gpsimd", fm_view(g_s, 0, D, gi * n,
                                                    (gi + 1) * n),
                                  gst[:].rearrange("p (c t) -> p c t", c=C),
                                  gtoks)
                    prevx = prevg
                P.barrier()

        _phase6()
        def _phase7():
            with ExitStack() as ph:
                n = 512
                ngr = OWN // n
                wob = ph.enter_context(nc.sbuf_tensor("p7wo", [128, C * D], BF16))
                otb = ph.enter_context(nc.sbuf_tensor("p7ot", [128, C * n], BF16))
                xnb = ph.enter_context(nc.sbuf_tensor("p7xn", [128, C * n], F32))
                gb = ph.enter_context(nc.sbuf_tensor("p7g", [128, C * n], F32))
                xb = ph.enter_context(nc.sbuf_tensor("p7x", [128, C * n], F32))
                two = P.dma("gpsimd", wob[:].rearrange("p (c j) -> p c j", c=C),
                            fm_view(wo_d, 0, D, 0, D))
                prevo = prevn = prevgl = prevxl = None
                for gi in range(ngr):
                    tot = P.dma("sync", otb[:].rearrange("p (c t) -> p c t", c=C),
                                fm_view(ot_s, 0, D, gi * n, (gi + 1) * n), [prevo])
                    txn = P.dma("sync", xnb[:].rearrange("p (c t) -> p c t", c=C),
                                fm_view(xn_s, 0, D, gi * n, (gi + 1) * n), [prevn])
                    tgl = P.dma("sync", gb[:].rearrange("p (c t) -> p c t", c=C),
                                fm_view(g_s, 0, D, gi * n, (gi + 1) * n), [prevgl])
                    txl = P.dma("sync", xb[:].rearrange("p (c t) -> p c t", c=C),
                                fm_view(xo, 0, D, gi * n, (gi + 1) * n), [prevxl])
                    otoks = []
                    mmtoks = []
                    for j in range(C):
                        bk = banks[(j % 2)]
                        w0 = bk.wwaits() + [tot, two]
                        tmm = None
                        for c in range(C):
                            tmm = P.emit("tensor", (lambda e, j=j, c=c, bk=bk, n=n:
                                         e.matmul(bk.ap[:, 0:n],
                                                  wob[:, c * D + j * 128:
                                                     c * D + (j + 1) * 128],
                                                  otb[:, c * n:(c + 1) * n],
                                                  start=(c == 0),
                                                  stop=(c == C - 1))),
                                         w0 if c == 0 else (),
                                         "TV" if c == C - 1 else None)
                        bk.set_writer(tmm)
                        mmtoks.append(tmm)
                        sl = slice(j * n, (j + 1) * n)
                        ts_ = P.emit("vector", (lambda e, sl=sl, bk=bk, n=n:
                                     e.tensor_add(xnb[:, sl], bk.ap[:, 0:n],
                                                  xnb[:, sl])),
                                     bk.rwaits() + [txn], "VV")
                        bk.add_reader(ts_)
                        tsg = P.emit("vector", (lambda e, sl=sl: e.tensor_mul(
                            gb[:, sl], xnb[:, sl], gb[:, sl])), [ts_, tgl], "VV")
                        tout = P.emit("vector", (lambda e, sl=sl: e.tensor_add(
                            xb[:, sl], xb[:, sl], gb[:, sl])), [tsg, txl], "VV")
                        otoks.append(tout)
                    tso = P.dma_out("sync", fm_view(out_d, 0, D, gi * n,
                                                    (gi + 1) * n),
                                    xb[:].rearrange("p (c t) -> p c t", c=C),
                                    otoks)
                    prevo = prevn = prevgl = prevxl = tso
                P.barrier()

        _phase7()
        if dbg:
            for name in dbg:
                src = scr[name]
                dst = dbg_outs[name]
                nr = src.shape[0]
                for rr in range(0, nr, 512):
                    P.dma_out("gpsimd", dst[rr:rr + 512, :],
                              src[rr:rr + 512, :])
            P.barrier()

        P.finalize(blk)
    return nc


_CACHE = {}


def _host_inputs(L0_attn_x, L0_attn_qkv, L0_attn_o_proj, L0_attn_gamma,
                 L0_attn_beta, L0_ffn_w_up, L0_ffn_w_down):
    x = np.asarray(L0_attn_x, dtype=np.float32)
    qkv = np.ascontiguousarray(np.asarray(L0_attn_qkv, dtype=np.float32))
    wo = np.ascontiguousarray(np.asarray(L0_attn_o_proj, dtype=np.float32))
    wup = np.ascontiguousarray(np.asarray(L0_ffn_w_up, dtype=np.float32))
    wdn = np.ascontiguousarray(np.asarray(L0_ffn_w_down, dtype=np.float32))
    gamC = np.ascontiguousarray(
        np.asarray(L0_attn_gamma, np.float32).reshape(C, 128).T)
    betC = np.ascontiguousarray(
        np.asarray(L0_attn_beta, np.float32).reshape(C, 128).T)
    ii = np.arange(128)
    pc = np.zeros((128, 128), dtype=np.int64)
    a = ii[:, None] & ii[None, :]
    for b in range(7):
        pc += (a >> b) & 1
    h128 = (1.0 - 2.0 * (pc & 1)).astype(np.float32)
    identf = np.eye(128, dtype=np.float32)
    identb = np.eye(128).astype(ml_dtypes.bfloat16)
    consts = np.zeros((128, 8), dtype=np.float32)
    consts[:, 0] = -0.5
    consts[:, 1] = 0.5
    consts[:, 2] = EPS
    consts[:, 3] = 1.0
    ones_row = np.ones((1, 128), dtype=np.float32)
    shared = dict(qkv=qkv, o_proj=wo, w_up=wup, w_down=wdn, gammaC=gamC,
                  betaC=betC, h128f=h128, identf=identf, identb=identb,
                  consts=consts, ones_row=ones_row)
    in_maps = []
    metas = []
    for core in range(8):
        b = core // 2
        role = core % 2
        tiles = TILES_ROLE[role]
        rows = np.concatenate([np.arange(g * 128, (g + 1) * 128)
                               for g in tiles])
        x_seq = np.ascontiguousarray(x[b].T)
        x_own = np.ascontiguousarray(x[b][rows].T)
        masks = np.zeros((16, 128, 512), dtype=np.float32)
        for i, g in enumerate(tiles):
            cgt = CNT[g]
            k0 = (cgt - 1) * 512
            q_glob = g * 128 + np.arange(128)[:, None]
            k_glob = k0 + np.arange(512)[None, :]
            masks[i] = np.where(k_glob <= q_glob, 0.0, NEG)
        m = dict(shared)
        m["x_seq"] = x_seq
        m["x_own"] = x_own
        m["masks"] = np.ascontiguousarray(masks.reshape(16 * 128, 512))
        in_maps.append(m)
        metas.append((b, rows))
    return in_maps, metas


def kernel(**inputs):
    if "nc" not in _CACHE:
        _CACHE["nc"] = build_program()
    nc = _CACHE["nc"]
    in_maps, metas = _host_inputs(**inputs)
    res = run_bass_kernel_spmd(nc, in_maps, core_ids=list(range(8)))
    out = np.empty((4, SEQ, D), dtype=np.float32)
    for core in range(8):
        b, rows = metas[core]
        out[b][rows] = np.ascontiguousarray(res.results[core]["outT"]).T
    return out


# version 26
# speedup vs baseline: 1.0473x; 1.0191x over previous
import math
import numpy as np
import ml_dtypes
import concourse.bass as bass
import concourse.mybir as mybir
from concourse.bass_utils import run_bass_kernel_spmd
from contextlib import ExitStack

F32 = mybir.dt.float32
BF16 = mybir.dt.bfloat16
I32 = mybir.dt.int32
ALU = mybir.AluOpType
ACT = mybir.ActivationFunctionType
AX = mybir.AxisListType

D = 2048
C = 16
SEQ = 4096
OWN = 2048
FI = 4096
FC = 32
EPS = 1e-5
C1 = 1.0 / math.sqrt(2048.0)
C2 = 1.0 / 64.0
CS = 1.0 / math.sqrt(2048.0)
QB = 127.0
NEG = -1.0e12

TILES_ROLE = [
    [0, 1, 4, 5, 8, 9, 12, 13, 16, 17, 20, 21, 24, 25, 28, 29],
    [2, 3, 6, 7, 10, 11, 14, 15, 18, 19, 22, 23, 26, 27, 30, 31],
]
CNT = [(g + 4) // 4 for g in range(32)]

ENGS = ("sync", "gpsimd", "vector", "scalar", "tensor")
OWN_SEM = {"sync": "LD", "gpsimd": "GD", "vector": "VV", "scalar": "SV",
           "tensor": "TV"}
SKIP_SELF = set()


class Prog:
    NCHAIN = 6

    def __init__(self, nc, stack):
        self.nc = nc
        self.items = {e: [] for e in ENGS}
        self.sems = {}
        self.cnt = {}
        names = ["TV", "VV", "SV", "GC", "BS", "BG", "BV", "BC", "BT"]
        for q in ("LD", "GD"):
            for r in range(self.NCHAIN):
                names.append(f"{q}{r}")
        for name in names:
            self.sems[name] = stack.enter_context(nc.semaphore(name))
            self.cnt[name] = 0
        self.qidx = {"sync": 0, "gpsimd": 0}
        self.qidx_b = {"sync": 0, "gpsimd": 0}

    def emit(self, eng, fn, waits=(), inc=None, amt=1):
        tok = None
        incs = []
        if inc is not None:
            self.cnt[inc] += amt
            tok = (inc, self.cnt[inc])
            incs.append((inc, amt))
        w = tuple(t for t in waits if t is not None)
        self.items[eng].append([fn, w, incs])
        return tok

    def dma(self, eng, out, in_, waits=(), lane="a"):
        q = "LD" if eng == "sync" else "GD"
        nch_a = 4
        if lane == "a":
            idx = self.qidx[eng]
            self.qidx[eng] = idx + 1
            r = idx % nch_a
            k = idx // nch_a
        else:
            idx = self.qidx_b[eng]
            self.qidx_b[eng] = idx + 1
            r = nch_a + idx % (self.NCHAIN - nch_a)
            k = idx // (self.NCHAIN - nch_a)
        sem = f"{q}{r}"
        w = list(waits)
        if k > 0:
            w.append((sem, 16 * k))
        return self.emit(eng, (lambda e, o=out, i=in_: e.dma_start(out=o, in_=i)),
                         w, sem, 16)

    def dma_out(self, eng, out, in_, waits=()):
        return self.dma(eng, out, in_, waits)

    def barrier(self):
        bar = {"sync": "BS", "gpsimd": "BG", "vector": "BV", "scalar": "BC",
               "tensor": "BT"}
        toks = []
        for eng in ENGS:
            b = bar[eng]
            if eng in ("sync", "gpsimd"):
                self.cnt[b] += 1
                toks.append((b, self.cnt[b]))
                q = "LD" if eng == "sync" else "GD"
                w = [(f"{q}{r}", self.cnt[f"{q}{r}"])
                     for r in range(self.NCHAIN)]
                self.items[eng].append([(lambda e: e.nop()), tuple(w),
                                        [(b, 1)]])
            else:
                toks.append(self.dummy[eng](b))
        for eng in ENGS:
            self.items[eng].append([None, tuple(toks), []])

    def finalize(self, block):
        for eng in ENGS:
            items = self.items[eng]

            def body(e, items=items, eng=eng):
                waited = {}
                for fn, waits, incs in items:
                    for (s, v) in waits:
                        if (eng, s) in SKIP_SELF:
                            continue
                        if waited.get(s, -1) >= v:
                            continue
                        e.wait_ge(self.sems[s], v)
                        waited[s] = v
                    if fn is None:
                        continue
                    inst = fn(e)
                    for (s, amt) in incs:
                        inst.then_inc(self.sems[s], amt)
            getattr(block, eng)(body)


class Bank:
    def __init__(self, ap):
        self.ap = ap
        self.writer = None
        self.readers = []

    def wwaits(self):
        w = [t for t in self.readers if t is not None]
        if self.writer is not None:
            w.append(self.writer)
        self.readers = []
        return w

    def set_writer(self, tok):
        self.writer = tok

    def rwaits(self):
        return [self.writer] if self.writer else []

    def add_reader(self, tok):
        if tok is not None:
            self.readers.append(tok)


def fm_view(dram_ap, r0, r1, c0, c1):
    v = dram_ap[r0:r1, c0:c1].rearrange("(c p) t -> c p t", p=128)
    return v.transpose([1, 0, 2])


def build_program(dbg=()):
    nc = bass.Bass("TRN2", target_bir_lowering=False,
                   detect_race_conditions=False)
    xs = nc.dram_tensor("x_seq", [D, SEQ], F32, kind="ExternalInput").ap()
    xo = nc.dram_tensor("x_own", [D, OWN], F32, kind="ExternalInput").ap()
    qkv = nc.dram_tensor("qkv", [D, 3 * D], F32, kind="ExternalInput").ap()
    wo_d = nc.dram_tensor("o_proj", [D, D], F32, kind="ExternalInput").ap()
    wup = nc.dram_tensor("w_up", [FI, D], F32, kind="ExternalInput").ap()
    wdn = nc.dram_tensor("w_down", [D, FI], F32, kind="ExternalInput").ap()
    gam = nc.dram_tensor("gammaC", [128, C], F32, kind="ExternalInput").ap()
    bet = nc.dram_tensor("betaC", [128, C], F32, kind="ExternalInput").ap()
    msk = nc.dram_tensor("masks", [16 * 128, 512], F32, kind="ExternalInput").ap()
    h128f = nc.dram_tensor("h128f", [128, 128], F32, kind="ExternalInput").ap()
    idf_d = nc.dram_tensor("identf", [128, 128], F32, kind="ExternalInput").ap()
    idb_d = nc.dram_tensor("identb", [128, 128], BF16, kind="ExternalInput").ap()
    cst_d = nc.dram_tensor("consts", [128, 8], F32, kind="ExternalInput").ap()
    onr_d = nc.dram_tensor("ones_row", [1, 128], F32, kind="ExternalInput").ap()
    out_d = nc.dram_tensor("outT", [D, OWN], F32, kind="ExternalOutput").ap()

    xn_s = nc.dram_tensor("s_xn", [D, OWN], F32).ap()
    xn2_s = nc.dram_tensor("s_xn2", [D, SEQ], BF16).ap()
    qt_s = nc.dram_tensor("s_qt", [D, OWN], BF16).ap()
    kt_s = nc.dram_tensor("s_kt", [D, SEQ], BF16).ap()
    v_s = nc.dram_tensor("s_v", [SEQ, D], BF16).ap()
    ot_s = nc.dram_tensor("s_ot", [D, OWN], BF16).ap()
    h_s = nc.dram_tensor("s_h", [FI, OWN], F32).ap()
    hq_s = nc.dram_tensor("s_hq", [FI, OWN], BF16).ap()
    g_s = nc.dram_tensor("s_g", [D, OWN], F32).ap()
    acc_s = nc.dram_tensor("s_acc", [128, 256], F32).ap()
    scr = {"s_xn": xn_s, "s_xn2": xn2_s, "s_qt": qt_s, "s_kt": kt_s,
           "s_v": v_s, "s_ot": ot_s, "s_h": h_s, "s_hq": hq_s, "s_g": g_s}
    dbg_outs = {}
    for name in dbg:
        t = scr[name]
        dbg_outs[name] = nc.dram_tensor(
            "dbg_" + name, list(t.shape), F32, kind="ExternalOutput").ap()

    with ExitStack() as st:
        en = st.enter_context
        P = Prog(nc, st)
        gamC = en(nc.sbuf_tensor("gamC", [128, C], F32))
        betC = en(nc.sbuf_tensor("betC", [128, C], F32))
        H128 = en(nc.sbuf_tensor("H128", [128, 128], F32))
        IDF = en(nc.sbuf_tensor("IDF", [128, 128], F32))
        IDB = en(nc.sbuf_tensor("IDB", [128, 128], BF16))
        CSTS = en(nc.sbuf_tensor("CSTS", [128, 8], F32))
        ONR = en(nc.sbuf_tensor("ONR", [1, 128], F32))
        ONC = en(nc.sbuf_tensor("ONC", [128, 1], F32))
        ONCB = en(nc.sbuf_tensor("ONCB", [128, 1], BF16))
        f2tm = en(nc.sbuf_tensor("f2tm", [128, 16], F32))
        m2tm = en(nc.sbuf_tensor("m2tm", [128, 16], F32))
        wrow = en(nc.sbuf_tensor("wrow", [1, 4], F32))
        WCOL = en(nc.sbuf_tensor("WCOL", [128, 2], F32))
        nc.const_aps.register(F32, -0.5, CSTS[:, 0:1])
        nc.const_aps.register(F32, 0.5, CSTS[:, 1:2])
        nc.const_aps.register(F32, EPS, CSTS[:, 2:3])
        banks = [Bank(en(nc.psum_tensor(f"pb{i}", [128, 512], F32))[:])
                 for i in range(8)]
        DUMS = en(nc.sbuf_tensor("DUMS", [1, 1], F32))

        def _dummy_vector(b):
            return P.emit("vector", lambda e: e.engine_nop(), (), b)

        def _dummy_scalar(b):
            return P.emit("scalar", (lambda e: e.activation(
                DUMS[:], CSTS[0:1, 3:4], ACT.Identity)), (), b)

        def _dummy_tensor(b):
            bk = banks[7]
            w = bk.wwaits()
            tok = P.emit("tensor", (lambda e: e.matmul(
                bk.ap[0:1, 0:1], ONC[0:1, 0:1], ONC[0:1, 0:1],
                start=True, stop=True)), w, b)
            bk.set_writer(tok)
            return tok
        P.dummy = {"vector": _dummy_vector, "scalar": _dummy_scalar,
                   "tensor": _dummy_tensor}
        blk = en(nc.Block())

        tg = P.dma("sync", gamC[:], gam[:, :])
        tbq = P.dma("sync", betC[:], bet[:, :])
        th = P.dma("sync", H128[:], h128f[:, :])
        tif = P.dma("sync", IDF[:], idf_d[:, :])
        tib = P.dma("sync", IDB[:], idb_d[:, :])
        tc = P.dma("sync", CSTS[:], cst_d[:, :])
        tonr = P.dma("sync", ONR[:], onr_d[:, :])
        tonc = P.emit("vector", lambda e: e.tensor_copy(ONC[:], CSTS[:, 3:4]),
                      [tc], "VV")
        toncb = P.emit("vector", lambda e: e.tensor_copy(ONCB[:], CSTS[:, 3:4]),
                       [tc], "VV")

        def ln_group(n, xbuf, x_tok, xb16, xb16_tok, sqbuf, xn2buf, xnbuf,
                     rows, want_xn, gd):
            b_sx, b_sq, b_mu, b_r12, b_r1 = (banks[0], banks[1], banks[2],
                                             banks[3], banks[4])
            tsq = P.emit("scalar", (lambda e, xbuf=xbuf, sqbuf=sqbuf, n=n:
                         e.activation(sqbuf[:, 0:C * n], xbuf[:, 0:C * n],
                                      ACT.Square)),
                         [x_tok] + gd.get("sq", []), "SV")

            def mk_sums(bank, src, tok):
                w = bank.wwaits() + [tok, toncb]
                t = None
                for c in range(C):
                    last = (c == C - 1)
                    t = P.emit("tensor", (lambda e, c=c, src=src, bank=bank,
                               n=n, last=last:
                               e.matmul(bank.ap[0:1, 0:n], ONCB[:],
                                        src[:, c * n:(c + 1) * n],
                                        start=(c == 0), stop=last)),
                               w if c == 0 else (), "TV" if last else None)
                bank.set_writer(t)
                return t
            t1 = mk_sums(b_sx, xb16, xb16_tok)
            t2 = mk_sums(b_sq, sqbuf, tsq)
            r = rows
            tmu = P.emit("vector", (lambda e, n=n: e.tensor_scalar_mul(
                r["mu"][:, 0:n], b_sx.ap[0:1, 0:n], 1.0 / D)),
                [t1] + gd.get("rows", []), "VV")
            b_sx.add_reader(tmu)
            tm2 = P.emit("vector", (lambda e, n=n: e.tensor_mul(
                r["tmp"][:, 0:n], r["mu"][:, 0:n], r["mu"][:, 0:n])), [tmu], "VV")
            tvar = P.emit("vector", (lambda e, n=n: e.scalar_tensor_tensor(
                r["var"][:, 0:n], b_sq.ap[0:1, 0:n], 1.0 / D, r["tmp"][:, 0:n],
                ALU.mult, ALU.subtract)), [t2, tm2], "VV")
            b_sq.add_reader(tvar)
            tl1 = P.emit("scalar", (lambda e, n=n: e.activation(
                r["tmp"][:, 0:n], r["var"][:, 0:n], ACT.Ln, bias=EPS)),
                [tvar, tc], "SV")
            tr1 = P.emit("scalar", (lambda e, n=n: e.activation(
                r["r1"][:, 0:n], r["tmp"][:, 0:n], ACT.Exp, scale=-0.5)),
                [tl1], "SV")
            ta = P.emit("vector", (lambda e, n=n: e.tensor_mul(
                r["tmp"][:, 0:n], r["var"][:, 0:n], r["r1"][:, 0:n])), [tr1], "VV")
            tb2 = P.emit("vector", (lambda e, n=n: e.tensor_mul(
                r["tmp"][:, 0:n], r["tmp"][:, 0:n], r["r1"][:, 0:n])), [ta], "VV")
            tl2 = P.emit("scalar", (lambda e, n=n: e.activation(
                r["s2r"][:, 0:n], r["tmp"][:, 0:n], ACT.Ln, bias=EPS)), [tb2], "SV")
            tr2 = P.emit("scalar", (lambda e, n=n: e.activation(
                r["s2r"][:, 0:n], r["s2r"][:, 0:n], ACT.Exp, scale=-0.5)),
                [tl2], "SV")
            tr12 = P.emit("vector", (lambda e, n=n: e.tensor_mul(
                r["r12"][:, 0:n], r["r1"][:, 0:n], r["s2r"][:, 0:n])), [tr2], "VV")

            def bcast(bank, row, tok):
                w = bank.wwaits() + [tok, tonr]
                t = P.emit("tensor", (lambda e, bank=bank, row=row, n=n:
                           e.matmul(bank.ap[:, 0:n], ONR[:], row[:, 0:n],
                                    start=True, stop=True)), w, "TV")
                bank.set_writer(t)
                return t
            tmub = bcast(b_mu, r["mu"], tmu)
            tr12b = bcast(b_r12, r["r12"], tr12)
            tr1b = bcast(b_r1, r["r1"], tr1) if want_xn else None
            tmus = P.emit("vector", (lambda e, n=n: e.tensor_copy(
                r["mub"][:, 0:n], b_mu.ap[:, 0:n])),
                [tmub] + gd.get("rowsb", []), "VV")
            b_mu.add_reader(tmus)
            tr12s = P.emit("vector", (lambda e, n=n: e.tensor_copy(
                r["r12b"][:, 0:n], b_r12.ap[:, 0:n])), [tr12b], "VV")
            b_r12.add_reader(tr12s)
            if want_xn:
                tr1s = P.emit("vector", (lambda e, n=n: e.tensor_copy(
                    r["r1b"][:, 0:n], b_r1.ap[:, 0:n])), [tr1b], "VV")
                b_r1.add_reader(tr1s)
            toks_xn2, toks_xn = [], []
            for c in range(C):
                sl = slice(c * n, (c + 1) * n)
                tt1 = P.emit("vector", (lambda e, sl=sl, xbuf=xbuf, n=n:
                             e.tensor_sub(xbuf[:, sl], xbuf[:, sl],
                                          r["mub"][:, 0:n])),
                             [tmus, tsq, t1], "VV")
                if want_xn:
                    txn = P.emit("vector", (lambda e, sl=sl, xbuf=xbuf,
                                 xnbuf=xnbuf, n=n: e.tensor_mul(
                                     xnbuf[:, sl], xbuf[:, sl],
                                     r["r1b"][:, 0:n])),
                                 [tt1, tr1s, t2] + gd.get("xn", []), "VV")
                    toks_xn.append(txn)
                tt2 = P.emit("vector", (lambda e, sl=sl, xbuf=xbuf, n=n:
                             e.tensor_mul(xbuf[:, sl], xbuf[:, sl],
                                          r["r12b"][:, 0:n])),
                             [tt1, tr12s], "VV")
                t2f = P.emit("scalar", (lambda e, sl=sl, c=c, xbuf=xbuf,
                             xn2buf=xn2buf: e.activation(
                                 xn2buf[:, sl], xbuf[:, sl], ACT.Identity,
                                 bias=betC[:, c:c + 1], scale=gamC[:, c:c + 1])),
                             [tt2, tg, tbq, t1] + gd.get("x2", []), "SV")
                toks_xn2.append(t2f)
            new_guard = {"rows": [tmub, tr12b] + ([tr1b] if tr1b else []),
                         "rowsb": toks_xn2, "sq": [t2]}
            return toks_xn2, toks_xn, new_guard

        WPHW = 512
        WPT = []
        for (_wd, _hw) in ((wup, WPHW), (wdn, WPHW)):
            _per = _wd.shape[1] // _hw
            for _ti in range((_wd.shape[0] // 128) * _per):
                WPT.append((_wd, _hw, _per, _ti))

        def wp_make(ph, lo, hi, nbuf=8):
            hw = WPHW
            wt2 = [ph.enter_context(nc.sbuf_tensor(f"wm{lo}_{i}",
                   [128, hw], F32)) for i in range(nbuf)]
            abf = ph.enter_context(nc.sbuf_tensor(f"wm{lo}a", [128, hw], F32))
            accv = ph.enter_context(nc.sbuf_tensor(f"wm{lo}c",
                                                   [128, hi - lo], F32))
            st = {"pend": [], "pos": lo, "tprev": None, "tread": {}, "cnt": 0}

            def step(k):
                for (tw, wt, gc, cnt) in st["pend"]:
                    ta = P.emit("scalar", (lambda e, wt=wt, gc=gc:
                                e.activation(abf[:], wt[:], ACT.Abs,
                                             accum_out=accv[:, gc:gc + 1])),
                                [tw, st["tprev"]], "SV")
                    st["tprev"] = ta
                    st["tread"][cnt] = ta
                st["pend"] = []
                for _ in range(min(k, hi - st["pos"])):
                    gidx = st["pos"]
                    wdram, hw_, per, ti = WPT[gidx]
                    cnt = st["cnt"]
                    wt = wt2[cnt % nbuf]
                    rt, hf = ti // per, ti % per
                    eng = "sync" if cnt % 2 == 0 else "gpsimd"
                    tw = P.dma(eng, wt[:],
                               wdram[rt * 128:(rt + 1) * 128,
                                     hf * hw_:(hf + 1) * hw_],
                               [st["tread"].get(cnt - nbuf)], lane="b")
                    st["pend"].append((tw, wt, gidx - lo, cnt))
                    st["pos"] += 1
                    st["cnt"] = cnt + 1

            def flush():
                while st["pend"] or st["pos"] < hi:
                    step(nbuf)
                P.dma("gpsimd", acc_s[:, lo:hi], accv[:],
                      [st["tprev"]], lane="b")
            return step, flush

        def _phase0():
            with ExitStack() as ph:
                n = 512
                ngr = OWN // n
                xb = [ph.enter_context(nc.sbuf_tensor("p1x0", [128, C * n],
                                                      F32))] * 2
                xnb = ph.enter_context(nc.sbuf_tensor("p1xn", [128, C * n], F32))
                x2b2 = [ph.enter_context(nc.sbuf_tensor(f"p1x2{i}",
                        [128, C * n], BF16)) for i in range(2)]
                wq = ph.enter_context(nc.sbuf_tensor("p1wq", [128, C * D], BF16))
                qst2 = [ph.enter_context(nc.sbuf_tensor("p1qs",
                        [128, C * n], BF16))] * 2
                rows = {k: ph.enter_context(nc.sbuf_tensor(f"p1r{k}", [1, n], F32))
                        for k in ("mu", "var", "r1", "r12", "tmp", "s2r")}
                for k in ("mub", "r12b", "r1b"):
                    rows[k] = ph.enter_context(
                        nc.sbuf_tensor(f"p1r{k}", [128, n], F32))
                twq = P.dma("gpsimd", wq[:].rearrange("p (c j) -> p c j", c=C),
                            fm_view(qkv, 0, D, 0, D))
                wp1s, wp1f = wp_make(ph, 0, 64, nbuf=4)
                prev = [None, None]
                guard = {}
                guards = [{}, {}]
                xguard = []
                for gi in range(ngr):
                    buf = xb[gi % 2]
                    x2b = x2b2[gi % 2]
                    qst = qst2[gi % 2]
                    guard = guards[gi % 2]
                    oth = guards[(gi + 1) % 2]
                    for key in ("rows", "rowsb", "xn", "sq"):
                        guard[key] = oth.get(key, [])
                    tx = P.dma("sync", buf[:].rearrange("p (c t) -> p c t", c=C),
                               fm_view(xo, 0, D, gi * n, (gi + 1) * n),
                               [prev[gi % 2]] + guard.get("x2", []) + xguard)
                    txc = P.dma("gpsimd",
                                x2b[:].rearrange("p (c t) -> p c t", c=C),
                                fm_view(xo, 0, D, gi * n, (gi + 1) * n),
                                guard.get("x2", []))
                    t2, txn, ng = ln_group(n, buf, tx, x2b, txc, qst, x2b,
                                           xnb, rows, True, guard)
                    guard.update(ng)
                    xguard = t2
                    tspill = P.dma("gpsimd",
                                   fm_view(xn_s, 0, D, gi * n, (gi + 1) * n),
                                   xnb[:].rearrange("p (c t) -> p c t", c=C), txn)
                    guard["xn"] = [tspill]
                    evtoks = []
                    for j in range(C):
                        wp1s(1)
                        bk = banks[5 + (j % 2)]
                        w0 = bk.wwaits() + t2 + [twq]
                        tmm = None
                        for c in range(C):
                            tmm = P.emit("tensor", (lambda e, j=j, c=c, bk=bk,
                                         n=n, x2b=x2b:
                                         e.matmul(bk.ap[:, 0:n],
                                                  wq[:, c * D + j * 128:
                                                     c * D + (j + 1) * 128],
                                                  x2b[:, c * n:(c + 1) * n],
                                                  start=(c == 0), stop=(c == C - 1))),
                                         w0 if c == 0 else (),
                                         "TV" if c == C - 1 else None)
                        bk.set_writer(tmm)
                        tev = P.emit("scalar", (lambda e, j=j, bk=bk, n=n,
                            qst=qst: e.copy(
                            qst[:, j * n:(j + 1) * n], bk.ap[:, 0:n])),
                            bk.rwaits() + [t2[0]], "SV")
                        bk.add_reader(tev)
                        evtoks.append(tev)
                    tq = P.dma("gpsimd", fm_view(qt_s, 0, D, gi * n, (gi + 1) * n),
                               qst[:].rearrange("p (c t) -> p c t", c=C), evtoks)
                    prev[gi % 2] = tq
                    guard["x2"] = [tq]
                    guard["sq"] = guard.get("sq", []) + [tq]
                    guards[gi % 2] = guard
                wp1f()
                P.barrier()

        _phase0()
        def _phase1():
            with ExitStack() as ph:
                n = 512
                ngr = SEQ // n
                xb = [ph.enter_context(nc.sbuf_tensor(f"p2x{i}", [128, C * n], F32))
                      for i in range(2)]
                x2b2 = [ph.enter_context(nc.sbuf_tensor(f"p2x2{i}",
                        [128, C * n], BF16)) for i in range(2)]
                wk = ph.enter_context(nc.sbuf_tensor("p2wk", [128, C * D], BF16))
                kst2 = [ph.enter_context(nc.sbuf_tensor("p2ks",
                        [128, C * n], BF16))] * 2
                rows = {k: ph.enter_context(nc.sbuf_tensor(f"p2r{k}", [1, n], F32))
                        for k in ("mu", "var", "r1", "r12", "tmp", "s2r")}
                for k in ("mub", "r12b", "r1b"):
                    rows[k] = ph.enter_context(
                        nc.sbuf_tensor(f"p2r{k}", [128, n], F32))
                twk = P.dma("gpsimd", wk[:].rearrange("p (c j) -> p c j", c=C),
                            fm_view(qkv, 0, D, D, 2 * D))
                wp2s, wp2f = wp_make(ph, 64, 256, nbuf=4)
                prev = [None, None]
                guard = {}
                guards = [{}, {}]
                for gi in range(ngr):
                    buf = xb[gi % 2]
                    x2b = x2b2[gi % 2]
                    kst = kst2[gi % 2]
                    guard = guards[gi % 2]
                    oth = guards[(gi + 1) % 2]
                    for key in ("rows", "rowsb", "xn", "sq"):
                        guard[key] = oth.get(key, [])
                    tx = P.dma("sync", buf[:].rearrange("p (c t) -> p c t", c=C),
                               fm_view(xs, 0, D, gi * n, (gi + 1) * n),
                               [prev[gi % 2]])
                    txc = P.dma("gpsimd",
                                x2b[:].rearrange("p (c t) -> p c t", c=C),
                                fm_view(xs, 0, D, gi * n, (gi + 1) * n),
                                guard.get("x2", []))
                    t2, _, ng = ln_group(n, buf, tx, x2b, txc, kst, x2b,
                                         None, rows, False, guard)
                    guard.update(ng)
                    tx2 = P.dma("gpsimd",
                                fm_view(xn2_s, 0, D, gi * n, (gi + 1) * n),
                                x2b[:].rearrange("p (c t) -> p c t", c=C), t2)
                    evtoks = []
                    for j in range(C):
                        wp2s(2)
                        bk = banks[5 + (j % 2)]
                        w0 = bk.wwaits() + t2 + [twk]
                        tmm = None
                        for c in range(C):
                            tmm = P.emit("tensor", (lambda e, j=j, c=c, bk=bk,
                                         n=n, x2b=x2b:
                                         e.matmul(bk.ap[:, 0:n],
                                                  wk[:, c * D + j * 128:
                                                     c * D + (j + 1) * 128],
                                                  x2b[:, c * n:(c + 1) * n],
                                                  start=(c == 0), stop=(c == C - 1))),
                                         w0 if c == 0 else (),
                                         "TV" if c == C - 1 else None)
                        bk.set_writer(tmm)
                        tev = P.emit("scalar", (lambda e, j=j, bk=bk, n=n,
                            kst=kst: e.copy(
                            kst[:, j * n:(j + 1) * n], bk.ap[:, 0:n])),
                            bk.rwaits() + [t2[0]], "SV")
                        bk.add_reader(tev)
                        evtoks.append(tev)
                    tk = P.dma("gpsimd", fm_view(kt_s, 0, D, gi * n, (gi + 1) * n),
                               kst[:].rearrange("p (c t) -> p c t", c=C), evtoks)
                    prev[gi % 2] = tk
                    guard["x2"] = [tk, tx2]
                    guard["sq"] = guard.get("sq", []) + [tk]
                    guards[gi % 2] = guard
                wp2f()
                P.barrier()

        _phase1()
        def _phase2():
            with ExitStack() as ph:
                n = 512
                ngr = SEQ // n
                x2b = [ph.enter_context(nc.sbuf_tensor(f"p2bx{i}", [128, C * n],
                                                       BF16)) for i in range(2)]
                wv = ph.enter_context(nc.sbuf_tensor("p2bwv", [128, C * D], BF16))
                vst = ph.enter_context(nc.sbuf_tensor("p2bvs", [128, 4 * D], BF16))
                twv = P.dma("gpsimd", wv[:].rearrange("p (c j) -> p c j", c=C),
                            fm_view(qkv, 0, D, 2 * D, 3 * D))
                prev = [None, None]
                vspill = None
                for gi in range(ngr):
                    buf = x2b[gi % 2]
                    tx = P.dma("sync", buf[:].rearrange("p (c t) -> p c t", c=C),
                               fm_view(xn2_s, 0, D, gi * n, (gi + 1) * n),
                               [prev[gi % 2]])
                    evtoks = []
                    rdtoks = []
                    for ts in range(4):
                        for vn in range(4):
                            bk = banks[(ts * 4 + vn) % 4]
                            w0 = bk.wwaits() + [tx, twv]
                            tmm = None
                            for c in range(C):
                                tmm = P.emit("tensor",
                                             (lambda e, ts=ts, vn=vn, c=c, bk=bk,
                                              buf=buf, n=n:
                                              e.matmul(bk.ap[:, 0:512],
                                                       buf[:, c * n + ts * 128:
                                                          c * n + (ts + 1) * 128],
                                                       wv[:, c * D + vn * 512:
                                                          c * D + (vn + 1) * 512],
                                                       start=(c == 0),
                                                       stop=(c == C - 1))),
                                             w0 if c == 0 else (),
                                             "TV" if c == C - 1 else None)
                            bk.set_writer(tmm)
                            rdtoks.append(tmm)
                            tev = P.emit("scalar", (lambda e, ts=ts, vn=vn, bk=bk:
                                         e.copy(vst[:, ts * D + vn * 512:
                                                    ts * D + (vn + 1) * 512],
                                                bk.ap[:, 0:512])),
                                         bk.rwaits() + ([vspill] if vspill else []),
                                         "SV")
                            bk.add_reader(tev)
                            evtoks.append(tev)
                    vspill = P.dma("gpsimd",
                                   v_s[gi * n:(gi + 1) * n, :]
                                   .rearrange("(s p) d -> s p d", p=128)
                                   .transpose([1, 0, 2]),
                                   vst[:].rearrange("p (s d) -> p s d", s=4),
                                   evtoks)
                    prev[gi % 2] = P.emit("vector", lambda e: e.engine_nop(),
                                          rdtoks, "VV")
                P.barrier()

        _phase2()
        def _phase3():
            with ExitStack() as ph:
                qsb = ph.enter_context(nc.sbuf_tensor("p3q", [128, C * 512], BF16))
                ksb = [ph.enter_context(nc.sbuf_tensor(f"p3k{i}", [128, C * 512],
                                                       BF16)) for i in range(2)]
                vsb = [ph.enter_context(nc.sbuf_tensor(f"p3v{i}", [128, 4 * D],
                                                       BF16)) for i in range(2)]
                oacc = ph.enter_context(nc.sbuf_tensor("p3o", [128, 4 * D], F32))
                mrow = [ph.enter_context(nc.sbuf_tensor(f"p3m{i}", [128, 4], F32))
                        for i in range(2)]
                lrow = ph.enter_context(nc.sbuf_tensor("p3l", [128, 4], F32))
                srow = ph.enter_context(nc.sbuf_tensor("p3s", [128, 4], F32))
                drow = ph.enter_context(nc.sbuf_tensor("p3d", [128, 4], F32))
                nbr = ph.enter_context(nc.sbuf_tensor("p3nb", [128, 4], F32))
                rmax = ph.enter_context(nc.sbuf_tensor("p3rm", [128, 4], F32))
                masks = ph.enter_context(nc.sbuf_tensor("p3msk", [128, 16 * 512],
                                                        F32))
                smsk = ph.enter_context(nc.sbuf_tensor("p3sm", [128, 4 * 512],
                                                       F32))
                pb16 = ph.enter_context(nc.sbuf_tensor("p3p", [128, 4 * 512],
                                                       BF16))
                ptb = ph.enter_context(nc.sbuf_tensor("p3pt", [128, 512], BF16))
                ob16 = ph.enter_context(nc.sbuf_tensor("p3ob", [128, D], BF16))
                otst = ph.enter_context(nc.sbuf_tensor("p3ot", [128, C * 512],
                                                       BF16))
                linv = ph.enter_context(nc.sbuf_tensor("p3li", [128, 4], F32))
                tmsk = P.dma("sync",
                             masks[:].rearrange("p (q t) -> p q t", q=16),
                             msk[:, :].rearrange("(q p) t -> q p t", p=128)
                             .transpose([1, 0, 2]))
                b_sc4 = [banks[0], banks[1], banks[2], banks[3]]
                b_pt = banks[4]
                b_av2 = [banks[5], banks[6]]
                b_fin = banks[7]
                for sti in range(4):
                    kmax = CNT[TILES_ROLE[0][sti * 4 + 3]]
                    tq = P.dma("sync", qsb[:].rearrange("p (c t) -> p c t", c=C),
                               fm_view(qt_s, 0, D, sti * 512, (sti + 1) * 512))
                    oacc_done = {qi: [] for qi in range(4)}
                    prev_tex = {}
                    prev_tl = {}
                    prev_tdx = {}
                    prev_smsk = {}
                    prev_ptm = {}
                    kprev = [None, None]
                    vprev = [None, None]
                    for kt in range(kmax):
                        kb = ksb[kt % 2]
                        vb = vsb[kt % 2]
                        tkl = P.dma("sync", kb[:].rearrange("p (c t) -> p c t", c=C),
                                    fm_view(kt_s, 0, D, kt * 512, (kt + 1) * 512),
                                    [kprev[kt % 2]])
                        tvl = P.dma("sync",
                                    vb[:].rearrange("p (s d) -> p s d", s=4),
                                    v_s[kt * 512:(kt + 1) * 512, :]
                                    .rearrange("(s p) d -> s p d", p=128)
                                    .transpose([1, 0, 2]), [vprev[kt % 2]])
                        kread, vread = [], []
                        active = [qi for qi in range(4)
                                  if kt < CNT[TILES_ROLE[0][sti * 4 + qi]]]
                        texs, tdxs = {}, {}
                        for qi in active:
                            g_cnt = CNT[TILES_ROLE[0][sti * 4 + qi]]
                            first = (kt == 0)
                            diag = (kt == g_cnt - 1)
                            bsc = b_sc4[qi]
                            w0 = bsc.wwaits() + [tq, tkl]
                            tmm = None
                            for c in range(C):
                                tmm = P.emit("tensor",
                                             (lambda e, qi=qi, c=c, bsc=bsc, kb=kb:
                                              e.matmul(bsc.ap[:, 0:512],
                                                       qsb[:, c * 512 + qi * 128:
                                                          c * 512 + (qi + 1) * 128],
                                                       kb[:, c * 512:(c + 1) * 512],
                                                       start=(c == 0),
                                                       stop=(c == C - 1))),
                                             w0 if c == 0 else (),
                                             "TV" if c == C - 1 else None)
                            bsc.set_writer(tmm)
                            kread.append(tmm)
                            if diag:
                                qidx = sti * 4 + qi
                                msl = slice(qi * 512, (qi + 1) * 512)
                                tmadd = P.emit("vector",
                                               (lambda e, qidx=qidx, bsc=bsc,
                                                msl=msl:
                                                e.tensor_add(
                                                    smsk[:, msl], bsc.ap[:, 0:512],
                                                    masks[:, qidx * 512:
                                                          (qidx + 1) * 512])),
                                               bsc.rwaits() + [tmsk,
                                               prev_smsk.get(qi)], "VV")
                                bsc.add_reader(tmadd)
                                sc_src = smsk[:, msl]
                                tprep = [tmadd]
                            else:
                                sc_src = bsc.ap[:, 0:512]
                                tprep = bsc.rwaits()
                            trm = P.emit("vector", (lambda e, qi=qi, s=sc_src:
                                         e.reduce_max(rmax[:, qi:qi + 1], s,
                                                      axis=AX.X)), tprep, "VV")
                            if not diag:
                                bsc.add_reader(trm)
                            mo, mn = mrow[kt % 2], mrow[(kt + 1) % 2]
                            tdx = None
                            if first:
                                tmn = P.emit("vector", (lambda e, qi=qi, mn=mn:
                                             e.tensor_copy(mn[:, qi:qi + 1],
                                                           rmax[:, qi:qi + 1])),
                                             [trm], "VV")
                            else:
                                tmn = P.emit("vector", (lambda e, qi=qi, mo=mo, mn=mn:
                                             e.tensor_max(mn[:, qi:qi + 1],
                                                          mo[:, qi:qi + 1],
                                                          rmax[:, qi:qi + 1])),
                                             [trm], "VV")
                                tda = P.emit("vector", (lambda e, qi=qi, mo=mo, mn=mn:
                                             e.tensor_sub(drow[:, qi:qi + 1],
                                                          mo[:, qi:qi + 1],
                                                          mn[:, qi:qi + 1])),
                                             [tmn, prev_tdx.get(qi)], "VV")
                                tdx = P.emit("scalar", (lambda e, qi=qi:
                                             e.activation(drow[:, qi:qi + 1],
                                                          drow[:, qi:qi + 1],
                                                          ACT.Exp, scale=CS)),
                                             [tda], "SV")
                            tnb = P.emit("vector", (lambda e, qi=qi, mn=mn:
                                         e.tensor_scalar_mul(nbr[:, qi:qi + 1],
                                                             mn[:, qi:qi + 1],
                                                             -CS)),
                                         [tmn, prev_tex.get(qi)], "VV")
                            psl = slice(qi * 512, (qi + 1) * 512)
                            tex = P.emit("scalar", (lambda e, qi=qi, s=sc_src,
                                         psl=psl:
                                         e.activation(pb16[:, psl], s, ACT.Exp,
                                                      bias=nbr[:, qi:qi + 1],
                                                      scale=CS,
                                                      accum_out=srow[:, qi:qi + 1])),
                                         [tnb, prev_tl.get(qi),
                                          prev_ptm.get(qi)] + tprep, "SV")
                            prev_tex[qi] = tex
                            texs[qi] = tex
                            tdxs[qi] = tdx
                            if not diag:
                                bsc.add_reader(tex)
                            if diag:
                                prev_smsk[qi] = tex
                            if first:
                                tl = P.emit("vector", (lambda e, qi=qi:
                                            e.tensor_copy(lrow[:, qi:qi + 1],
                                                          srow[:, qi:qi + 1])),
                                            [tex], "VV")
                            else:
                                tl = P.emit("vector", (lambda e, qi=qi:
                                            e.scalar_tensor_tensor(
                                                lrow[:, qi:qi + 1],
                                                lrow[:, qi:qi + 1],
                                                drow[:, qi:qi + 1],
                                                srow[:, qi:qi + 1],
                                                ALU.mult, ALU.add)),
                                            [tex, tdx], "VV")
                            prev_tl[qi] = tl
                        for qi in active:
                            g_cnt = CNT[TILES_ROLE[0][sti * 4 + qi]]
                            first = (kt == 0)
                            diag = (kt == g_cnt - 1)
                            tdx = tdxs[qi]
                            psl = slice(qi * 512, (qi + 1) * 512)
                            wpt = b_pt.wwaits() + [texs[qi], tib]
                            tptm = None
                            for k4 in range(4):
                                tptm = P.emit("tensor", (lambda e, k4=k4, psl=psl,
                                              qi=qi:
                                              e.matmul(b_pt.ap[:, k4 * 128:
                                                               (k4 + 1) * 128],
                                                       pb16[:, qi * 512 + k4 * 128:
                                                            qi * 512 +
                                                            (k4 + 1) * 128],
                                                       IDB[:], start=True,
                                                       stop=True)),
                                              wpt if k4 == 0 else (),
                                              "TV" if k4 == 3 else None)
                            b_pt.set_writer(tptm)
                            prev_ptm[qi] = tptm
                            tpte = P.emit("scalar", lambda e: e.copy(ptb[:],
                                          b_pt.ap[:]), b_pt.rwaits(), "SV")
                            b_pt.add_reader(tpte)
                            for vn in range(4):
                                bav = b_av2[vn % 2]
                                wav = bav.wwaits() + [tpte, tvl]
                                tav = None
                                for k4 in range(4):
                                    tav = P.emit("tensor",
                                                 (lambda e, vn=vn, k4=k4, bav=bav,
                                                  vb=vb:
                                                  e.matmul(bav.ap[:, 0:512],
                                                           ptb[:, k4 * 128:
                                                               (k4 + 1) * 128],
                                                           vb[:, k4 * D + vn * 512:
                                                              k4 * D +
                                                              (vn + 1) * 512],
                                                           start=(k4 == 0),
                                                           stop=(k4 == 3))),
                                                 wav if k4 == 0 else (),
                                                 "TV" if k4 == 3 else None)
                                bav.set_writer(tav)
                                vread.append(tav)
                                osl = slice(qi * D + vn * 512,
                                            qi * D + (vn + 1) * 512)
                                if first:
                                    tou = P.emit("vector",
                                                 (lambda e, osl=osl, bav=bav:
                                                  e.tensor_copy(oacc[:, osl],
                                                                bav.ap[:, 0:512])),
                                                 bav.rwaits(), "VV")
                                else:
                                    tou = P.emit("vector",
                                                 (lambda e, osl=osl, qi=qi, bav=bav:
                                                  e.scalar_tensor_tensor(
                                                      oacc[:, osl], oacc[:, osl],
                                                      drow[:, qi:qi + 1],
                                                      bav.ap[:, 0:512],
                                                      ALU.mult, ALU.add)),
                                                 bav.rwaits() + [tdx], "VV")
                                bav.add_reader(tou)
                                if diag:
                                    oacc_done[qi].append(tou)
                        kprev[kt % 2] = (P.emit("vector", lambda e: e.engine_nop(),
                                                kread, "VV") if kread else None)
                        vprev[kt % 2] = (P.emit("vector", lambda e: e.engine_nop(),
                                                vread, "VV") if vread else None)
                    evt = []
                    prev_tt = []
                    for qi in range(4):
                        tri = P.emit("vector", (lambda e, qi=qi:
                                     e.reciprocal(linv[:, qi:qi + 1],
                                                  lrow[:, qi:qi + 1])),
                                     oacc_done[qi], "VV")
                        tob = P.emit("scalar", (lambda e, qi=qi:
                                     e.activation(ob16[:],
                                                  oacc[:, qi * D:(qi + 1) * D],
                                                  ACT.Identity,
                                                  scale=linv[:, qi:qi + 1])),
                                     [tri] + oacc_done[qi] + prev_tt, "SV")
                        prev_tt = []
                        for grp in range(4):
                            wfin = b_fin.wwaits() + [tob, tib]
                            tt = None
                            for k4 in range(4):
                                oc = grp * 4 + k4
                                tt = P.emit("tensor", (lambda e, oc=oc, k4=k4:
                                            e.matmul(b_fin.ap[:, k4 * 128:
                                                              (k4 + 1) * 128],
                                                     ob16[:, oc * 128:
                                                          (oc + 1) * 128],
                                                     IDB[:], start=True,
                                                     stop=True)),
                                            wfin if k4 == 0 else (),
                                            "TV" if k4 == 3 else None)
                            b_fin.set_writer(tt)
                            prev_tt.append(tt)
                            dst = otst[:].rearrange("p (c t) -> p c t", c=C)[
                                :, grp * 4:(grp + 1) * 4, qi * 128:(qi + 1) * 128]
                            tte = P.emit("scalar", (lambda e, dst=dst:
                                         e.copy(dst, b_fin.ap[:].rearrange(
                                             "p (c t) -> p c t", c=4))),
                                         b_fin.rwaits(), "SV")
                            b_fin.add_reader(tte)
                            evt.append(tte)
                    P.dma("gpsimd", fm_view(ot_s, 0, D, sti * 512, (sti + 1) * 512),
                          otst[:].rearrange("p (c t) -> p c t", c=C), evt)
                    P.barrier()

        _phase3()
        def weight_prep(ph, wdram, nrows, ncol, wq_t, widx):
            ntil = nrows // 128
            nch = ncol // 128
            NW = 4 if ncol <= 2048 else 2
            wt2 = [ph.enter_context(nc.sbuf_tensor(f"wp{widx}w{i}",
                   [128, ncol], F32)) for i in range(NW)]
            ab = None
            s1b = ph.enter_context(nc.sbuf_tensor(f"wp{widx}s1", [128, ncol],
                                                  F32))
            s2b = ph.enter_context(nc.sbuf_tensor(f"wp{widx}s2", [128, ncol],
                                                  F32))
            trn = ph.enter_context(nc.sbuf_tensor(f"wp{widx}t", [128, ncol],
                                                  BF16))
            rsum = ph.enter_context(nc.sbuf_tensor(f"wp{widx}r", [1, 1], F32))
            invc = ph.enter_context(nc.sbuf_tensor(f"wp{widx}i", [128, 1],
                                                   F32))
            accb = ph.enter_context(nc.sbuf_tensor(f"wp{widx}ac",
                                                    [128, 128], F32))
            t1 = ph.enter_context(nc.sbuf_tensor(f"wp{widx}t1", [128, 1], F32))
            tld = P.dma("sync", accb[:],
                        acc_s[:, widx * 128:(widx + 1) * 128])
            bsum = banks[0]
            wsum = bsum.wwaits()
            tr1 = P.emit("vector", (lambda e: e.reduce_sum(
                t1[:], accb[:, 0:128], axis=AX.X)), [tld], "VV")
            tsl = P.emit("tensor", (lambda e: e.matmul(
                bsum.ap[0:1, 0:1], t1[:], ONC[:], start=True, stop=True)),
                wsum + [tr1, tonc], "TV")
            bsum.set_writer(tsl)
            trs = P.emit("vector", lambda e: e.tensor_copy(
                rsum[:], bsum.ap[0:1, 0:1]), bsum.rwaits(), "VV")
            bsum.add_reader(trs)
            tms = P.emit("vector", (lambda e: e.tensor_scalar(
                wrow[:, 2 * widx:2 * widx + 1], rsum[:], 1.0 / (1 << 23), EPS,
                ALU.mult, ALU.max)), [trs], "VV")
            tiv = P.emit("vector", (lambda e: e.reciprocal(
                wrow[:, 2 * widx + 1:2 * widx + 2],
                wrow[:, 2 * widx:2 * widx + 1])), [tms], "VV")
            tkr = P.emit("vector", (lambda e: e.tensor_scalar_mul(
                rsum[:], wrow[:, 2 * widx:2 * widx + 1], 0.5 / QB)),
                [tms, trs], "VV")
            binv = banks[1]
            wbi = binv.wwaits()
            tbi = P.emit("tensor", (lambda e: e.matmul(
                binv.ap[:, 0:1], ONR[:], wrow[:, 2 * widx + 1:2 * widx + 2],
                start=True, stop=True)), wbi + [tiv, tonr], "TV")
            tbk = P.emit("tensor", (lambda e: e.matmul(
                binv.ap[:, 1:2], ONR[:], rsum[:], start=True, stop=True)),
                [tkr], "TV")
            binv.set_writer(tbk)
            tin = P.emit("vector", lambda e: e.tensor_copy(invc[:],
                         binv.ap[:, 0:1]), [tbi], "VV")
            tkc = P.emit("vector", (lambda e: e.tensor_copy(
                WCOL[:, widx:widx + 1], binv.ap[:, 1:2])), [tbk], "VV")
            binv.add_reader(tin)
            binv.add_reader(tkc)
            wq_toks = []
            tprev2 = None
            tprev3 = None
            tread2 = {}
            for ti in range(ntil):
                wt = wt2[ti % NW]
                eng = "sync" if ti % 2 == 0 else "gpsimd"
                tw = P.dma(eng, wt[:], wdram[ti * 128:(ti + 1) * 128, :],
                           [tread2.get(ti - NW, tin)])
                ts1 = P.emit("scalar", (lambda e, wt=wt: e.activation(
                    s1b[:], wt[:], ACT.Sign, bias=-0.5, scale=invc[:, 0:1])),
                    [tw, tin, tc, tprev2, tprev3], "SV")
                ts2 = P.emit("scalar", (lambda e, wt=wt: e.activation(
                    s2b[:], wt[:], ACT.Sign, bias=0.5, scale=invc[:, 0:1])),
                    [ts1], "SV")
                tprev2 = ts2
                tread2[ti] = ts2
                tt3 = P.emit("vector", (lambda e: e.tensor_add(
                    trn[:], s1b[:], s2b[:])), [ts2], "VV")
                tprev3 = tt3
                for grp in range((nch + 3) // 4):
                    bkt = banks[2 + (grp % 2)]
                    wt_ = bkt.wwaits() + [tt3, tib]
                    k4n = min(4, nch - grp * 4)
                    tt = None
                    for k4 in range(k4n):
                        cc = grp * 4 + k4
                        tt = P.emit("tensor", (lambda e, cc=cc, k4=k4, bkt=bkt:
                                    e.matmul(bkt.ap[:, k4 * 128:(k4 + 1) * 128],
                                             trn[:, cc * 128:(cc + 1) * 128],
                                             IDB[:], start=True, stop=True)),
                                    wt_ if k4 == 0 else (),
                                    "TV" if k4 == k4n - 1 else None)
                    bkt.set_writer(tt)
                    dst = wq_t[:].rearrange("p (c f) -> p c f", c=nch)[
                        :, grp * 4:grp * 4 + k4n, ti * 128:(ti + 1) * 128]
                    tte = P.emit("vector", (lambda e, dst=dst, bkt=bkt,
                                 k4n=k4n: e.tensor_copy(dst,
                                     bkt.ap[:, 0:k4n * 128].rearrange(
                                         "p (c t) -> p c t", c=k4n))),
                                 bkt.rwaits(), "VV")
                    bkt.add_reader(tte)
                    wq_toks.append(tte)
            return wq_toks

        def quant_rounds(tag, nch, n, ubuf, utoks, qout, ftm, mstore,
                         qint, n0b, rbuf, s2b2, sc_r, sc_q, qguard,
                         rbuf2=None):
            nt = n // 128
            if rbuf2 is None:
                rbuf2 = rbuf
            cmul = C1 if tag == 1 else C2
            t = utoks
            tmax = P.emit("vector", (lambda e: e.tensor_copy(rbuf[:, 0:n],
                         ubuf[:, 0:n])), t + qguard, "VV")
            tmin = P.emit("vector", (lambda e: e.tensor_copy(rbuf[:, n:2 * n],
                         ubuf[:, 0:n])), t, "VV")
            for cc2 in range(1, nch):
                tmax = P.emit("vector", (lambda e, cc2=cc2: e.tensor_max(
                    rbuf[:, 0:n], rbuf[:, 0:n],
                    ubuf[:, cc2 * n:(cc2 + 1) * n])), [tmax], "VV")
                tmin = P.emit("vector", (lambda e, cc2=cc2: e.tensor_tensor(
                    rbuf[:, n:2 * n], rbuf[:, n:2 * n],
                    ubuf[:, cc2 * n:(cc2 + 1) * n], ALU.min)), [tmin], "VV")
            tam = P.emit("vector", (lambda e: e.scalar_tensor_tensor(
                rbuf[:, 0:n], rbuf[:, n:2 * n], -1.0, rbuf[:, 0:n],
                ALU.mult, ALU.max)), [tmax, tmin], "VV")
            btr = banks[5]
            wtr = btr.wwaits() + [tam, tif]
            tt = None
            for k4 in range(nt):
                tt = P.emit("tensor", (lambda e, k4=k4: e.matmul(
                    btr.ap[:, k4 * 128:(k4 + 1) * 128],
                    rbuf[:, k4 * 128:(k4 + 1) * 128], IDF[:],
                    start=True, stop=True)), wtr if k4 == 0 else (),
                    "TV" if k4 == nt - 1 else None)
            btr.set_writer(tt)
            ttm = []
            for k4 in range(nt):
                tr = P.emit("vector", (lambda e, k4=k4: e.reduce_max(
                    sc_r[:, k4:k4 + 1], btr.ap[:, k4 * 128:(k4 + 1) * 128],
                    axis=AX.X)), btr.rwaits(), "VV")
                btr.add_reader(tr)
                ttm.append(tr)
            if ftm is not None:
                tsc = P.emit("vector", (lambda e: e.tensor_mul(
                    sc_r[:, 0:nt], sc_r[:, 0:nt], ftm)), ttm, "VV")
            else:
                tsc = ttm[-1]
            tm_ = P.emit("vector", (lambda e: e.tensor_scalar(
                sc_r[:, 0:nt], sc_r[:, 0:nt], cmul, EPS, ALU.mult, ALU.max)),
                [tsc] + ttm, "VV")
            tok_m = None
            if mstore is not None:
                tok_m = P.emit("vector", (lambda e: e.tensor_copy(
                    mstore, sc_r[:, 0:nt])), [tm_], "VV")
            tqm = P.emit("vector", (lambda e: e.reciprocal(
                sc_q[:, 0:nt], sc_r[:, 0:nt])), [tm_], "VV")
            if ftm is not None:
                tqm = P.emit("vector", (lambda e: e.tensor_mul(
                    sc_q[:, 0:nt], sc_q[:, 0:nt], ftm)), [tqm], "VV")
            tqm = P.emit("vector", (lambda e: e.tensor_scalar_mul(
                sc_q[:, 0:nt], sc_q[:, 0:nt], QB * cmul)), [tqm], "VV")
            brow = banks[5]
            wrw = brow.wwaits() + [tqm, tif]
            tt = None
            for k4 in range(nt):
                tt = P.emit("tensor", (lambda e, k4=k4: e.matmul(
                    brow.ap[0:1, k4 * 128:(k4 + 1) * 128],
                    sc_q[:, k4:k4 + 1], IDF[:], start=True, stop=True)),
                    wrw if k4 == 0 else (), "TV" if k4 == nt - 1 else None)
            brow.set_writer(tt)
            trow = P.emit("vector", (lambda e: e.tensor_copy(
                rbuf[0:1, 0:n], brow.ap[0:1, 0:n])), brow.rwaits(), "VV")
            brow.add_reader(trow)
            brep = banks[6]
            wrep = brep.wwaits() + [trow, tonr]
            trep = P.emit("tensor", (lambda e: e.matmul(
                brep.ap[:, 0:n], ONR[:], rbuf[0:1, 0:n],
                start=True, stop=True)), wrep, "TV")
            brep.set_writer(trep)
            trepc = P.emit("vector", (lambda e: e.tensor_copy(
                rbuf[:, n:2 * n], brep.ap[:, 0:n])), brep.rwaits(), "VV")
            brep.add_reader(trepc)
            qtoks = []
            nb = min(nch, qint.shape[1] // n)
            for cb in range(0, nch, nb):
                nbb = min(nb, nch - cb)
                for cc2 in range(cb, cb + nbb):
                    sl = slice(cc2 * n, (cc2 + 1) * n)
                    P.emit("vector", (lambda e, sl=sl: e.tensor_mul(
                        ubuf[:, sl], ubuf[:, sl], rbuf[:, n:2 * n])),
                        [trepc], "VV")
                bsl = slice(cb * n, (cb + nbb) * n)
                m = nbb * n
                P.emit("vector", (lambda e, bsl=bsl, m=m: e.tensor_copy(
                    qint[:, 0:m], ubuf[:, bsl])), (), "VV")
                P.emit("vector", (lambda e, bsl=bsl, m=m:
                    e.scalar_tensor_tensor(
                        n0b[:, 0:m], qint[:, 0:m], 0.5, ubuf[:, bsl],
                        ALU.add, ALU.is_le)), (), "VV")
                P.emit("vector", (lambda e, bsl=bsl, m=m:
                    e.scalar_tensor_tensor(
                        s2b2[:, 0:m], qint[:, 0:m], -0.5, ubuf[:, bsl],
                        ALU.add, ALU.is_ge)), (), "VV")
                P.emit("vector", (lambda e, m=m: e.tensor_sub(
                    n0b[:, 0:m], n0b[:, 0:m], s2b2[:, 0:m])), (), "VV")
                tqf = P.emit("vector", (lambda e, bsl=bsl, m=m: e.tensor_tensor(
                    qout[:, bsl], qint[:, 0:m], n0b[:, 0:m], ALU.add)),
                    (), "VV")
                qtoks.append(tqf)
            return qtoks, tok_m

        def fwht_stages(nch, n, a, b, toks):
            nst = int(math.log2(nch))
            cur, oth = a, b
            t = list(toks)
            for stg in range(nst):
                s = 1 << stg
                nblk = nch // (2 * s)
                av = cur[:].rearrange("p (blk two s t) -> p blk two s t",
                                      blk=nblk, two=2, s=s)
                ov = oth[:].rearrange("p (blk two s t) -> p blk two s t",
                                      blk=nblk, two=2, s=s)
                gb = 0
                vb = nblk - gb
                tt = []
                tt.append(P.emit("vector", (lambda e, av=av, ov=ov, vb=vb:
                          e.tensor_add(ov[:, 0:vb, 0], av[:, 0:vb, 0],
                                       av[:, 0:vb, 1])), t, "VV"))
                tt.append(P.emit("vector", (lambda e, av=av, ov=ov, vb=vb:
                          e.tensor_sub(ov[:, 0:vb, 1], av[:, 0:vb, 0],
                                       av[:, 0:vb, 1])), t, "VV"))
                if gb:
                    tt.append(P.emit("gpsimd", (lambda e, av=av, ov=ov, vb=vb,
                              nblk=nblk: e.tensor_add(
                                  ov[:, vb:nblk, 0], av[:, vb:nblk, 0],
                                  av[:, vb:nblk, 1])), t, "GC"))
                    tt.append(P.emit("gpsimd", (lambda e, av=av, ov=ov, vb=vb,
                              nblk=nblk: e.tensor_sub(
                                  ov[:, vb:nblk, 1], av[:, vb:nblk, 0],
                                  av[:, vb:nblk, 1])), t, "GC"))
                t = tt
                cur, oth = oth, cur
            return cur, oth, t

        def _phase4():
            with ExitStack() as ph:
                wqt = ph.enter_context(nc.sbuf_tensor("p4wqt", [128, C * FI],
                                                      BF16))
                with ExitStack() as pp:
                    weight_prep(pp, wup, FI, D, wqt, 0)
                    P.barrier()
                n = 256
                ngr = OWN // n
                xnb = ph.enter_context(nc.sbuf_tensor("p4xn", [128, C * n], F32))
                pong = ph.enter_context(nc.sbuf_tensor("p4pg", [128, C * n], F32))
                s2b2 = ph.enter_context(nc.sbuf_tensor("p4s2", [128, n], F32))
                n0b = ph.enter_context(nc.sbuf_tensor("p4n0", [128, n], F32))
                rbuf = ph.enter_context(nc.sbuf_tensor("p4rb", [128, 2 * n], F32))
                qint = ph.enter_context(nc.sbuf_tensor("p4qi", [128, n], I32))
                xqb = ph.enter_context(nc.sbuf_tensor("p4xq", [128, C * n], BF16))
                hst2 = ph.enter_context(nc.sbuf_tensor("p4h2", [128, FC * n], F32))
                hst = n0b
                sc_r = ph.enter_context(nc.sbuf_tensor("p4sr", [128, 2], F32))
                sc_q = ph.enter_context(nc.sbuf_tensor("p4sq", [128, 2], F32))
                prevx = None
                prevh = None
                for gi in range(ngr):
                    tx = P.dma("sync", xnb[:].rearrange("p (c t) -> p c t", c=C),
                               fm_view(xn_s, 0, D, gi * n, (gi + 1) * n), [prevx])
                    fb, ob, ft = fwht_stages(C, n, xnb, pong, [tx])
                    utoks = []
                    for cc in range(C):
                        bk = banks[cc % 4]
                        w0 = bk.wwaits() + ft + [th]
                        tmm = P.emit("tensor", (lambda e, cc=cc, fb=fb, bk=bk, n=n:
                                     e.matmul(bk.ap[:, 0:n], H128[:],
                                              fb[:, cc * n:(cc + 1) * n],
                                              start=True, stop=True)), w0, "TV")
                        bk.set_writer(tmm)
                        te = P.emit("scalar", (lambda e, cc=cc, ob=ob, bk=bk, n=n:
                                    e.copy(ob[:, cc * n:(cc + 1) * n],
                                           bk.ap[:, 0:n])),
                                    bk.rwaits(), "SV")
                        bk.add_reader(te)
                        utoks.append(te)
                    qtoks, _ = quant_rounds(1, C, n, ob, [utoks[-1]], xqb, None,
                                            None, qint, n0b, rbuf, s2b2,
                                            sc_r, sc_q, [prevh] if prevh else [])
                    tf = P.emit("vector", (lambda e, gi=gi: e.tensor_scalar(
                        f2tm[:, 2 * gi:2 * gi + 2], sc_r[:, 0:2],
                        WCOL[:, 0:1], None, ALU.mult)), [qtoks[0]], "VV")
                    tf2 = P.emit("vector", (lambda e, gi=gi: e.tensor_mul(
                        f2tm[:, 2 * gi:2 * gi + 2], f2tm[:, 2 * gi:2 * gi + 2],
                        f2tm[:, 2 * gi:2 * gi + 2])), [tf], "VV")
                    sqtoks = []
                    for j in range(FC):
                        bk = banks[4 + (j % 2)]
                        w0 = bk.wwaits() + [qtoks[-1]]
                        tmm = None
                        for c in range(C):
                            tmm = P.emit("tensor", (lambda e, j=j, c=c, bk=bk, n=n:
                                         e.matmul(bk.ap[:, 0:n],
                                                  wqt[:, c * FI + j * 128:
                                                     c * FI + (j + 1) * 128],
                                                  xqb[:, c * n:(c + 1) * n],
                                                  start=(c == 0),
                                                  stop=(c == C - 1))),
                                         w0 if c == 0 else (),
                                         "TV" if c == C - 1 else None)
                        bk.set_writer(tmm)
                        trl = P.emit("scalar", (lambda e, bk=bk, j=j, n=n:
                                     e.activation(hst2[:, j * n:(j + 1) * n],
                                                  bk.ap[:, 0:n], ACT.Relu)),
                            bk.rwaits() + [prevh], "SV")
                        bk.add_reader(trl)
                        tsq2 = P.emit("vector", (lambda e, j=j, n=n: e.tensor_mul(
                            hst2[:, j * n:(j + 1) * n],
                            hst2[:, j * n:(j + 1) * n],
                            hst2[:, j * n:(j + 1) * n])),
                            [trl], "VV")
                        sqtoks.append(tsq2)
                    prevh = P.dma("gpsimd", fm_view(h_s, 0, FI, gi * n,
                                                    (gi + 1) * n),
                                  hst2[:].rearrange("p (c t) -> p c t", c=FC),
                                  sqtoks)
                    prevx = prevh
                P.barrier()

        _phase4()
        def _phase5():
            with ExitStack() as ph:
                n = 512
                NB = 4
                ngr = OWN // n
                hb = ph.enter_context(nc.sbuf_tensor("p5h", [128, FC * n], F32))
                pong = ph.enter_context(nc.sbuf_tensor("p5pg", [128, FC * n], F32))
                s2b2 = ph.enter_context(nc.sbuf_tensor("p5s2", [128, NB * n],
                                                       F32))
                n0b = ph.enter_context(nc.sbuf_tensor("p5n0", [128, NB * n],
                                                      F32))
                rbuf = ph.enter_context(nc.sbuf_tensor("p5rb", [128, 2 * n], F32))
                rbuf2 = ph.enter_context(nc.sbuf_tensor("p5rb2", [128, NB * n],
                                                        F32))
                qint = ph.enter_context(nc.sbuf_tensor("p5qi", [128, NB * n],
                                                       I32))
                hqb = ph.enter_context(nc.sbuf_tensor("p5hq", [128, FC * n], BF16))
                sc_r = ph.enter_context(nc.sbuf_tensor("p5sr", [128, 4], F32))
                sc_q = ph.enter_context(nc.sbuf_tensor("p5sq", [128, 4], F32))
                prevx = None
                prevq = None
                for gi in range(ngr):
                    tx = P.dma("sync", hb[:].rearrange("p (c t) -> p c t", c=FC),
                               fm_view(h_s, 0, FI, gi * n, (gi + 1) * n), [prevx])
                    fb, ob, ft = fwht_stages(FC, n, hb, pong, [tx])
                    utoks = []
                    for cc in range(FC):
                        bk = banks[cc % 4]
                        w0 = bk.wwaits() + ft + [th]
                        tmm = P.emit("tensor", (lambda e, cc=cc, fb=fb, bk=bk, n=n:
                                     e.matmul(bk.ap[:, 0:n], H128[:],
                                              fb[:, cc * n:(cc + 1) * n],
                                              start=True, stop=True)), w0, "TV")
                        bk.set_writer(tmm)
                        te = P.emit("scalar", (lambda e, cc=cc, ob=ob, bk=bk, n=n:
                                    e.copy(ob[:, cc * n:(cc + 1) * n],
                                           bk.ap[:, 0:n])),
                                    bk.rwaits(), "SV")
                        bk.add_reader(te)
                        utoks.append(te)
                    nt = n // 128
                    qtoks, _ = quant_rounds(2, FC, n, ob, [utoks[-1]], hqb,
                                            f2tm[:, nt * gi:nt * gi + nt],
                                            m2tm[:, nt * gi:nt * gi + nt],
                                            qint, n0b, rbuf, s2b2, sc_r, sc_q,
                                            list(prevq or []), rbuf2)
                    sp_toks = []
                    for bi, cb in enumerate(range(0, FC, 4)):
                        tsp = P.dma("gpsimd",
                                    fm_view(hq_s, cb * 128, (cb + 4) * 128,
                                            gi * n, (gi + 1) * n),
                                    hqb[:, cb * n:(cb + 4) * n].rearrange(
                                        "p (c t) -> p c t", c=4),
                                    [qtoks[bi]])
                        sp_toks.append(tsp)
                    prevq = sp_toks
                    prevx = sp_toks[-1]
                P.barrier()

        _phase5()
        def _phase6():
            with ExitStack() as ph:
                wqd = ph.enter_context(nc.sbuf_tensor("p6wqd", [128, FC * D],
                                                      BF16))
                with ExitStack() as pp:
                    weight_prep(pp, wdn, D, FI, wqd, 1)
                    P.barrier()
                n = 512
                ngr = OWN // n
                nt = n // 128
                hqb = ph.enter_context(nc.sbuf_tensor("p6hq", [128, FC * n],
                                                      BF16))
                fdr = ph.enter_context(nc.sbuf_tensor("p6fd", [128, nt], F32))
                fdrep = ph.enter_context(nc.sbuf_tensor("p6fe", [128, n],
                                                        F32))
                frow = ph.enter_context(nc.sbuf_tensor("p6fr", [1, n], F32))
                o2b = ph.enter_context(nc.sbuf_tensor("p6o2", [128, 2 * n],
                                                       F32))
                gst = ph.enter_context(nc.sbuf_tensor("p6g", [128, C * n], F32))
                prevx = None
                prevg = None
                for gi in range(ngr):
                    tx = P.dma("sync", hqb[:].rearrange("p (c t) -> p c t", c=FC),
                               fm_view(hq_s, 0, FI, gi * n, (gi + 1) * n), [prevx])
                    tfd = P.emit("vector", (lambda e, gi=gi, nt=nt:
                        e.tensor_scalar(
                        fdr[:, 0:nt], m2tm[:, nt * gi:nt * gi + nt],
                        WCOL[:, 1:2],
                        None, ALU.mult)), [prevg], "VV")
                    btr = banks[5]
                    wtr = btr.wwaits() + [tfd, tif]
                    tt = None
                    for k4 in range(nt):
                        tt = P.emit("tensor", (lambda e, k4=k4: e.matmul(
                            btr.ap[0:1, k4 * 128:(k4 + 1) * 128],
                            fdr[:, k4:k4 + 1], IDF[:], start=True, stop=True)),
                            wtr if k4 == 0 else (),
                            "TV" if k4 == nt - 1 else None)
                    btr.set_writer(tt)
                    trw = P.emit("vector", (lambda e: e.tensor_copy(
                        frow[:], btr.ap[0:1, 0:n])), btr.rwaits(), "VV")
                    btr.add_reader(trw)
                    brep = banks[6]
                    wrep = brep.wwaits() + [trw, tonr]
                    trep = P.emit("tensor", (lambda e: e.matmul(
                        brep.ap[:, 0:n], ONR[:], frow[:], start=True, stop=True)),
                        wrep, "TV")
                    brep.set_writer(trep)
                    tfde = P.emit("vector", (lambda e: e.tensor_copy(
                        fdrep[:, 0:n], brep.ap[:, 0:n])),
                        brep.rwaits() + [prevg], "VV")
                    brep.add_reader(tfde)
                    gtoks = []
                    prev_sil = [None, None]
                    for j in range(C):
                        bk = banks[(j % 2)]
                        w0 = bk.wwaits() + [tx]
                        tmm = None
                        for c in range(FC):
                            tmm = P.emit("tensor", (lambda e, j=j, c=c, bk=bk, n=n:
                                         e.matmul(bk.ap[:, 0:n],
                                                  wqd[:, c * D + j * 128:
                                                     c * D + (j + 1) * 128],
                                                  hqb[:, c * n:(c + 1) * n],
                                                  start=(c == 0),
                                                  stop=(c == FC - 1))),
                                         w0 if c == 0 else (),
                                         "TV" if c == FC - 1 else None)
                        bk.set_writer(tmm)
                        osl = slice((j % 2) * n, (j % 2) * n + n)
                        tsc2 = P.emit("vector", (lambda e, bk=bk, osl=osl, n=n:
                                      e.tensor_mul(o2b[:, osl], bk.ap[:, 0:n],
                                                   fdrep[:, 0:n])),
                                      bk.rwaits() + [tfde, prev_sil[j % 2]],
                                      "VV")
                        bk.add_reader(tsc2)
                        tsil = P.emit("scalar", (lambda e, j=j, osl=osl, n=n:
                                      e.activation(gst[:, j * n:(j + 1) * n],
                                                   o2b[:, osl], ACT.Silu)),
                            [tsc2, prevg], "SV")
                        prev_sil[j % 2] = tsil
                        gtoks.append(tsil)
                    prevg = P.dma("gpsimd", fm_view(g_s, 0, D, gi * n,
                                                    (gi + 1) * n),
                                  gst[:].rearrange("p (c t) -> p c t", c=C),
                                  gtoks)
                    prevx = prevg
                P.barrier()

        _phase6()
        def _phase7():
            with ExitStack() as ph:
                n = 512
                ngr = OWN // n
                wob = ph.enter_context(nc.sbuf_tensor("p7wo", [128, C * D], BF16))
                otb = ph.enter_context(nc.sbuf_tensor("p7ot", [128, C * n], BF16))
                xnb = ph.enter_context(nc.sbuf_tensor("p7xn", [128, C * n], F32))
                gb = ph.enter_context(nc.sbuf_tensor("p7g", [128, C * n], F32))
                xb = ph.enter_context(nc.sbuf_tensor("p7x", [128, C * n], F32))
                two = P.dma("gpsimd", wob[:].rearrange("p (c j) -> p c j", c=C),
                            fm_view(wo_d, 0, D, 0, D))
                prevo = prevn = prevgl = prevxl = None
                for gi in range(ngr):
                    tot = P.dma("sync", otb[:].rearrange("p (c t) -> p c t", c=C),
                                fm_view(ot_s, 0, D, gi * n, (gi + 1) * n), [prevo])
                    txn = P.dma("sync", xnb[:].rearrange("p (c t) -> p c t", c=C),
                                fm_view(xn_s, 0, D, gi * n, (gi + 1) * n), [prevn])
                    tgl = P.dma("sync", gb[:].rearrange("p (c t) -> p c t", c=C),
                                fm_view(g_s, 0, D, gi * n, (gi + 1) * n), [prevgl])
                    txl = P.dma("sync", xb[:].rearrange("p (c t) -> p c t", c=C),
                                fm_view(xo, 0, D, gi * n, (gi + 1) * n), [prevxl])
                    otoks = []
                    mmtoks = []
                    for j in range(C):
                        bk = banks[(j % 2)]
                        w0 = bk.wwaits() + [tot, two]
                        tmm = None
                        for c in range(C):
                            tmm = P.emit("tensor", (lambda e, j=j, c=c, bk=bk, n=n:
                                         e.matmul(bk.ap[:, 0:n],
                                                  wob[:, c * D + j * 128:
                                                     c * D + (j + 1) * 128],
                                                  otb[:, c * n:(c + 1) * n],
                                                  start=(c == 0),
                                                  stop=(c == C - 1))),
                                         w0 if c == 0 else (),
                                         "TV" if c == C - 1 else None)
                        bk.set_writer(tmm)
                        mmtoks.append(tmm)
                        sl = slice(j * n, (j + 1) * n)
                        ts_ = P.emit("vector", (lambda e, sl=sl, bk=bk, n=n:
                                     e.tensor_add(xnb[:, sl], bk.ap[:, 0:n],
                                                  xnb[:, sl])),
                                     bk.rwaits() + [txn], "VV")
                        bk.add_reader(ts_)
                        tsg = P.emit("vector", (lambda e, sl=sl: e.tensor_mul(
                            gb[:, sl], xnb[:, sl], gb[:, sl])), [ts_, tgl], "VV")
                        tout = P.emit("vector", (lambda e, sl=sl: e.tensor_add(
                            xb[:, sl], xb[:, sl], gb[:, sl])), [tsg, txl], "VV")
                        otoks.append(tout)
                    tso = P.dma_out("sync", fm_view(out_d, 0, D, gi * n,
                                                    (gi + 1) * n),
                                    xb[:].rearrange("p (c t) -> p c t", c=C),
                                    otoks)
                    prevo = prevn = prevgl = prevxl = tso
                P.barrier()

        _phase7()
        if dbg:
            for name in dbg:
                src = scr[name]
                dst = dbg_outs[name]
                nr = src.shape[0]
                for rr in range(0, nr, 512):
                    P.dma_out("gpsimd", dst[rr:rr + 512, :],
                              src[rr:rr + 512, :])
            P.barrier()

        P.finalize(blk)
    return nc


_CACHE = {}


def _host_inputs(L0_attn_x, L0_attn_qkv, L0_attn_o_proj, L0_attn_gamma,
                 L0_attn_beta, L0_ffn_w_up, L0_ffn_w_down):
    x = np.asarray(L0_attn_x, dtype=np.float32)
    qkv = np.ascontiguousarray(np.asarray(L0_attn_qkv, dtype=np.float32))
    wo = np.ascontiguousarray(np.asarray(L0_attn_o_proj, dtype=np.float32))
    wup = np.ascontiguousarray(np.asarray(L0_ffn_w_up, dtype=np.float32))
    wdn = np.ascontiguousarray(np.asarray(L0_ffn_w_down, dtype=np.float32))
    gamC = np.ascontiguousarray(
        np.asarray(L0_attn_gamma, np.float32).reshape(C, 128).T)
    betC = np.ascontiguousarray(
        np.asarray(L0_attn_beta, np.float32).reshape(C, 128).T)
    ii = np.arange(128)
    pc = np.zeros((128, 128), dtype=np.int64)
    a = ii[:, None] & ii[None, :]
    for b in range(7):
        pc += (a >> b) & 1
    h128 = (1.0 - 2.0 * (pc & 1)).astype(np.float32)
    identf = np.eye(128, dtype=np.float32)
    identb = np.eye(128).astype(ml_dtypes.bfloat16)
    consts = np.zeros((128, 8), dtype=np.float32)
    consts[:, 0] = -0.5
    consts[:, 1] = 0.5
    consts[:, 2] = EPS
    consts[:, 3] = 1.0
    ones_row = np.ones((1, 128), dtype=np.float32)
    shared = dict(qkv=qkv, o_proj=wo, w_up=wup, w_down=wdn, gammaC=gamC,
                  betaC=betC, h128f=h128, identf=identf, identb=identb,
                  consts=consts, ones_row=ones_row)
    in_maps = []
    metas = []
    for core in range(8):
        b = core // 2
        role = core % 2
        tiles = TILES_ROLE[role]
        rows = np.concatenate([np.arange(g * 128, (g + 1) * 128)
                               for g in tiles])
        x_seq = np.ascontiguousarray(x[b].T)
        x_own = np.ascontiguousarray(x[b][rows].T)
        masks = np.zeros((16, 128, 512), dtype=np.float32)
        for i, g in enumerate(tiles):
            cgt = CNT[g]
            k0 = (cgt - 1) * 512
            q_glob = g * 128 + np.arange(128)[:, None]
            k_glob = k0 + np.arange(512)[None, :]
            masks[i] = np.where(k_glob <= q_glob, 0.0, NEG)
        m = dict(shared)
        m["x_seq"] = x_seq
        m["x_own"] = x_own
        m["masks"] = np.ascontiguousarray(masks.reshape(16 * 128, 512))
        in_maps.append(m)
        metas.append((b, rows))
    return in_maps, metas


def kernel(**inputs):
    if "nc" not in _CACHE:
        _CACHE["nc"] = build_program()
    nc = _CACHE["nc"]
    in_maps, metas = _host_inputs(**inputs)
    res = run_bass_kernel_spmd(nc, in_maps, core_ids=list(range(8)))
    out = np.empty((4, SEQ, D), dtype=np.float32)
    for core in range(8):
        b, rows = metas[core]
        out[b][rows] = np.ascontiguousarray(res.results[core]["outT"]).T
    return out
